# version 10
# baseline (speedup 1.0000x reference)
import sys
sys.path.insert(0, "/opt/trn_rl_repo")
import contextlib
import numpy as np
import ml_dtypes
import concourse.bass as bass
import concourse.mybir as mybir
from concourse.bass_utils import run_bass_kernel_spmd

F32 = mybir.dt.float32
BF16 = mybir.dt.bfloat16
AF = mybir.ActivationFunctionType
ALU = mybir.AluOpType
AXX = mybir.AxisListType.X
BF = ml_dtypes.bfloat16

D = 2048
E = 8
CH = 512
NT3 = 12
NT_OWN = 8
TOKW = 1536
QSCALE = float(1.0 / np.float64(np.float32(np.sqrt(np.float64(128.0)))))
EPS = 1e-5


class Ctr:
    def __init__(self, h):
        self.h = h
        self.n = 0


def build_program(stage="full"):
    nc = bass.Bass("TRN2", target_bir_lowering=False)

    dr_x_own = nc.dram_tensor("x_own", [1024, D], F32, kind="ExternalInput").ap()
    dr_x_halo = nc.dram_tensor("x_halo", [512, D], F32, kind="ExternalInput").ap()
    dr_wg = nc.dram_tensor("wg", [128, 16, 8], F32, kind="ExternalInput").ap()
    dr_wi0 = nc.dram_tensor("wi0", [E, 4, 128, 16, 256], F32, kind="ExternalInput").ap()
    dr_wi1 = nc.dram_tensor("wi1", [E, 4, 128, 16, 256], F32, kind="ExternalInput").ap()
    dr_wo = nc.dram_tensor("wo", [E, 2, 128, 4, 2048], F32, kind="ExternalInput").ap()
    dr_wqk = nc.dram_tensor("wqk", [32, 128, 16, 128], F32, kind="ExternalInput").ap()
    dr_wv = nc.dram_tensor("wv", [8, 128, 16, 256], F32, kind="ExternalInput").ap()
    dr_wout = nc.dram_tensor("wout", [4, 4, 128, 4, 512], F32, kind="ExternalInput").ap()
    dr_band = nc.dram_tensor("band", [128, 8, 512], BF16, kind="ExternalInput").ap()
    dr_mpre = nc.dram_tensor("mpre", [128, 4, 512], BF16, kind="ExternalInput").ap()
    dr_ident = nc.dram_tensor("ident", [128, 128], F32, kind="ExternalInput").ap()
    dr_y = nc.dram_tensor("y", [1024, D], F32, kind="ExternalOutput").ap()
    dbg = {}
    if stage != "full":
        dbg["own"] = nc.dram_tensor("dbg_own", [1024, D], F32, kind="ExternalOutput").ap()
        dbg["halo"] = nc.dram_tensor("dbg_halo", [512, D], F32, kind="ExternalOutput").ap()

    es = contextlib.ExitStack()
    sbuf = lambda n, s, d: es.enter_context(nc.sbuf_tensor(n, s, d))

    X = sbuf("X", [128, NT_OWN * D], F32)
    XNT = sbuf("XNT", [128, 16 * TOKW], BF16)
    X_halo = sbuf("X_halo", [128, 4 * D], BF16)
    xn_tmp = [sbuf(f"xn_tmp{i}", [128, D], BF16) for i in range(2)]
    wg_sb = sbuf("wg_sb", [128, 16 * 8], BF16)
    what_sb = sbuf("what_sb", [128, NT3 * 8], F32)
    stats = sbuf("stats", [128, 64], F32)
    lg_all = sbuf("lg_all", [128, NT3 * 8], F32)
    top_all = sbuf("top_all", [128, NT3 * 8], F32)
    r_scr = sbuf("r_scr", [128, 3 * 16], F32)
    m_scr = sbuf("m_scr", [128, 16], F32)
    ident_sb = sbuf("ident_sb", [128, 128], F32)
    ident_bf = sbuf("ident_bf", [128, 128], BF16)
    ones_col_bf = sbuf("ones_col_bf", [128, 1], BF16)
    ones_row_f = sbuf("ones_row_f", [1, 128], F32)
    neg_c = sbuf("neg_c", [128, 1], F32)
    cchain = sbuf("cchain", [128, 8], F32)
    cmax = sbuf("cmax", [1, 1], F32)
    rs_sb = sbuf("rs_sb", [1, 512], F32)
    rsbc_sb = sbuf("rsbc_sb", [128, 512], BF16)

    pb = [es.enter_context(nc.psum_tensor(f"pb{i}", [128, 512], F32)) for i in range(8)]

    sem = lambda n: Ctr(es.enter_context(nc.semaphore(n)))
    s_mm = sem("s_mm")
    s_v = sem("s_v")
    s_s = sem("s_s")
    s_g = sem("s_g")
    s_stpY = sem("s_stpY")
    s_stpG = sem("s_stpG")
    s_tp = [sem("s_tpA"), sem("s_tpB")]
    s_wi = [sem("s_wi0"), sem("s_wi1")]
    s_wo = [sem("s_wo0"), sem("s_wo1")]
    s_wqk = [sem("s_wqk0"), sem("s_wqk1")]
    s_wv = sem("s_wv")
    s_wout = [sem("s_wout0"), sem("s_wout1")]
    s_mask = sem("s_mask")
    s_st = sem("s_st")

    ev = {k: [] for k in ("T", "V", "S", "G", "Y")}

    def emit(which):
        def deco(f):
            ev[which].append(f)
        return deco

    def drain():
        with nc.Block() as block:
            @block.tensor
            def _(eng):
                for f in ev["T"]:
                    f(eng)

            @block.vector
            def _(eng):
                for f in ev["V"]:
                    f(eng)

            @block.scalar
            def _(eng):
                for f in ev["S"]:
                    f(eng)

            @block.gpsimd
            def _(eng):
                for f in ev["G"]:
                    f(eng)

            @block.sync
            def _(eng):
                for f in ev["Y"]:
                    f(eng)
        for k in ev:
            ev[k].clear()

    bank_free = [None] * 8
    bank_fill = [0] * 8

    def bw(b):
        return bank_free[b]

    def wait_pairs(eng, pairs):
        for p in pairs:
            if p is not None:
                eng.wait_ge(p[0].h, p[1])

    def guard_now():
        return [(s_mm, s_mm.n), (s_v, s_v.n), (s_s, s_s.n)]

    @emit("Y")
    def _(eng):
        eng.dma_start(out=X[:].rearrange("p (t d) -> p t d", t=NT_OWN),
                      in_=dr_x_own.rearrange("(t p) d -> p t d", p=128)).then_inc(s_stpY.h, 16)
        eng.dma_start(out=ident_sb[:], in_=dr_ident).then_inc(s_stpY.h, 16)
    s_stpY.n += 32

    @emit("G")
    def _(eng):
        eng.memset(ones_col_bf[:], 1.0)
        eng.memset(ones_row_f[:], 1.0).then_inc(s_g.h, 1)
        eng.dma_start(out=X_halo[:].rearrange("p (t d) -> p t d", t=4),
                      in_=dr_x_halo.rearrange("(t p) d -> p t d", p=128)).then_inc(s_stpG.h, 16)
        eng.dma_start(out=wg_sb[:].rearrange("p (k j) -> p k j", k=16),
                      in_=dr_wg).then_inc(s_stpG.h, 16)
        eng.dma_start(out=ident_bf[:], in_=dr_ident).then_inc(s_stpG.h, 16)
    s_g.n += 1
    s_stpG.n += 48
    setupY_val = s_stpY.n
    setupG_val = s_stpG.n

    xnt_uses = [0, 0]
    last_xnt_reader = [0]

    def x_tile(tt):
        if tt < 4:
            return X_halo[:, tt * D:(tt + 1) * D]
        return X[:, (tt - 4) * D:(tt - 3) * D]

    def emit_ln_tile_stats(call, i, tt, dep, td, st):
        @emit("V")
        def _(eng, i=i, tt=tt, dep=list(dep), call=call, td=td):
            if i == 0 and call == 0:
                eng.wait_ge(s_stpY.h, setupY_val)
                eng.wait_ge(s_stpG.h, setupG_val)
            if td is not None:
                eng.wait_ge(s_v.h, td)
            elif i == 0:
                wait_pairs(eng, dep)
            eng.tensor_reduce(stats[:, i:i + 1], x_tile(tt), AXX,
                              ALU.add).then_inc(s_v.h, 1)
        s_v.n += 1
        st["red"][i] = s_v.n

        @emit("S")
        def _(eng, i=i, tt=tt, dep=list(dep), call=call, td=td,
              w=st["sq"].get(i - 2, 0), w_tp=256 * xnt_uses[i % 2]):
            if i == 0:
                if call == 0:
                    eng.wait_ge(s_stpY.h, setupY_val)
                    eng.wait_ge(s_stpG.h, setupG_val)
                wait_pairs(eng, dep)
            if w_tp > 0:
                eng.wait_ge(s_tp[i % 2].h, w_tp)
            if td is not None:
                eng.wait_ge(s_v.h, td)
            if w > 0:
                eng.wait_ge(s_s.h, w)
            eng.activation(xn_tmp[i % 2][:], x_tile(tt), AF.Square,
                           accum_out=stats[:, 12 + i:13 + i]).then_inc(s_s.h, 1)
        s_s.n += 1
        st["sq"][i] = s_s.n

    def emit_ln_batch(call, st, h0, h1):
        vw = max(st["red"][j] for j in range(h0, h1))
        sw = max(st["sq"][j] for j in range(h0, h1))

        @emit("V")
        def _(eng, h0=h0, h1=h1, vw=vw):
            eng.wait_ge(s_v.h, vw)
            eng.tensor_scalar(stats[:, 24 + h0:24 + h1], stats[:, h0:h1], 1.0 / D,
                              None, ALU.mult).then_inc(s_v.h, 1)
        s_v.n += 1
        v_mu = s_v.n

        @emit("V")
        def _(eng, h0=h0, h1=h1, v_mu=v_mu):
            eng.wait_ge(s_v.h, v_mu)
            eng.tensor_tensor(stats[:, 48 + h0:48 + h1], stats[:, 24 + h0:24 + h1],
                              stats[:, 24 + h0:24 + h1], ALU.mult).then_inc(s_v.h, 1)
        s_v.n += 1
        v_musq = s_v.n

        @emit("V")
        def _(eng, h0=h0, h1=h1, v_musq=v_musq, sw=sw):
            eng.wait_ge(s_v.h, v_musq)
            eng.wait_ge(s_s.h, sw)
            eng.scalar_tensor_tensor(stats[:, 48 + h0:48 + h1],
                                     stats[:, 12 + h0:12 + h1], 1.0 / D,
                                     stats[:, 48 + h0:48 + h1], ALU.mult,
                                     ALU.subtract).then_inc(s_v.h, 1)
        s_v.n += 1
        v_var = s_v.n

        @emit("V")
        def _(eng, h0=h0, h1=h1, v_var=v_var):
            eng.wait_ge(s_v.h, v_var)
            eng.tensor_scalar(stats[:, 48 + h0:48 + h1], stats[:, 48 + h0:48 + h1],
                              EPS, None, ALU.add).then_inc(s_v.h, 1)
        s_v.n += 1
        v_vpe = s_v.n

        @emit("S")
        def _(eng, h0=h0, h1=h1, v_vpe=v_vpe):
            eng.wait_ge(s_v.h, v_vpe)
            eng.activation(stats[:, 36 + h0:36 + h1], stats[:, 48 + h0:48 + h1],
                           AF.Sqrt).then_inc(s_s.h, 1)
        s_s.n += 1
        s_sqrt = s_s.n

        @emit("V")
        def _(eng, h0=h0, h1=h1, s_sqrt=s_sqrt):
            eng.wait_ge(s_s.h, s_sqrt)
            eng.reciprocal(stats[:, 36 + h0:36 + h1],
                           stats[:, 36 + h0:36 + h1]).then_inc(s_v.h, 1)
        s_v.n += 1
        rstd_v = s_v.n
        for j in range(h0, h1):
            _emit_ln_norm(call, j, st["tts"][j], st, rstd_v)

    def _emit_ln_norm(call, i, tt, st, rstd_v):
        slot = i % 2
        tp_wait = 256 * xnt_uses[slot]
        xnt_uses[slot] += 1

        @emit("V")
        def _(eng, i=i, tt=tt, slot=slot, tp_wait=tp_wait, rstd_v=rstd_v):
            eng.wait_ge(s_v.h, rstd_v)
            if tp_wait:
                eng.wait_ge(s_tp[slot].h, tp_wait)
            eng.tensor_scalar(xn_tmp[slot][:], x_tile(tt), stats[:, 24 + i:25 + i],
                              stats[:, 36 + i:37 + i], ALU.subtract,
                              ALU.mult).then_inc(s_v.h, 1)
        s_v.n += 1
        v_norm = s_v.n

        @emit("Y" if slot == 0 else "S")
        def _(eng, i=i, tt=tt, slot=slot, v_norm=v_norm, xg=st["xg"]):
            eng.wait_ge(s_v.h, v_norm)
            if i <= 1:
                eng.wait_ge(s_mm.h, xg)
            for kt in range(16):
                eng.dma_start(out=XNT[:, kt * TOKW + tt * 128:kt * TOKW + tt * 128 + 128],
                              in_=xn_tmp[slot][:, kt * 128:(kt + 1) * 128],
                              transpose=True).then_inc(s_tp[slot].h, 16)
        s_tp[slot].n += 256
        st["thr"][tt] = (slot, s_tp[slot].n)

    def _new_ln_state(tts, tp_prev):
        return {"sq": {}, "red": {}, "thr": {}, "done": [],
                "tts": list(tts), "tp_prev": tp_prev, "xg": last_xnt_reader[0]}

    def make_ln_hook(call, tts, dep, tp_prev):
        st = _new_ln_state(tts, tp_prev)
        n = len(tts)

        q = n // 4

        def hook(tt, td):
            if not st["done"]:
                st["xg"] = last_xnt_reader[0]
            i = tts.index(tt)
            emit_ln_tile_stats(call, i, tt, dep, td, st)
            st["done"].append(tt)
            if (i + 1) % q == 0:
                emit_ln_batch(call, st, i + 1 - q, i + 1)
        return hook, st

    def emit_ln(call, tts, dep, dep_tile=None, hook_state=None, tp_prev=None):
        n = len(tts)
        if hook_state is None:
            st = _new_ln_state(tts, tp_prev)
            st["xg"] = last_xnt_reader[0]
            q = n // 4
            for i, tt in enumerate(tts):
                td = dep_tile.get(tt) if dep_tile else None
                emit_ln_tile_stats(call, i, tt, dep, td, st)
                if (i + 1) % q == 0:
                    emit_ln_batch(call, st, i + 1 - q, i + 1)
        else:
            st = hook_state
            assert len(st["done"]) == len(tts), st["done"]
        return (s_tp[0].n, s_tp[1].n), st["thr"]

    def emit_routing(call, tts, xnt_ready, tile_thr):
        n = len(tts)
        for i, tt in enumerate(tts):
            b = 6 + (i % 2)

            @emit("T")
            def _(eng, i=i, tt=tt, b=b, bfw=bw(b), call=call, thr=tile_thr[tt]):
                eng.wait_ge(s_tp[thr[0]].h, thr[1])
                if i == 0 and call == 0:
                    eng.wait_ge(s_stpG.h, setupG_val)
                wait_pairs(eng, [bfw])
                mm = None
                for kt in range(16):
                    mm = eng.matmul(pb[b][:, 0:8],
                                    XNT[:, kt * TOKW + tt * 128:kt * TOKW + tt * 128 + 128],
                                    wg_sb[:, kt * 8:(kt + 1) * 8],
                                    start=(kt == 0), stop=(kt == 15),
                                    skip_group_check=True)
                mm.then_inc(s_mm.h, 1)
            s_mm.n += 1
            bank_fill[b] = s_mm.n

            @emit("S")
            def _(eng, tt=tt, b=b, fill=bank_fill[b]):
                eng.wait_ge(s_mm.h, fill)
                eng.copy(lg_all[:, tt * 8:(tt + 1) * 8], pb[b][:, 0:8]).then_inc(s_s.h, 1)
            s_s.n += 1
            bank_free[b] = (s_s, s_s.n)
            s_lg = s_s.n

            @emit("V")
            def _(eng, tt=tt, s_lg=s_lg):
                eng.wait_ge(s_s.h, s_lg)
                eng.max(top_all[:, tt * 8:(tt + 1) * 8],
                        lg_all[:, tt * 8:(tt + 1) * 8]).then_inc(s_v.h, 1)
            s_v.n += 1
            v_max = s_v.n

            @emit("V")
            def _(eng, i=i, tt=tt, v_max=v_max):
                eng.wait_ge(s_v.h, v_max)
                eng.tensor_tensor(r_scr[:, i:i + 1], top_all[:, tt * 8:tt * 8 + 1],
                                  top_all[:, tt * 8 + 1:tt * 8 + 2],
                                  ALU.subtract).then_inc(s_v.h, 1)
            s_v.n += 1
        v_d12 = s_v.n

        @emit("S")
        def _(eng, v_d12=v_d12, n=n):
            eng.wait_ge(s_v.h, v_d12)
            eng.activation(r_scr[:, 16:16 + n], r_scr[:, 0:n],
                           AF.Sigmoid).then_inc(s_s.h, 1)
        s_s.n += 1
        s_sig = s_s.n

        @emit("V")
        def _(eng, s_sig=s_sig, n=n):
            eng.wait_ge(s_s.h, s_sig)
            eng.tensor_scalar(r_scr[:, 16:16 + n], r_scr[:, 16:16 + n], 0.5, None,
                              ALU.mult).then_inc(s_v.h, 1)
        s_v.n += 1
        v_w1 = s_v.n

        @emit("V")
        def _(eng, v_w1=v_w1, n=n):
            eng.wait_ge(s_v.h, v_w1)
            eng.tensor_scalar(r_scr[:, 32:32 + n], r_scr[:, 16:16 + n], -1.0, 0.5,
                              ALU.mult, ALU.add).then_inc(s_v.h, 1)
        s_v.n += 1
        prev = s_v.n

        for i, tt in enumerate(tts):
            @emit("V")
            def _(eng, i=i, tt=tt, pw=prev):
                eng.wait_ge(s_v.h, pw)
                sl = slice(tt * 8, (tt + 1) * 8)
                eng.tensor_scalar(m_scr[:, 0:8], lg_all[:, sl],
                                  top_all[:, tt * 8:tt * 8 + 1], None, ALU.is_equal)
                eng.tensor_scalar(m_scr[:, 8:16], lg_all[:, sl],
                                  top_all[:, tt * 8 + 1:tt * 8 + 2], None,
                                  ALU.is_equal).then_inc(s_v.h, 1)
            s_v.n += 1
            v_m = s_v.n

            @emit("V")
            def _(eng, i=i, v_m=v_m):
                eng.wait_ge(s_v.h, v_m)
                eng.tensor_scalar(m_scr[:, 8:16], m_scr[:, 8:16],
                                  r_scr[:, 32 + i:33 + i], None,
                                  ALU.mult).then_inc(s_v.h, 1)
            s_v.n += 1
            v_m2 = s_v.n

            @emit("V")
            def _(eng, i=i, tt=tt, v_m2=v_m2):
                eng.wait_ge(s_v.h, v_m2)
                eng.scalar_tensor_tensor(what_sb[:, tt * 8:(tt + 1) * 8],
                                         m_scr[:, 0:8], r_scr[:, 16 + i:17 + i],
                                         m_scr[:, 8:16], ALU.mult,
                                         ALU.add).then_inc(s_v.h, 1)
            s_v.n += 1
            prev = s_v.n
        return s_v.n

    def emit_ffn(call, tts, groups, v_routing, xnt_ready, region_guard, pre_drain=None, tile_hook=None):
        ng = len(groups)
        with contextlib.ExitStack() as fes:
            fsb = lambda nme, s, d: fes.enter_context(
                nc.sbuf_tensor(f"{nme}_c{call}", s, d))
            wi_sb = [fsb(f"wi_sb{i}", [128, 2 * 16 * 256], BF16) for i in range(2)]
            wo_sb = fsb("wo_sb", [128, 4 * 2048], BF16)
            inter = fsb("inter", [128, 4 * TOKW], BF16)

            rg = dict(region_guard) if isinstance(region_guard, dict) else {
                "wi0": region_guard, "wi1": region_guard, "wo": region_guard,
                "inter": region_guard}
            wi_state = {"n": 0, "free": {-2: rg["wi0"], -1: rg["wi1"]}}
            wo_state = {"n": 0, "free": {-1: rg["wo"]}}

            def load_wi(e, q):
                nl = wi_state["n"]
                slot = nl % 2
                gw = wi_state["free"].get(nl - 2)

                @emit("G")
                def _(eng, e=e, q=q, slot=slot, gw=gw):
                    if gw is not None:
                        wait_pairs(eng, gw if isinstance(gw, list) else [gw])
                    eng.dma_start(
                        out=wi_sb[slot][:, 0:4096].rearrange("p (k j) -> p k j", k=16),
                        in_=dr_wi0[e, q]).then_inc(s_wi[slot].h, 16)
                    eng.dma_start(
                        out=wi_sb[slot][:, 4096:8192].rearrange("p (k j) -> p k j", k=16),
                        in_=dr_wi1[e, q]).then_inc(s_wi[slot].h, 16)
                s_wi[slot].n += 32
                wi_state["n"] += 1
                return slot, s_wi[slot].n, nl

            def load_wo(e, ha):
                nl = wo_state["n"]
                gw = wo_state["free"].get(nl - 1)

                @emit("G")
                def _(eng, e=e, ha=ha, gw=gw):
                    if gw is not None:
                        wait_pairs(eng, gw if isinstance(gw, list) else [gw])
                    eng.dma_start(out=wo_sb[:].rearrange("p (k j) -> p k j", k=4),
                                  in_=dr_wo[e, ha]).then_inc(s_wo[0].h, 16)
                s_wo[0].n += 16
                wo_state["n"] += 1
                return s_wo[0].n, nl

            inter_guard = [rg["inter"]]
            wo_bank_ctr = [0]
            tile_evac = {}
            h_done_guard = [None]

            def emit_wo(e, ha, inter_ready):
                wo_val, wo_idx = load_wo(e, ha)
                for j, tt in enumerate(tts):
                    record_evac = (e == E - 1 and ha == 1)
                    bs = (0, 1, 2, 3) if (wo_bank_ctr[0] % 2 == 0) else (4, 5, 6, 7)
                    wo_bank_ctr[0] += 1
                    bfws = [bw(b) for b in bs]

                    @emit("T")
                    def _(eng, tt=tt, bs=bs, j=j, wo_val=wo_val,
                          inter_ready=inter_ready, bfws=bfws):
                        if j == 0:
                            eng.wait_ge(s_wo[0].h, wo_val)
                            eng.wait_ge(s_v.h, inter_ready)
                        wait_pairs(eng, bfws)
                        mm = None
                        for kt in range(4):
                            for dg in range(4):
                                mm = eng.matmul(
                                    pb[bs[dg]][:],
                                    inter[:, kt * TOKW + tt * 128:kt * TOKW + tt * 128 + 128],
                                    wo_sb[:, kt * 2048 + dg * 512:kt * 2048 + dg * 512 + 512],
                                    start=(kt == 0), stop=(kt == 3),
                                    skip_group_check=True)
                        mm.then_inc(s_mm.h, 1)
                    s_mm.n += 1
                    for b in bs:
                        bank_fill[b] = s_mm.n

                    @emit("V")
                    def _(eng, e=e, tt=tt, bs=bs, fill=s_mm.n):
                        eng.wait_ge(s_mm.h, fill)
                        xa = x_tile(tt)
                        last = None
                        for dg in range(4):
                            xs = xa[:, dg * 512:(dg + 1) * 512]
                            last = eng.scalar_tensor_tensor(
                                xs, pb[bs[dg]][:], what_sb[:, tt * 8 + e:tt * 8 + e + 1],
                                xs, ALU.mult, ALU.add)
                        last.then_inc(s_v.h, 1)
                    s_v.n += 1
                    for b in bs:
                        bank_free[b] = (s_v, s_v.n)
                    if record_evac:
                        tile_evac[tt] = s_v.n
                        if tile_hook is not None:
                            tile_hook(tt, s_v.n)
                wo_state["free"][wo_idx] = (s_mm, s_mm.n)
                inter_guard[0] = (s_mm, s_mm.n)

            wi_cur = None
            for e in range(E):
                for mt in range(8):
                    q, idx = mt // 2, mt % 2
                    if idx == 0:
                        wi_cur = load_wi(e, q)
                    wi_slot, wi_val, wi_idx = wi_cur
                    fills = {}
                    for w in (0, 1):
                        banks = (0, 1, 2) if w == 0 else (3, 4, 5)
                        bfws = [bw(banks[gi]) for gi in range(ng)]

                        @emit("T")
                        def _(eng, e=e, mt=mt, w=w, banks=banks, idx=idx,
                              wi_slot=wi_slot, wi_val=wi_val, bfws=bfws,
                              groups=list(groups), xnt_ready=xnt_ready):
                            if idx == 0 and w == 0:
                                eng.wait_ge(s_wi[wi_slot].h, wi_val)
                            if e == 0 and mt == 0 and w == 0:
                                eng.wait_ge(s_tp[0].h, xnt_ready[0])
                                eng.wait_ge(s_tp[1].h, xnt_ready[1])
                            wait_pairs(eng, bfws)
                            woff = w * 4096
                            mm = None
                            for kt in range(16):
                                for gi, g0 in enumerate(groups):
                                    mm = eng.matmul(
                                        pb[banks[gi]][:],
                                        wi_sb[wi_slot][:, woff + kt * 256 + idx * 128:
                                                       woff + kt * 256 + idx * 128 + 128],
                                        XNT[:, kt * TOKW + g0:kt * TOKW + g0 + 512],
                                        start=(kt == 0), stop=(kt == 15),
                                        skip_group_check=True)
                            mm.then_inc(s_mm.h, 1)
                        s_mm.n += 1
                        for gi in range(ng):
                            bank_fill[banks[gi]] = s_mm.n
                        fills[w] = s_mm.n
                    if idx == 1:
                        wi_state["free"][wi_idx] = (s_mm, s_mm.n)

                    ig = inter_guard[0] if mt % 4 == 0 else None

                    @emit("S")
                    def _(eng, mt=mt, h0f=fills[0], ng=ng, groups=list(groups), ig=ig):
                        eng.wait_ge(s_mm.h, h0f)
                        if ig is not None:
                            wait_pairs(eng, ig if isinstance(ig, list) else [ig])
                        last = None
                        for gi, g0 in enumerate(groups):
                            last = eng.activation(
                                inter[:, (mt % 4) * TOKW + g0:(mt % 4) * TOKW + g0 + 512],
                                pb[gi][:], AF.Silu)
                        last.then_inc(s_s.h, 1)
                    s_s.n += 1
                    for b in (0, 1, 2)[:ng]:
                        bank_free[b] = (s_s, s_s.n)
                    s_silu = s_s.n

                    @emit("V")
                    def _(eng, mt=mt, s_silu=s_silu, h1f=fills[1], ng=ng,
                          groups=list(groups)):
                        eng.wait_ge(s_s.h, s_silu)
                        eng.wait_ge(s_mm.h, h1f)
                        last = None
                        for gi, g0 in enumerate(groups):
                            sl = slice((mt % 4) * TOKW + g0, (mt % 4) * TOKW + g0 + 512)
                            last = eng.tensor_tensor(inter[:, sl], inter[:, sl],
                                                     pb[(3, 4, 5)[gi]][:], ALU.mult)
                        last.then_inc(s_v.h, 1)
                    s_v.n += 1
                    for b in (3, 4, 5)[:ng]:
                        bank_free[b] = (s_v, s_v.n)
                    if e == E - 1 and mt == 7:
                        h_done_guard[0] = (s_mm, s_mm.n)
                    if mt % 4 == 3:
                        emit_wo(e, mt // 4, s_v.n)

            g = guard_now()
            if pre_drain is not None:
                pre_drain()
            return g, tile_evac, h_done_guard[0]

    def emit_attention(xnt_ready, tile_thr, region_guard, early_guard,
                       tile_hook=None):
        with contextlib.ExitStack() as aes:
            asb = lambda nme, s, d: aes.enter_context(nc.sbuf_tensor(nme, s, d))
            wv_sb = asb("wv_sb", [128, 16 * 256], BF16)
            wqk_sb = [asb(f"wqk_sb{i}", [128, 2 * 16 * 128], BF16) for i in range(2)]
            wout_sb = [asb(f"wout_sb{i}", [128, 4 * 512], BF16) for i in range(2)]
            av_sb = asb("av_sb", [128, 4 * 1024], BF16)
            E_sb = [asb(f"E_sb{i}", [128, 512], BF16) for i in range(2)]
            v4 = asb("v4", [128, NT3 * 256], BF16)
            q_sb = asb("q_sb", [128, 1024], BF16)
            k_sb = asb("k_sb", [128, TOKW], BF16)
            masks = asb("masks", [128, 8 * 512], BF16)
            mpre = asb("mpre_sb", [128, 4 * 512], BF16)

            masks_state = [False]

            def emit_masks_load():
                @emit("G")
                def _(eng, rg=list(region_guard)):
                    wait_pairs(eng, rg)
                    eng.dma_start(out=masks[:].rearrange("p (j i) -> p j i", j=8),
                                  in_=dr_band).then_inc(s_mask.h, 16)
                    eng.dma_start(out=mpre[:].rearrange("p (j i) -> p j i", j=4),
                                  in_=dr_mpre).then_inc(s_mask.h, 16)
                s_mask.n += 32
                masks_state[0] = True
            mask_val = s_mask.n + 32

            wqk_state = {"n": 0, "free": {-2: [early_guard], -1: [early_guard]}}
            wv_state = {"n": 0, "free": {-1: [early_guard]}}
            wout_state = {"n": 0, "free": {-2: [early_guard], -1: [early_guard]}}

            def load_wqk(h):
                nl = wqk_state["n"]
                slot = nl % 2
                gw = wqk_state["free"].get(nl - 2)

                @emit("G")
                def _(eng, h=h, slot=slot, gw=gw):
                    if gw is not None:
                        wait_pairs(eng, gw if isinstance(gw, list) else [gw])
                    eng.dma_start(
                        out=wqk_sb[slot][:, 0:2048].rearrange("p (k j) -> p k j", k=16),
                        in_=dr_wqk[h]).then_inc(s_wqk[slot].h, 16)
                    eng.dma_start(
                        out=wqk_sb[slot][:, 2048:4096].rearrange("p (k j) -> p k j", k=16),
                        in_=dr_wqk[16 + h]).then_inc(s_wqk[slot].h, 16)
                s_wqk[slot].n += 32
                wqk_state["n"] += 1
                return slot, s_wqk[slot].n, nl

            def load_wv(vg):
                nl = wv_state["n"]
                gw = wv_state["free"].get(nl - 1)

                @emit("G")
                def _(eng, vg=vg, gw=gw):
                    if gw is not None:
                        wait_pairs(eng, gw if isinstance(gw, list) else [gw])
                    eng.dma_start(out=wv_sb[:].rearrange("p (k j) -> p k j", k=16),
                                  in_=dr_wv[vg]).then_inc(s_wv.h, 16)
                s_wv.n += 16
                wv_state["n"] += 1
                return s_wv.n, nl

            def load_wout(qset, dg):
                nl = wout_state["n"]
                slot = nl % 2
                gw = wout_state["free"].get(nl - 2)

                @emit("G")
                def _(eng, qset=qset, dg=dg, slot=slot, gw=gw):
                    if gw is not None:
                        wait_pairs(eng, gw if isinstance(gw, list) else [gw])
                    eng.dma_start(out=wout_sb[slot][:].rearrange("p (k j) -> p k j", k=4),
                                  in_=dr_wout[qset, dg]).then_inc(s_wout[slot].h, 16)
                s_wout[slot].n += 16
                wout_state["n"] += 1
                return slot, s_wout[slot].n, nl

            negc_done = [None]
            att_tile_evac = {}
            pending_tail = []
            att_qk_done = [None]
            v4_guard = [region_guard]
            qk_guard = [region_guard]
            av_guard = [region_guard]
            rsbc_guard = [None]
            qkb_ctr = [0]

            def emit_scores(jj, jtile, b, ch, qk_ready, first):
                bfw_ = bw(b)

                @emit("T")
                def _(eng, jj=jj, jtile=jtile, b=b, ch=ch, bfw_=bfw_,
                      qk_ready=qk_ready, first=first, mv=mask_val):
                    if jj == 0:
                        eng.wait_ge(s_v.h, qk_ready[0])
                        eng.wait_ge(s_s.h, qk_ready[1])
                        if first:
                            eng.wait_ge(s_mask.h, mv)
                            eng.wait_ge(s_stpG.h, setupG_val)
                    wait_pairs(eng, [bfw_])
                    eng.matmul(pb[b][:], k_sb[:, jtile * 128:(jtile + 1) * 128],
                               q_sb[:, ch * 512:(ch + 1) * 512],
                               start=True, stop=False, skip_group_check=True)
                    if ch == 1 and jj < 4:
                        msk = mpre[:, jj * 512:(jj + 1) * 512]
                    else:
                        msk = masks[:, jj * 512:(jj + 1) * 512]
                    eng.matmul(pb[b][:], ident_bf[:], msk, start=False, stop=True,
                               skip_group_check=True).then_inc(s_mm.h, 1)
                s_mm.n += 1
                bank_fill[b] = s_mm.n
                return s_mm.n

            def emit_exp_avsum(jj, jtile, b, hh, e_fill, av_bank, sum_bank, sc_fill):
                eslot = jj % 2
                e_prev = e_fill.get(jj - 2)
                ndv = negc_done[0]

                @emit("S")
                def _(eng, jj=jj, b=b, eslot=eslot, sc_fill=sc_fill, e_prev=e_prev,
                      ndv=ndv):
                    eng.wait_ge(s_mm.h, sc_fill)
                    eng.wait_ge(s_v.h, ndv)
                    if e_prev is not None:
                        eng.wait_ge(s_mm.h, e_prev)
                    eng.activation(E_sb[eslot][:], pb[b][:], AF.Exp,
                                   bias=neg_c[:, 0:1]).then_inc(s_s.h, 1)
                s_s.n += 1
                bank_free[b] = (s_s, s_s.n)
                s_exp = s_s.n

                abw = bw(av_bank) if jj == 0 else None
                sbw = bw(sum_bank) if jj == 0 else None

                @emit("T")
                def _(eng, hh=hh, jj=jj, jtile=jtile, eslot=eslot, s_exp=s_exp,
                      abw=abw, sbw=sbw, av_bank=av_bank, sum_bank=sum_bank):
                    eng.wait_ge(s_s.h, s_exp)
                    if jj == 0:
                        wait_pairs(eng, [abw, sbw])
                    eng.matmul(pb[av_bank][:],
                               v4[:, jtile * 256 + hh * 128:jtile * 256 + hh * 128 + 128],
                               E_sb[eslot][:], start=(jj == 0), stop=(jj == 7),
                               skip_group_check=True)
                    eng.matmul(pb[sum_bank][0:1, :], ones_col_bf[:], E_sb[eslot][:],
                               start=(jj == 0), stop=(jj == 7),
                               skip_group_check=True).then_inc(s_mm.h, 1)
                s_mm.n += 1
                e_fill[jj] = s_mm.n
                if jj == 7:
                    bank_fill[av_bank] = s_mm.n
                    bank_fill[sum_bank] = s_mm.n

            for qset in range(4):
                av_writes = []
                for vgi in range(2):
                    vg = qset * 2 + vgi
                    wv_val, wv_idx = load_wv(vg)
                    v4g = v4_guard[0]
                    for tt in range(NT3):
                        b = tt % 2

                        @emit("T")
                        def _(eng, vg=vg, tt=tt, b=b, wv_val=wv_val, bfw=bw(b),
                              thr=tile_thr[tt]):
                            if tt == 0:
                                eng.wait_ge(s_wv.h, wv_val)
                            if vg == 0:
                                eng.wait_ge(s_tp[thr[0]].h, thr[1])
                            wait_pairs(eng, [bfw])
                            mm = None
                            for kt in range(16):
                                mm = eng.matmul(
                                    pb[b][:, 0:256],
                                    XNT[:, kt * TOKW + tt * 128:kt * TOKW + tt * 128 + 128],
                                    wv_sb[:, kt * 256:(kt + 1) * 256],
                                    start=(kt == 0), stop=(kt == 15),
                                    skip_group_check=True)
                            mm.then_inc(s_mm.h, 1)
                        s_mm.n += 1
                        bank_fill[b] = s_mm.n
                        last_xnt_reader[0] = s_mm.n

                        @emit("V")
                        def _(eng, tt=tt, b=b, fill=bank_fill[b], v4g=v4g):
                            eng.wait_ge(s_mm.h, fill)
                            if tt == 0:
                                wait_pairs(eng, v4g if isinstance(v4g, list) else [v4g])
                            eng.tensor_copy(v4[:, tt * 256:(tt + 1) * 256],
                                            pb[b][:, 0:256]).then_inc(s_v.h, 1)
                        s_v.n += 1
                        bank_free[b] = (s_v, s_v.n)
                    wv_state["free"][wv_idx] = (s_mm, s_mm.n)

                    for hh2 in range(2):
                        h = vg * 2 + hh2
                        wqk_slot, wqk_val, wqk_idx = load_wqk(h)
                        if not masks_state[0]:
                            emit_masks_load()
                        qkg = qk_guard[0]
                        for which, ngr, base in ((0, 2, 512), (1, 3, 0)):
                            for g in range(ngr):
                                b = 2 + (qkb_ctr[0] % 2)
                                qkb_ctr[0] += 1

                                tls = [tile_thr[t] for t in
                                       range((base + g * 512) // 128,
                                             (base + g * 512) // 128 + 4)]
                                gthr = [(sl, max(v for s2, v in tls if s2 == sl))
                                        for sl in {s2 for s2, _ in tls}]

                                @emit("T")
                                def _(eng, which=which, g=g, b=b, base=base,
                                      wqk_slot=wqk_slot, wqk_val=wqk_val, bfw=bw(b),
                                      gthr=gthr, h=h):
                                    if which == 0 and g == 0:
                                        eng.wait_ge(s_wqk[wqk_slot].h, wqk_val)
                                    if h == 0:
                                        for sl, v in gthr:
                                            eng.wait_ge(s_tp[sl].h, v)
                                    wait_pairs(eng, [bfw])
                                    woff = which * 2048
                                    mm = None
                                    for kt in range(16):
                                        mm = eng.matmul(
                                            pb[b][:],
                                            wqk_sb[wqk_slot][:, woff + kt * 128:
                                                             woff + kt * 128 + 128],
                                            XNT[:, kt * TOKW + base + g * 512:
                                                kt * TOKW + base + g * 512 + 512],
                                            start=(kt == 0), stop=(kt == 15),
                                            skip_group_check=True)
                                    mm.then_inc(s_mm.h, 1)
                                s_mm.n += 1
                                bank_fill[b] = s_mm.n
                                last_xnt_reader[0] = s_mm.n

                                if which == 0:
                                    @emit("V")
                                    def _(eng, g=g, b=b, fill=bank_fill[b], qkg=qkg):
                                        eng.wait_ge(s_mm.h, fill)
                                        if g == 0:
                                            wait_pairs(eng, qkg if isinstance(qkg, list) else [qkg])
                                        eng.tensor_scalar(
                                            q_sb[:, g * 512:(g + 1) * 512], pb[b][:],
                                            QSCALE, None, ALU.mult).then_inc(s_v.h, 1)
                                    s_v.n += 1
                                    bank_free[b] = (s_v, s_v.n)
                                else:
                                    @emit("S")
                                    def _(eng, g=g, b=b, fill=bank_fill[b], qkg=qkg):
                                        eng.wait_ge(s_mm.h, fill)
                                        if g == 0:
                                            wait_pairs(eng, qkg if isinstance(qkg, list) else [qkg])
                                        eng.copy(k_sb[:, g * 512:(g + 1) * 512],
                                                 pb[b][:]).then_inc(s_s.h, 1)
                                    s_s.n += 1
                                    bank_free[b] = (s_s, s_s.n)
                        qk_ready = (s_v.n, s_s.n)
                        wqk_state["free"][wqk_idx] = (s_mm, s_mm.n)

                        for ch in range(2):
                            jt_tiles = list(range(0, 8)) if ch == 0 else list(range(4, 12))
                            first = (h == 0 and ch == 0)
                            av_bank, sum_bank = 4 + ((h * 2 + ch) % 2), 6
                            e_fill = {}

                            if first:
                                first_banks = [2, 3, 7, 1]
                                vmasks = []
                                for jj in range(4):
                                    b = first_banks[jj]
                                    vmasks.append(emit_scores(jj, jt_tiles[jj], b, ch,
                                                              qk_ready, first))

                                    @emit("V")
                                    def _(eng, jj=jj, b=b, vm=vmasks[-1]):
                                        eng.wait_ge(s_mm.h, vm)
                                        eng.reduce_max(cchain[:, jj:jj + 1], pb[b][:],
                                                       AXX).then_inc(s_v.h, 1)
                                    s_v.n += 1
                                v_rm = s_v.n

                                @emit("V")
                                def _(eng, v_rm=v_rm):
                                    eng.wait_ge(s_v.h, v_rm)
                                    eng.reduce_max(cchain[:, 4:5], cchain[:, 0:4],
                                                   AXX).then_inc(s_v.h, 1)
                                s_v.n += 1
                                v_m8 = s_v.n

                                @emit("T")
                                def _(eng, v_m8=v_m8, bfw=bw(0)):
                                    eng.wait_ge(s_v.h, v_m8)
                                    wait_pairs(eng, [bfw])
                                    eng.transpose(pb[0][0:1, 0:128], cchain[:, 4:5],
                                                  ident_sb[:]).then_inc(s_mm.h, 1)
                                s_mm.n += 1
                                bank_fill[0] = s_mm.n
                                f0 = s_mm.n

                                @emit("V")
                                def _(eng, f0=f0):
                                    eng.wait_ge(s_mm.h, f0)
                                    eng.reduce_max(cmax[:], pb[0][0:1, 0:128],
                                                   AXX).then_inc(s_v.h, 1)
                                s_v.n += 1
                                bank_free[0] = (s_v, s_v.n)
                                v_cm = s_v.n

                                @emit("T")
                                def _(eng, v_cm=v_cm, bfw=bw(0)):
                                    eng.wait_ge(s_v.h, v_cm)
                                    wait_pairs(eng, [bfw])
                                    eng.matmul(pb[0][:, 0:1], ones_row_f[:], cmax[:],
                                               start=True, stop=True,
                                               skip_group_check=True).then_inc(s_mm.h, 1)
                                s_mm.n += 1
                                bank_fill[0] = s_mm.n
                                f0b = s_mm.n

                                @emit("V")
                                def _(eng, f0b=f0b):
                                    eng.wait_ge(s_mm.h, f0b)
                                    eng.tensor_scalar(neg_c[:], pb[0][:, 0:1], -1.0,
                                                      None, ALU.mult).then_inc(s_v.h, 1)
                                s_v.n += 1
                                bank_free[0] = (s_v, s_v.n)
                                negc_done[0] = s_v.n

                                while pending_tail:
                                    pending_tail.pop(0)()
                                for jj in range(4):
                                    emit_exp_avsum(jj, jt_tiles[jj], first_banks[jj],
                                                   hh2, e_fill, av_bank, sum_bank,
                                                   vmasks[jj])
                                for jj in range(4, 8):
                                    b = 2 + (jj % 2)
                                    vm = emit_scores(jj, jt_tiles[jj], b, ch, qk_ready,
                                                     first)
                                    emit_exp_avsum(jj, jt_tiles[jj], b, hh2, e_fill,
                                                   av_bank, sum_bank, vm)
                                negc_ready_first = negc_done[0]
                            else:
                                sc_banks = [2, 3, 7]
                                fills_jj = {}
                                for jj in range(3):
                                    fills_jj[jj] = emit_scores(jj, jt_tiles[jj],
                                                               sc_banks[jj % 3], ch,
                                                               qk_ready, first)
                                while pending_tail:
                                    pending_tail.pop(0)()
                                for jj in range(8):
                                    emit_exp_avsum(jj, jt_tiles[jj], sc_banks[jj % 3],
                                                   hh2, e_fill, av_bank, sum_bank,
                                                   fills_jj[jj])
                                    if jj + 3 < 8:
                                        fills_jj[jj + 3] = emit_scores(
                                            jj + 3, jt_tiles[jj + 3],
                                            sc_banks[(jj + 3) % 3], ch, qk_ready, first)

                            avsum_fill = s_mm.n

                            def emit_tail(h=h, ch=ch, av_bank=av_bank,
                                          fill=avsum_fill, ag=av_guard[0],
                                          rg=rsbc_guard[0]):
                                @emit("V")
                                def _(eng, fill=fill):
                                    eng.wait_ge(s_mm.h, fill)
                                    eng.reciprocal(rs_sb[:],
                                                   pb[6][0:1, :]).then_inc(s_v.h, 1)
                                s_v.n += 1
                                bank_free[6] = (s_v, s_v.n)
                                v_rs = s_v.n

                                @emit("T")
                                def _(eng, v_rs=v_rs, bfw=bw(6)):
                                    eng.wait_ge(s_v.h, v_rs)
                                    wait_pairs(eng, [bfw])
                                    eng.matmul(pb[6][:], ones_row_f[:], rs_sb[:],
                                               start=True, stop=True,
                                               skip_group_check=True).then_inc(s_mm.h, 1)
                                s_mm.n += 1
                                bank_fill[6] = s_mm.n
                                f_bc = s_mm.n

                                @emit("S")
                                def _(eng, f_bc=f_bc, rg=rg):
                                    eng.wait_ge(s_mm.h, f_bc)
                                    if rg is not None:
                                        eng.wait_ge(s_v.h, rg)
                                    eng.copy(rsbc_sb[:], pb[6][:]).then_inc(s_s.h, 1)
                                s_s.n += 1
                                bank_free[6] = (s_s, s_s.n)
                                s_bc = s_s.n

                                @emit("V")
                                def _(eng, h=h, ch=ch, s_bc=s_bc, ag=ag,
                                      av_bank=av_bank):
                                    eng.wait_ge(s_s.h, s_bc)
                                    wait_pairs(eng, ag if isinstance(ag, list) else [ag])
                                    hl = h % 4
                                    eng.tensor_tensor(
                                        av_sb[:, hl * 1024 + ch * 512:hl * 1024 + ch * 512 + 512],
                                        pb[av_bank][:], rsbc_sb[:],
                                        ALU.mult).then_inc(s_v.h, 1)
                                s_v.n += 1
                                bank_free[av_bank] = (s_v, s_v.n)
                                av_writes.append(s_v.n)
                                rsbc_guard[0] = s_v.n

                            pending_tail.append(emit_tail)
                        qk_guard[0] = (s_mm, s_mm.n)
                        if h == 15:
                            att_qk_done[0] = (s_mm, s_mm.n)
                    v4_guard[0] = (s_mm, s_mm.n)

                while pending_tail:
                    pending_tail.pop(0)()
                av_ready = max(av_writes)
                for dg in range(4):
                    wslot, wval, widx = load_wout(qset, dg)
                    for j, tt in enumerate(range(4, 12)):
                        b = 6 + (j % 2)

                        @emit("T")
                        def _(eng, dg=dg, tt=tt, b=b, j=j, wslot=wslot, wval=wval,
                              av_ready=av_ready, bfw=bw(b)):
                            if j == 0:
                                eng.wait_ge(s_wout[wslot].h, wval)
                                eng.wait_ge(s_v.h, av_ready)
                            wait_pairs(eng, [bfw])
                            ch2 = (tt - 4) // 4
                            pos = ((tt - 4) % 4) * 128
                            mm = None
                            for ekt in range(4):
                                mm = eng.matmul(
                                    pb[b][:],
                                    av_sb[:, ekt * 1024 + ch2 * 512 + pos:
                                          ekt * 1024 + ch2 * 512 + pos + 128],
                                    wout_sb[wslot][:, ekt * 512:(ekt + 1) * 512],
                                    start=(ekt == 0), stop=(ekt == 3),
                                    skip_group_check=True)
                            mm.then_inc(s_mm.h, 1)
                        s_mm.n += 1
                        bank_fill[b] = s_mm.n

                        @emit("V")
                        def _(eng, tt=tt, b=b, fill=bank_fill[b], dg=dg):
                            eng.wait_ge(s_mm.h, fill)
                            xs = X[:, (tt - 4) * D + dg * 512:(tt - 4) * D + dg * 512 + 512]
                            eng.tensor_tensor(xs, xs, pb[b][:], ALU.add).then_inc(s_v.h, 1)
                        s_v.n += 1
                        bank_free[b] = (s_v, s_v.n)
                        if qset == 3 and dg == 3:
                            att_tile_evac[tt] = s_v.n
                            if tile_hook is not None:
                                tile_hook(tt, s_v.n)
                    wout_state["free"][widx] = (s_mm, s_mm.n)
                av_guard[0] = (s_mm, s_mm.n)

            g = guard_now()
            if stage == "attn":
                _dbg(emit, X, X_halo, dbg, s_v, s_st)
            return g, att_tile_evac, att_qk_done[0]

    def _dbg(emit_, X_, Xh_, dbg_, sv_, sst_):
        rv = sv_.n

        @emit_("Y")
        def _(eng, rv=rv):
            eng.wait_ge(sv_.h, rv)
            eng.dma_start(out=dbg_["own"].rearrange("(t p) d -> p t d", p=128),
                          in_=X_[:].rearrange("p (t d) -> p t d", t=NT_OWN)
                          ).then_inc(sst_.h, 16)
        sst_.n += 16

        @emit_("G")
        def _(eng, rv=rv):
            eng.wait_ge(sv_.h, rv)
            eng.dma_start(out=dbg_["halo"].rearrange("(t p) d -> p t d", p=128),
                          in_=Xh_[:].rearrange("p (t d) -> p t d", t=4)
                          ).then_inc(sst_.h, 16)
        sst_.n += 16

        @emit_("Y")
        def _(eng, tot=sst_.n):
            eng.wait_ge(sst_.h, tot)

    xr, thr1 = emit_ln(0, list(range(12)), [])
    vr = emit_routing(0, list(range(12)), xr, thr1)
    pre1 = (lambda: _dbg(emit, X, X_halo, dbg, s_v, s_st)) if stage == "ffn1" else None
    ln2_hook, ln2_state = make_ln_hook(1, list(range(12)), [], xr)
    g_ffn1, te1, g_h7 = emit_ffn(0, list(range(12)), [0, 512, 1024], vr, xr,
                                 guard_now(), pre_drain=pre1,
                                 tile_hook=None if stage == "ffn1" else ln2_hook)
    if stage == "ffn1":
        drain()
        es.close()
        return nc

    xr2, thr2 = emit_ln(1, list(range(12)), [], hook_state=ln2_state)
    ln3_hook, ln3_state = make_ln_hook(2, list(range(4, 12)), [], xr2)
    g_att, te2, g_qk15 = emit_attention(xr2, thr2,
                                        g_ffn1 + [(s_v, s_v.n), (s_s, s_s.n)],
                                        g_h7,
                                        tile_hook=None if stage == "attn" else ln3_hook)
    if stage == "attn":
        drain()
        es.close()
        return nc

    xr3, thr3 = emit_ln(2, list(range(4, 12)), [], hook_state=ln3_state)
    vr3 = emit_routing(2, list(range(4, 12)), xr3, thr3)

    def pre_final():
        final_v = s_v.n

        @emit("Y")
        def _(eng, final_v=final_v):
            eng.wait_ge(s_v.h, final_v)
            eng.dma_start(out=dr_y.rearrange("(t p) d -> p t d", p=128),
                          in_=X[:].rearrange("p (t d) -> p t d", t=NT_OWN)
                          ).then_inc(s_st.h, 16)
        s_st.n += 16

        @emit("Y")
        def _(eng, tot=s_st.n):
            eng.wait_ge(s_st.h, tot)

    full_att_guard = g_att + [(s_v, s_v.n), (s_s, s_s.n)]
    emit_ffn(2, list(range(4, 12)), [512, 1024], vr3, xr3,
             {"wi0": [g_qk15], "wi1": full_att_guard, "wo": full_att_guard,
              "inter": full_att_guard}, pre_drain=pre_final)
    drain()
    es.close()
    return nc


def host_inputs(L0_attn_x, L0_attn_w_qkv, L0_attn_w_out, L0_ffn_w_gate,
                L0_ffn_wi_0, L0_ffn_wi_1, L0_ffn_wo):
    x = np.ascontiguousarray(np.asarray(L0_attn_x, dtype=np.float32).reshape(8192, D))
    wqkT = np.ascontiguousarray(np.asarray(L0_attn_w_qkv, dtype=np.float32).T)
    woutT = np.ascontiguousarray(np.asarray(L0_attn_w_out, dtype=np.float32).T)
    wg = np.asarray(L0_ffn_w_gate, dtype=np.float32)
    wi0 = np.asarray(L0_ffn_wi_0, dtype=np.float32)
    wi1 = np.asarray(L0_ffn_wi_1, dtype=np.float32)
    wo = np.asarray(L0_ffn_wo, dtype=np.float32)

    wg_t = np.ascontiguousarray(wg.reshape(16, 128, 8).transpose(1, 0, 2))
    wi0_t = np.ascontiguousarray(wi0.reshape(E, 16, 128, 4, 256).transpose(0, 3, 2, 1, 4))
    wi1_t = np.ascontiguousarray(wi1.reshape(E, 16, 128, 4, 256).transpose(0, 3, 2, 1, 4))
    wo_t = np.ascontiguousarray(wo.reshape(E, 2, 4, 128, 2048).transpose(0, 1, 3, 2, 4))
    wqk_t = np.ascontiguousarray(
        wqkT[:, :4096].reshape(16, 128, 32, 128).transpose(2, 1, 0, 3))
    wv_t = np.ascontiguousarray(
        wqkT[:, 4096:].reshape(16, 128, 8, 256).transpose(2, 1, 0, 3))
    wout_t = np.ascontiguousarray(
        woutT.reshape(4, 4, 128, 4, 512).transpose(0, 3, 2, 1, 4))

    p = np.arange(128)[:, None]
    i = np.arange(512)[None, :]
    band = np.zeros((128, 8, 512), dtype=np.float32)
    for jt in range(8):
        j = 128 * jt + p
        ok = (i >= j - 512) & (i <= j)
        band[:, jt, :] = np.where(ok, 0.0, np.float32(-1e9))
    band = band.astype(BF)
    kill = np.full((128, 4, 512), -1e9, dtype=np.float32).astype(BF)
    ident = np.eye(128, dtype=np.float32)

    shared = dict(wg=wg_t, wi0=wi0_t, wi1=wi1_t, wo=wo_t, wqk=wqk_t, wv=wv_t,
                  wout=wout_t, band=band, ident=ident)
    in_maps = []
    for c in range(8):
        g0, g1, g2 = 2 * c, 2 * c + 1, (2 * c + 2) % 16
        x_halo = x[g0 * CH:(g0 + 1) * CH]
        x_own = np.concatenate([x[g1 * CH:(g1 + 1) * CH], x[g2 * CH:(g2 + 1) * CH]])
        q2_first = (g2 % 4 == 0)
        mpre_c = kill if q2_first else np.ascontiguousarray(band[:, 0:4])
        m = dict(shared)
        m["x_own"] = np.ascontiguousarray(x_own)
        m["x_halo"] = np.ascontiguousarray(x_halo)
        m["mpre"] = np.ascontiguousarray(mpre_c)
        in_maps.append(m)
    return in_maps


def assemble(results):
    y = np.empty((8192, D), dtype=np.float32)
    for c in range(8):
        g1, g2 = 2 * c + 1, (2 * c + 2) % 16
        yc = results[c]["y"]
        y[g1 * CH:(g1 + 1) * CH] = yc[0:CH]
        y[g2 * CH:(g2 + 1) * CH] = yc[CH:2 * CH]
    return y.reshape(4, 2048, D)


def kernel(L0_attn_x, L0_attn_w_qkv, L0_attn_w_out, L0_ffn_w_gate,
           L0_ffn_wi_0, L0_ffn_wi_1, L0_ffn_wo):
    in_maps = host_inputs(L0_attn_x, L0_attn_w_qkv, L0_attn_w_out, L0_ffn_w_gate,
                          L0_ffn_wi_0, L0_ffn_wi_1, L0_ffn_wo)
    nc = build_program("full")
    res = run_bass_kernel_spmd(nc, in_maps, core_ids=list(range(8)))
    return assemble(res.results)


# version 11
# speedup vs baseline: 1.1532x; 1.1532x over previous
import sys
sys.path.insert(0, "/opt/trn_rl_repo")
import contextlib
import numpy as np
import ml_dtypes
import concourse.bass as bass
import concourse.mybir as mybir
from concourse.bass_utils import run_bass_kernel_spmd

F32 = mybir.dt.float32
BF16 = mybir.dt.bfloat16
AF = mybir.ActivationFunctionType
ALU = mybir.AluOpType
AXX = mybir.AxisListType.X
BF = ml_dtypes.bfloat16

D = 2048
E = 8
CH = 512
NT3 = 12
NT_OWN = 8
TOKW = 1536
QSCALE = float(1.0 / np.float64(np.float32(np.sqrt(np.float64(128.0)))))
EPS = 1e-5


class Ctr:
    def __init__(self, h):
        self.h = h
        self.n = 0


def build_program(stage="full"):
    nc = bass.Bass("TRN2", target_bir_lowering=False)

    dr_x_own = nc.dram_tensor("x_own", [1024, D], F32, kind="ExternalInput").ap()
    dr_x_halo = nc.dram_tensor("x_halo", [512, D], F32, kind="ExternalInput").ap()
    dr_wg = nc.dram_tensor("wg", [128, 16, 8], F32, kind="ExternalInput").ap()
    dr_wi0 = nc.dram_tensor("wi0", [E, 4, 128, 16, 256], F32, kind="ExternalInput").ap()
    dr_wi1 = nc.dram_tensor("wi1", [E, 4, 128, 16, 256], F32, kind="ExternalInput").ap()
    dr_wo = nc.dram_tensor("wo", [E, 2, 128, 4, 2048], F32, kind="ExternalInput").ap()
    dr_wqk = nc.dram_tensor("wqk", [32, 128, 16, 128], F32, kind="ExternalInput").ap()
    dr_wv = nc.dram_tensor("wv", [8, 128, 16, 256], F32, kind="ExternalInput").ap()
    dr_wout = nc.dram_tensor("wout", [4, 4, 128, 4, 512], F32, kind="ExternalInput").ap()
    dr_band = nc.dram_tensor("band", [128, 8, 512], BF16, kind="ExternalInput").ap()
    dr_mpre = nc.dram_tensor("mpre", [128, 4, 512], BF16, kind="ExternalInput").ap()
    dr_ident = nc.dram_tensor("ident", [128, 128], F32, kind="ExternalInput").ap()
    dr_y = nc.dram_tensor("y", [1024, D], F32, kind="ExternalOutput").ap()
    dbg = {}
    if stage != "full":
        dbg["own"] = nc.dram_tensor("dbg_own", [1024, D], F32, kind="ExternalOutput").ap()
        dbg["halo"] = nc.dram_tensor("dbg_halo", [512, D], F32, kind="ExternalOutput").ap()

    es = contextlib.ExitStack()
    sbuf = lambda n, s, d: es.enter_context(nc.sbuf_tensor(n, s, d))

    X = sbuf("X", [128, NT_OWN * D], F32)
    XNT = sbuf("XNT", [128, 16 * TOKW], BF16)
    X_halo = sbuf("X_halo", [128, 4 * D], BF16)
    xn_tmp = [sbuf(f"xn_tmp{i}", [128, D], BF16) for i in range(2)]
    wg_sb = sbuf("wg_sb", [128, 16 * 8], BF16)
    what_sb = sbuf("what_sb", [128, NT3 * 8], F32)
    stats = sbuf("stats", [128, 64], F32)
    lg_all = sbuf("lg_all", [128, NT3 * 8], F32)
    top_all = sbuf("top_all", [128, NT3 * 8], F32)
    r_scr = sbuf("r_scr", [128, 3 * 16], F32)
    m_scr = sbuf("m_scr", [128, 16], F32)
    ident_sb = sbuf("ident_sb", [128, 128], F32)
    ident_bf = sbuf("ident_bf", [128, 128], BF16)
    ones_col_bf = sbuf("ones_col_bf", [128, 1], BF16)
    ones_row_f = sbuf("ones_row_f", [1, 128], F32)
    neg_c = sbuf("neg_c", [128, 1], F32)
    cchain = sbuf("cchain", [128, 8], F32)
    cmax = sbuf("cmax", [1, 1], F32)
    rs_sb = sbuf("rs_sb", [1, 512], F32)
    rsbc_sb = sbuf("rsbc_sb", [128, 512], BF16)

    pb = [es.enter_context(nc.psum_tensor(f"pb{i}", [128, 512], F32)) for i in range(8)]

    sem = lambda n: Ctr(es.enter_context(nc.semaphore(n)))
    s_mm = sem("s_mm")
    s_v = sem("s_v")
    s_s = sem("s_s")
    s_g = sem("s_g")
    s_stpY = sem("s_stpY")
    s_stpG = sem("s_stpG")
    s_tp = [sem("s_tpA"), sem("s_tpB")]
    s_wi = [sem("s_wi0"), sem("s_wi1")]
    s_wo = [sem("s_wo0"), sem("s_wo1")]
    s_wqk = [sem("s_wqk0"), sem("s_wqk1")]
    s_wv = sem("s_wv")
    s_wout = [sem("s_wout0"), sem("s_wout1")]
    s_mask = sem("s_mask")
    s_st = sem("s_st")

    ev = {k: [] for k in ("T", "V", "S", "G", "Y")}

    def emit(which):
        def deco(f):
            ev[which].append(f)
        return deco

    def drain():
        with nc.Block() as block:
            @block.tensor
            def _(eng):
                for f in ev["T"]:
                    f(eng)

            @block.vector
            def _(eng):
                for f in ev["V"]:
                    f(eng)

            @block.scalar
            def _(eng):
                for f in ev["S"]:
                    f(eng)

            @block.gpsimd
            def _(eng):
                for f in ev["G"]:
                    f(eng)

            @block.sync
            def _(eng):
                for f in ev["Y"]:
                    f(eng)
        for k in ev:
            ev[k].clear()

    bank_free = [None] * 8
    bank_fill = [0] * 8

    def bw(b):
        return bank_free[b]

    def wait_pairs(eng, pairs):
        for p in pairs:
            if p is not None:
                eng.wait_ge(p[0].h, p[1])

    def guard_now():
        return [(s_mm, s_mm.n), (s_v, s_v.n), (s_s, s_s.n)]

    @emit("Y")
    def _(eng):
        eng.dma_start(out=X[:].rearrange("p (t d) -> p t d", t=NT_OWN),
                      in_=dr_x_own.rearrange("(t p) d -> p t d", p=128)).then_inc(s_stpY.h, 16)
        eng.dma_start(out=ident_sb[:], in_=dr_ident).then_inc(s_stpY.h, 16)
    s_stpY.n += 32

    @emit("G")
    def _(eng):
        eng.memset(ones_col_bf[:], 1.0)
        eng.memset(ones_row_f[:], 1.0).then_inc(s_g.h, 1)
        eng.dma_start(out=X_halo[:].rearrange("p (t d) -> p t d", t=4),
                      in_=dr_x_halo.rearrange("(t p) d -> p t d", p=128)).then_inc(s_stpG.h, 16)
        eng.dma_start(out=wg_sb[:].rearrange("p (k j) -> p k j", k=16),
                      in_=dr_wg).then_inc(s_stpG.h, 16)
        eng.dma_start(out=ident_bf[:], in_=dr_ident).then_inc(s_stpG.h, 16)
    s_g.n += 1
    s_stpG.n += 48
    setupY_val = s_stpY.n
    setupG_val = s_stpG.n

    xnt_uses = [0, 0]
    last_xnt_reader = [0]

    def x_tile(tt):
        if tt < 4:
            return X_halo[:, tt * D:(tt + 1) * D]
        return X[:, (tt - 4) * D:(tt - 3) * D]

    def emit_ln_tile_stats(call, i, tt, dep, td, st):
        @emit("V")
        def _(eng, i=i, tt=tt, dep=list(dep), call=call, td=td):
            if i == 0 and call == 0:
                eng.wait_ge(s_stpY.h, setupY_val)
                eng.wait_ge(s_stpG.h, setupG_val)
            if td is not None:
                eng.wait_ge(s_v.h, td)
            elif i == 0:
                wait_pairs(eng, dep)
            eng.tensor_reduce(stats[:, i:i + 1], x_tile(tt), AXX,
                              ALU.add).then_inc(s_v.h, 1)
        s_v.n += 1
        st["red"][i] = s_v.n

        @emit("S")
        def _(eng, i=i, tt=tt, dep=list(dep), call=call, td=td,
              w=st["sq"].get(i - 2, 0), w_tp=16 * xnt_uses[i % 2]):
            if i == 0:
                if call == 0:
                    eng.wait_ge(s_stpY.h, setupY_val)
                    eng.wait_ge(s_stpG.h, setupG_val)
                wait_pairs(eng, dep)
            if w_tp > 0:
                eng.wait_ge(s_tp[i % 2].h, w_tp)
            if td is not None:
                eng.wait_ge(s_v.h, td)
            if w > 0:
                eng.wait_ge(s_s.h, w)
            eng.activation(xn_tmp[i % 2][:], x_tile(tt), AF.Square,
                           accum_out=stats[:, 12 + i:13 + i]).then_inc(s_s.h, 1)
        s_s.n += 1
        st["sq"][i] = s_s.n

    def emit_ln_batch(call, st, h0, h1):
        vw = max(st["red"][j] for j in range(h0, h1))
        sw = max(st["sq"][j] for j in range(h0, h1))

        @emit("V")
        def _(eng, h0=h0, h1=h1, vw=vw):
            eng.wait_ge(s_v.h, vw)
            eng.tensor_scalar(stats[:, 24 + h0:24 + h1], stats[:, h0:h1], 1.0 / D,
                              None, ALU.mult).then_inc(s_v.h, 1)
        s_v.n += 1
        v_mu = s_v.n

        @emit("V")
        def _(eng, h0=h0, h1=h1, v_mu=v_mu):
            eng.wait_ge(s_v.h, v_mu)
            eng.tensor_tensor(stats[:, 48 + h0:48 + h1], stats[:, 24 + h0:24 + h1],
                              stats[:, 24 + h0:24 + h1], ALU.mult).then_inc(s_v.h, 1)
        s_v.n += 1
        v_musq = s_v.n

        @emit("V")
        def _(eng, h0=h0, h1=h1, v_musq=v_musq, sw=sw):
            eng.wait_ge(s_v.h, v_musq)
            eng.wait_ge(s_s.h, sw)
            eng.scalar_tensor_tensor(stats[:, 48 + h0:48 + h1],
                                     stats[:, 12 + h0:12 + h1], 1.0 / D,
                                     stats[:, 48 + h0:48 + h1], ALU.mult,
                                     ALU.subtract).then_inc(s_v.h, 1)
        s_v.n += 1
        v_var = s_v.n

        @emit("V")
        def _(eng, h0=h0, h1=h1, v_var=v_var):
            eng.wait_ge(s_v.h, v_var)
            eng.tensor_scalar(stats[:, 48 + h0:48 + h1], stats[:, 48 + h0:48 + h1],
                              EPS, None, ALU.add).then_inc(s_v.h, 1)
        s_v.n += 1
        v_vpe = s_v.n

        @emit("S")
        def _(eng, h0=h0, h1=h1, v_vpe=v_vpe):
            eng.wait_ge(s_v.h, v_vpe)
            eng.activation(stats[:, 36 + h0:36 + h1], stats[:, 48 + h0:48 + h1],
                           AF.Sqrt).then_inc(s_s.h, 1)
        s_s.n += 1
        s_sqrt = s_s.n

        @emit("V")
        def _(eng, h0=h0, h1=h1, s_sqrt=s_sqrt):
            eng.wait_ge(s_s.h, s_sqrt)
            eng.reciprocal(stats[:, 36 + h0:36 + h1],
                           stats[:, 36 + h0:36 + h1]).then_inc(s_v.h, 1)
        s_v.n += 1
        rstd_v = s_v.n
        for j in range(h0, h1):
            _emit_ln_norm(call, j, st["tts"][j], st, rstd_v)

    def _emit_ln_norm(call, i, tt, st, rstd_v):
        slot = i % 2
        tp_wait = 16 * xnt_uses[slot]
        xnt_uses[slot] += 1

        @emit("V")
        def _(eng, i=i, tt=tt, slot=slot, tp_wait=tp_wait, rstd_v=rstd_v):
            eng.wait_ge(s_v.h, rstd_v)
            if tp_wait:
                eng.wait_ge(s_tp[slot].h, tp_wait)
            eng.tensor_scalar(xn_tmp[slot][:], x_tile(tt), stats[:, 24 + i:25 + i],
                              stats[:, 36 + i:37 + i], ALU.subtract,
                              ALU.mult).then_inc(s_v.h, 1)
        s_v.n += 1
        v_norm = s_v.n

        @emit("Y" if slot == 0 else "S")
        def _(eng, i=i, tt=tt, slot=slot, v_norm=v_norm, xg=st["xg"]):
            eng.wait_ge(s_v.h, v_norm)
            if i <= 1:
                eng.wait_ge(s_mm.h, xg)
            eng.dma_start(
                out=XNT[:].rearrange("p (k j) -> p k j", k=16)[:, :, tt * 128:(tt + 1) * 128],
                in_=xn_tmp[slot][:].rearrange("p (k j) -> p k j", k=16),
                transpose=True).then_inc(s_tp[slot].h, 16)
        s_tp[slot].n += 16
        st["thr"][tt] = (slot, s_tp[slot].n)

    def _new_ln_state(tts, tp_prev):
        return {"sq": {}, "red": {}, "thr": {}, "done": [],
                "tts": list(tts), "tp_prev": tp_prev, "xg": last_xnt_reader[0]}

    def make_ln_hook(call, tts, dep, tp_prev):
        st = _new_ln_state(tts, tp_prev)
        n = len(tts)

        q = n // 4

        def hook(tt, td):
            if not st["done"]:
                st["xg"] = last_xnt_reader[0]
            i = tts.index(tt)
            emit_ln_tile_stats(call, i, tt, dep, td, st)
            st["done"].append(tt)
            if (i + 1) % q == 0:
                emit_ln_batch(call, st, i + 1 - q, i + 1)
        return hook, st

    def emit_ln(call, tts, dep, dep_tile=None, hook_state=None, tp_prev=None):
        n = len(tts)
        if hook_state is None:
            st = _new_ln_state(tts, tp_prev)
            st["xg"] = last_xnt_reader[0]
            q = n // 4
            for i, tt in enumerate(tts):
                td = dep_tile.get(tt) if dep_tile else None
                emit_ln_tile_stats(call, i, tt, dep, td, st)
                if (i + 1) % q == 0:
                    emit_ln_batch(call, st, i + 1 - q, i + 1)
        else:
            st = hook_state
            assert len(st["done"]) == len(tts), st["done"]
        return (s_tp[0].n, s_tp[1].n), st["thr"]

    def emit_routing(call, tts, xnt_ready, tile_thr):
        n = len(tts)
        for i, tt in enumerate(tts):
            b = 6 + (i % 2)

            @emit("T")
            def _(eng, i=i, tt=tt, b=b, bfw=bw(b), call=call, thr=tile_thr[tt]):
                eng.wait_ge(s_tp[thr[0]].h, thr[1])
                if i == 0 and call == 0:
                    eng.wait_ge(s_stpG.h, setupG_val)
                wait_pairs(eng, [bfw])
                mm = None
                for kt in range(16):
                    mm = eng.matmul(pb[b][:, 0:8],
                                    XNT[:, kt * TOKW + tt * 128:kt * TOKW + tt * 128 + 128],
                                    wg_sb[:, kt * 8:(kt + 1) * 8],
                                    start=(kt == 0), stop=(kt == 15),
                                    skip_group_check=True)
                mm.then_inc(s_mm.h, 1)
            s_mm.n += 1
            bank_fill[b] = s_mm.n

            @emit("S")
            def _(eng, tt=tt, b=b, fill=bank_fill[b]):
                eng.wait_ge(s_mm.h, fill)
                eng.copy(lg_all[:, tt * 8:(tt + 1) * 8], pb[b][:, 0:8]).then_inc(s_s.h, 1)
            s_s.n += 1
            bank_free[b] = (s_s, s_s.n)
            s_lg = s_s.n

            @emit("V")
            def _(eng, tt=tt, s_lg=s_lg):
                eng.wait_ge(s_s.h, s_lg)
                eng.max(top_all[:, tt * 8:(tt + 1) * 8],
                        lg_all[:, tt * 8:(tt + 1) * 8]).then_inc(s_v.h, 1)
            s_v.n += 1
            v_max = s_v.n

            @emit("V")
            def _(eng, i=i, tt=tt, v_max=v_max):
                eng.wait_ge(s_v.h, v_max)
                eng.tensor_tensor(r_scr[:, i:i + 1], top_all[:, tt * 8:tt * 8 + 1],
                                  top_all[:, tt * 8 + 1:tt * 8 + 2],
                                  ALU.subtract).then_inc(s_v.h, 1)
            s_v.n += 1
        v_d12 = s_v.n

        @emit("S")
        def _(eng, v_d12=v_d12, n=n):
            eng.wait_ge(s_v.h, v_d12)
            eng.activation(r_scr[:, 16:16 + n], r_scr[:, 0:n],
                           AF.Sigmoid).then_inc(s_s.h, 1)
        s_s.n += 1
        s_sig = s_s.n

        @emit("V")
        def _(eng, s_sig=s_sig, n=n):
            eng.wait_ge(s_s.h, s_sig)
            eng.tensor_scalar(r_scr[:, 16:16 + n], r_scr[:, 16:16 + n], 0.5, None,
                              ALU.mult).then_inc(s_v.h, 1)
        s_v.n += 1
        v_w1 = s_v.n

        @emit("V")
        def _(eng, v_w1=v_w1, n=n):
            eng.wait_ge(s_v.h, v_w1)
            eng.tensor_scalar(r_scr[:, 32:32 + n], r_scr[:, 16:16 + n], -1.0, 0.5,
                              ALU.mult, ALU.add).then_inc(s_v.h, 1)
        s_v.n += 1
        prev = s_v.n

        for i, tt in enumerate(tts):
            @emit("V")
            def _(eng, i=i, tt=tt, pw=prev):
                eng.wait_ge(s_v.h, pw)
                sl = slice(tt * 8, (tt + 1) * 8)
                eng.tensor_scalar(m_scr[:, 0:8], lg_all[:, sl],
                                  top_all[:, tt * 8:tt * 8 + 1], None, ALU.is_equal)
                eng.tensor_scalar(m_scr[:, 8:16], lg_all[:, sl],
                                  top_all[:, tt * 8 + 1:tt * 8 + 2], None,
                                  ALU.is_equal).then_inc(s_v.h, 1)
            s_v.n += 1
            v_m = s_v.n

            @emit("V")
            def _(eng, i=i, v_m=v_m):
                eng.wait_ge(s_v.h, v_m)
                eng.tensor_scalar(m_scr[:, 8:16], m_scr[:, 8:16],
                                  r_scr[:, 32 + i:33 + i], None,
                                  ALU.mult).then_inc(s_v.h, 1)
            s_v.n += 1
            v_m2 = s_v.n

            @emit("V")
            def _(eng, i=i, tt=tt, v_m2=v_m2):
                eng.wait_ge(s_v.h, v_m2)
                eng.scalar_tensor_tensor(what_sb[:, tt * 8:(tt + 1) * 8],
                                         m_scr[:, 0:8], r_scr[:, 16 + i:17 + i],
                                         m_scr[:, 8:16], ALU.mult,
                                         ALU.add).then_inc(s_v.h, 1)
            s_v.n += 1
            prev = s_v.n
        return s_v.n

    def emit_ffn(call, tts, groups, v_routing, xnt_ready, region_guard, pre_drain=None, tile_hook=None):
        ng = len(groups)
        with contextlib.ExitStack() as fes:
            fsb = lambda nme, s, d: fes.enter_context(
                nc.sbuf_tensor(f"{nme}_c{call}", s, d))
            wi_sb = [fsb(f"wi_sb{i}", [128, 2 * 16 * 256], BF16) for i in range(2)]
            wo_sb = fsb("wo_sb", [128, 4 * 2048], BF16)
            inter = fsb("inter", [128, 4 * TOKW], BF16)

            rg = dict(region_guard) if isinstance(region_guard, dict) else {
                "wi0": region_guard, "wi1": region_guard, "wo": region_guard,
                "inter": region_guard}
            wi_state = {"n": 0, "free": {-2: rg["wi0"], -1: rg["wi1"]}}
            wo_state = {"n": 0, "free": {-1: rg["wo"]}}

            def load_wi(e, q):
                nl = wi_state["n"]
                slot = nl % 2
                gw = wi_state["free"].get(nl - 2)

                @emit("G")
                def _(eng, e=e, q=q, slot=slot, gw=gw):
                    if gw is not None:
                        wait_pairs(eng, gw if isinstance(gw, list) else [gw])
                    eng.dma_start(
                        out=wi_sb[slot][:, 0:4096].rearrange("p (k j) -> p k j", k=16),
                        in_=dr_wi0[e, q]).then_inc(s_wi[slot].h, 16)
                    eng.dma_start(
                        out=wi_sb[slot][:, 4096:8192].rearrange("p (k j) -> p k j", k=16),
                        in_=dr_wi1[e, q]).then_inc(s_wi[slot].h, 16)
                s_wi[slot].n += 32
                wi_state["n"] += 1
                return slot, s_wi[slot].n, nl

            def load_wo(e, ha):
                nl = wo_state["n"]
                gw = wo_state["free"].get(nl - 1)

                @emit("G")
                def _(eng, e=e, ha=ha, gw=gw):
                    if gw is not None:
                        wait_pairs(eng, gw if isinstance(gw, list) else [gw])
                    eng.dma_start(out=wo_sb[:].rearrange("p (k j) -> p k j", k=4),
                                  in_=dr_wo[e, ha]).then_inc(s_wo[0].h, 16)
                s_wo[0].n += 16
                wo_state["n"] += 1
                return s_wo[0].n, nl

            inter_guard = [rg["inter"]]
            wo_bank_ctr = [0]
            tile_evac = {}
            h_done_guard = [None]

            def emit_wo(e, ha, inter_ready):
                wo_val, wo_idx = load_wo(e, ha)
                for j, tt in enumerate(tts):
                    record_evac = (e == E - 1 and ha == 1)
                    bs = (0, 1, 2, 3) if (wo_bank_ctr[0] % 2 == 0) else (4, 5, 6, 7)
                    wo_bank_ctr[0] += 1
                    bfws = [bw(b) for b in bs]

                    @emit("T")
                    def _(eng, tt=tt, bs=bs, j=j, wo_val=wo_val,
                          inter_ready=inter_ready, bfws=bfws):
                        if j == 0:
                            eng.wait_ge(s_wo[0].h, wo_val)
                            eng.wait_ge(s_v.h, inter_ready)
                        wait_pairs(eng, bfws)
                        mm = None
                        for kt in range(4):
                            for dg in range(4):
                                mm = eng.matmul(
                                    pb[bs[dg]][:],
                                    inter[:, kt * TOKW + tt * 128:kt * TOKW + tt * 128 + 128],
                                    wo_sb[:, kt * 2048 + dg * 512:kt * 2048 + dg * 512 + 512],
                                    start=(kt == 0), stop=(kt == 3),
                                    skip_group_check=True)
                        mm.then_inc(s_mm.h, 1)
                    s_mm.n += 1
                    for b in bs:
                        bank_fill[b] = s_mm.n

                    @emit("V")
                    def _(eng, e=e, tt=tt, bs=bs, fill=s_mm.n):
                        eng.wait_ge(s_mm.h, fill)
                        xa = x_tile(tt)
                        last = None
                        for dg in range(4):
                            xs = xa[:, dg * 512:(dg + 1) * 512]
                            last = eng.scalar_tensor_tensor(
                                xs, pb[bs[dg]][:], what_sb[:, tt * 8 + e:tt * 8 + e + 1],
                                xs, ALU.mult, ALU.add)
                        last.then_inc(s_v.h, 1)
                    s_v.n += 1
                    for b in bs:
                        bank_free[b] = (s_v, s_v.n)
                    if record_evac:
                        tile_evac[tt] = s_v.n
                        if tile_hook is not None:
                            tile_hook(tt, s_v.n)
                wo_state["free"][wo_idx] = (s_mm, s_mm.n)
                inter_guard[0] = (s_mm, s_mm.n)

            wi_cur = None
            for e in range(E):
                for mt in range(8):
                    q, idx = mt // 2, mt % 2
                    if idx == 0:
                        wi_cur = load_wi(e, q)
                    wi_slot, wi_val, wi_idx = wi_cur
                    fills = {}
                    for w in (0, 1):
                        banks = (0, 1, 2) if w == 0 else (3, 4, 5)
                        bfws = [bw(banks[gi]) for gi in range(ng)]

                        @emit("T")
                        def _(eng, e=e, mt=mt, w=w, banks=banks, idx=idx,
                              wi_slot=wi_slot, wi_val=wi_val, bfws=bfws,
                              groups=list(groups), xnt_ready=xnt_ready):
                            if idx == 0 and w == 0:
                                eng.wait_ge(s_wi[wi_slot].h, wi_val)
                            if e == 0 and mt == 0 and w == 0:
                                eng.wait_ge(s_tp[0].h, xnt_ready[0])
                                eng.wait_ge(s_tp[1].h, xnt_ready[1])
                            wait_pairs(eng, bfws)
                            woff = w * 4096
                            mm = None
                            for kt in range(16):
                                for gi, g0 in enumerate(groups):
                                    mm = eng.matmul(
                                        pb[banks[gi]][:],
                                        wi_sb[wi_slot][:, woff + kt * 256 + idx * 128:
                                                       woff + kt * 256 + idx * 128 + 128],
                                        XNT[:, kt * TOKW + g0:kt * TOKW + g0 + 512],
                                        start=(kt == 0), stop=(kt == 15),
                                        skip_group_check=True)
                            mm.then_inc(s_mm.h, 1)
                        s_mm.n += 1
                        for gi in range(ng):
                            bank_fill[banks[gi]] = s_mm.n
                        fills[w] = s_mm.n
                    if idx == 1:
                        wi_state["free"][wi_idx] = (s_mm, s_mm.n)

                    ig = inter_guard[0] if mt % 4 == 0 else None

                    @emit("S")
                    def _(eng, mt=mt, h0f=fills[0], ng=ng, groups=list(groups), ig=ig):
                        eng.wait_ge(s_mm.h, h0f)
                        if ig is not None:
                            wait_pairs(eng, ig if isinstance(ig, list) else [ig])
                        last = None
                        for gi, g0 in enumerate(groups):
                            last = eng.activation(
                                inter[:, (mt % 4) * TOKW + g0:(mt % 4) * TOKW + g0 + 512],
                                pb[gi][:], AF.Silu)
                        last.then_inc(s_s.h, 1)
                    s_s.n += 1
                    for b in (0, 1, 2)[:ng]:
                        bank_free[b] = (s_s, s_s.n)
                    s_silu = s_s.n

                    @emit("V")
                    def _(eng, mt=mt, s_silu=s_silu, h1f=fills[1], ng=ng,
                          groups=list(groups)):
                        eng.wait_ge(s_s.h, s_silu)
                        eng.wait_ge(s_mm.h, h1f)
                        last = None
                        for gi, g0 in enumerate(groups):
                            sl = slice((mt % 4) * TOKW + g0, (mt % 4) * TOKW + g0 + 512)
                            last = eng.tensor_tensor(inter[:, sl], inter[:, sl],
                                                     pb[(3, 4, 5)[gi]][:], ALU.mult)
                        last.then_inc(s_v.h, 1)
                    s_v.n += 1
                    for b in (3, 4, 5)[:ng]:
                        bank_free[b] = (s_v, s_v.n)
                    if e == E - 1 and mt == 7:
                        h_done_guard[0] = (s_mm, s_mm.n)
                    if mt % 4 == 3:
                        emit_wo(e, mt // 4, s_v.n)

            g = guard_now()
            if pre_drain is not None:
                pre_drain()
            return g, tile_evac, h_done_guard[0]

    def emit_attention(xnt_ready, tile_thr, region_guard, early_guard,
                       tile_hook=None):
        with contextlib.ExitStack() as aes:
            asb = lambda nme, s, d: aes.enter_context(nc.sbuf_tensor(nme, s, d))
            wv_sb = asb("wv_sb", [128, 16 * 256], BF16)
            wqk_sb = [asb(f"wqk_sb{i}", [128, 2 * 16 * 128], BF16) for i in range(2)]
            wout_sb = [asb(f"wout_sb{i}", [128, 4 * 512], BF16) for i in range(2)]
            av_sb = asb("av_sb", [128, 4 * 1024], BF16)
            E_sb = [asb(f"E_sb{i}", [128, 512], BF16) for i in range(2)]
            v4 = asb("v4", [128, NT3 * 256], BF16)
            q_sb = asb("q_sb", [128, 1024], BF16)
            k_sb = asb("k_sb", [128, TOKW], BF16)
            masks = asb("masks", [128, 8 * 512], BF16)
            mpre = asb("mpre_sb", [128, 4 * 512], BF16)

            masks_state = [False]

            def emit_masks_load():
                @emit("G")
                def _(eng, rg=list(region_guard)):
                    wait_pairs(eng, rg)
                    eng.dma_start(out=masks[:].rearrange("p (j i) -> p j i", j=8),
                                  in_=dr_band).then_inc(s_mask.h, 16)
                    eng.dma_start(out=mpre[:].rearrange("p (j i) -> p j i", j=4),
                                  in_=dr_mpre).then_inc(s_mask.h, 16)
                s_mask.n += 32
                masks_state[0] = True
            mask_val = s_mask.n + 32

            wqk_state = {"n": 0, "free": {-2: [early_guard], -1: [early_guard]}}
            wv_state = {"n": 0, "free": {-1: [early_guard]}}
            wout_state = {"n": 0, "free": {-2: [early_guard], -1: [early_guard]}}

            def load_wqk(h):
                nl = wqk_state["n"]
                slot = nl % 2
                gw = wqk_state["free"].get(nl - 2)

                @emit("G")
                def _(eng, h=h, slot=slot, gw=gw):
                    if gw is not None:
                        wait_pairs(eng, gw if isinstance(gw, list) else [gw])
                    eng.dma_start(
                        out=wqk_sb[slot][:, 0:2048].rearrange("p (k j) -> p k j", k=16),
                        in_=dr_wqk[h]).then_inc(s_wqk[slot].h, 16)
                    eng.dma_start(
                        out=wqk_sb[slot][:, 2048:4096].rearrange("p (k j) -> p k j", k=16),
                        in_=dr_wqk[16 + h]).then_inc(s_wqk[slot].h, 16)
                s_wqk[slot].n += 32
                wqk_state["n"] += 1
                return slot, s_wqk[slot].n, nl

            def load_wv(vg):
                nl = wv_state["n"]
                gw = wv_state["free"].get(nl - 1)

                @emit("G")
                def _(eng, vg=vg, gw=gw):
                    if gw is not None:
                        wait_pairs(eng, gw if isinstance(gw, list) else [gw])
                    eng.dma_start(out=wv_sb[:].rearrange("p (k j) -> p k j", k=16),
                                  in_=dr_wv[vg]).then_inc(s_wv.h, 16)
                s_wv.n += 16
                wv_state["n"] += 1
                return s_wv.n, nl

            def load_wout(qset, dg):
                nl = wout_state["n"]
                slot = nl % 2
                gw = wout_state["free"].get(nl - 2)

                @emit("G")
                def _(eng, qset=qset, dg=dg, slot=slot, gw=gw):
                    if gw is not None:
                        wait_pairs(eng, gw if isinstance(gw, list) else [gw])
                    eng.dma_start(out=wout_sb[slot][:].rearrange("p (k j) -> p k j", k=4),
                                  in_=dr_wout[qset, dg]).then_inc(s_wout[slot].h, 16)
                s_wout[slot].n += 16
                wout_state["n"] += 1
                return slot, s_wout[slot].n, nl

            negc_done = [None]
            att_tile_evac = {}
            pending_tail = []
            att_qk_done = [None]
            v4_guard = [region_guard]
            qk_guard = [region_guard]
            av_guard = [region_guard]
            rsbc_guard = [None]
            qkb_ctr = [0]

            def emit_scores(jj, jtile, b, ch, qk_ready, first):
                bfw_ = bw(b)

                @emit("T")
                def _(eng, jj=jj, jtile=jtile, b=b, ch=ch, bfw_=bfw_,
                      qk_ready=qk_ready, first=first, mv=mask_val):
                    if jj == 0:
                        eng.wait_ge(s_v.h, qk_ready[0])
                        eng.wait_ge(s_s.h, qk_ready[1])
                        if first:
                            eng.wait_ge(s_mask.h, mv)
                            eng.wait_ge(s_stpG.h, setupG_val)
                    wait_pairs(eng, [bfw_])
                    eng.matmul(pb[b][:], k_sb[:, jtile * 128:(jtile + 1) * 128],
                               q_sb[:, ch * 512:(ch + 1) * 512],
                               start=True, stop=False, skip_group_check=True)
                    if ch == 1 and jj < 4:
                        msk = mpre[:, jj * 512:(jj + 1) * 512]
                    else:
                        msk = masks[:, jj * 512:(jj + 1) * 512]
                    eng.matmul(pb[b][:], ident_bf[:], msk, start=False, stop=True,
                               skip_group_check=True).then_inc(s_mm.h, 1)
                s_mm.n += 1
                bank_fill[b] = s_mm.n
                return s_mm.n

            def emit_exp_avsum(jj, jtile, b, hh, e_fill, av_bank, sum_bank, sc_fill):
                eslot = jj % 2
                e_prev = e_fill.get(jj - 2)
                ndv = negc_done[0]

                @emit("S")
                def _(eng, jj=jj, b=b, eslot=eslot, sc_fill=sc_fill, e_prev=e_prev,
                      ndv=ndv):
                    eng.wait_ge(s_mm.h, sc_fill)
                    eng.wait_ge(s_v.h, ndv)
                    if e_prev is not None:
                        eng.wait_ge(s_mm.h, e_prev)
                    eng.activation(E_sb[eslot][:], pb[b][:], AF.Exp,
                                   bias=neg_c[:, 0:1]).then_inc(s_s.h, 1)
                s_s.n += 1
                bank_free[b] = (s_s, s_s.n)
                s_exp = s_s.n

                abw = bw(av_bank) if jj == 0 else None
                sbw = bw(sum_bank) if jj == 0 else None

                @emit("T")
                def _(eng, hh=hh, jj=jj, jtile=jtile, eslot=eslot, s_exp=s_exp,
                      abw=abw, sbw=sbw, av_bank=av_bank, sum_bank=sum_bank):
                    eng.wait_ge(s_s.h, s_exp)
                    if jj == 0:
                        wait_pairs(eng, [abw, sbw])
                    eng.matmul(pb[av_bank][:],
                               v4[:, jtile * 256 + hh * 128:jtile * 256 + hh * 128 + 128],
                               E_sb[eslot][:], start=(jj == 0), stop=(jj == 7),
                               skip_group_check=True)
                    eng.matmul(pb[sum_bank][0:1, :], ones_col_bf[:], E_sb[eslot][:],
                               start=(jj == 0), stop=(jj == 7),
                               skip_group_check=True).then_inc(s_mm.h, 1)
                s_mm.n += 1
                e_fill[jj] = s_mm.n
                if jj == 7:
                    bank_fill[av_bank] = s_mm.n
                    bank_fill[sum_bank] = s_mm.n

            for qset in range(4):
                av_writes = []
                for vgi in range(2):
                    vg = qset * 2 + vgi
                    wv_val, wv_idx = load_wv(vg)
                    v4g = v4_guard[0]
                    for tt in range(NT3):
                        b = tt % 2

                        @emit("T")
                        def _(eng, vg=vg, tt=tt, b=b, wv_val=wv_val, bfw=bw(b),
                              thr=tile_thr[tt]):
                            if tt == 0:
                                eng.wait_ge(s_wv.h, wv_val)
                            if vg == 0:
                                eng.wait_ge(s_tp[thr[0]].h, thr[1])
                            wait_pairs(eng, [bfw])
                            mm = None
                            for kt in range(16):
                                mm = eng.matmul(
                                    pb[b][:, 0:256],
                                    XNT[:, kt * TOKW + tt * 128:kt * TOKW + tt * 128 + 128],
                                    wv_sb[:, kt * 256:(kt + 1) * 256],
                                    start=(kt == 0), stop=(kt == 15),
                                    skip_group_check=True)
                            mm.then_inc(s_mm.h, 1)
                        s_mm.n += 1
                        bank_fill[b] = s_mm.n
                        last_xnt_reader[0] = s_mm.n

                        @emit("V")
                        def _(eng, tt=tt, b=b, fill=bank_fill[b], v4g=v4g):
                            eng.wait_ge(s_mm.h, fill)
                            if tt == 0:
                                wait_pairs(eng, v4g if isinstance(v4g, list) else [v4g])
                            eng.tensor_copy(v4[:, tt * 256:(tt + 1) * 256],
                                            pb[b][:, 0:256]).then_inc(s_v.h, 1)
                        s_v.n += 1
                        bank_free[b] = (s_v, s_v.n)
                    wv_state["free"][wv_idx] = (s_mm, s_mm.n)

                    for hh2 in range(2):
                        h = vg * 2 + hh2
                        wqk_slot, wqk_val, wqk_idx = load_wqk(h)
                        if not masks_state[0]:
                            emit_masks_load()
                        qkg = qk_guard[0]
                        for which, ngr, base in ((0, 2, 512), (1, 3, 0)):
                            for g in range(ngr):
                                b = 2 + (qkb_ctr[0] % 2)
                                qkb_ctr[0] += 1

                                tls = [tile_thr[t] for t in
                                       range((base + g * 512) // 128,
                                             (base + g * 512) // 128 + 4)]
                                gthr = [(sl, max(v for s2, v in tls if s2 == sl))
                                        for sl in {s2 for s2, _ in tls}]

                                @emit("T")
                                def _(eng, which=which, g=g, b=b, base=base,
                                      wqk_slot=wqk_slot, wqk_val=wqk_val, bfw=bw(b),
                                      gthr=gthr, h=h):
                                    if which == 0 and g == 0:
                                        eng.wait_ge(s_wqk[wqk_slot].h, wqk_val)
                                    if h == 0:
                                        for sl, v in gthr:
                                            eng.wait_ge(s_tp[sl].h, v)
                                    wait_pairs(eng, [bfw])
                                    woff = which * 2048
                                    mm = None
                                    for kt in range(16):
                                        mm = eng.matmul(
                                            pb[b][:],
                                            wqk_sb[wqk_slot][:, woff + kt * 128:
                                                             woff + kt * 128 + 128],
                                            XNT[:, kt * TOKW + base + g * 512:
                                                kt * TOKW + base + g * 512 + 512],
                                            start=(kt == 0), stop=(kt == 15),
                                            skip_group_check=True)
                                    mm.then_inc(s_mm.h, 1)
                                s_mm.n += 1
                                bank_fill[b] = s_mm.n
                                last_xnt_reader[0] = s_mm.n

                                if which == 0:
                                    @emit("V")
                                    def _(eng, g=g, b=b, fill=bank_fill[b], qkg=qkg):
                                        eng.wait_ge(s_mm.h, fill)
                                        if g == 0:
                                            wait_pairs(eng, qkg if isinstance(qkg, list) else [qkg])
                                        eng.tensor_scalar(
                                            q_sb[:, g * 512:(g + 1) * 512], pb[b][:],
                                            QSCALE, None, ALU.mult).then_inc(s_v.h, 1)
                                    s_v.n += 1
                                    bank_free[b] = (s_v, s_v.n)
                                else:
                                    @emit("S")
                                    def _(eng, g=g, b=b, fill=bank_fill[b], qkg=qkg):
                                        eng.wait_ge(s_mm.h, fill)
                                        if g == 0:
                                            wait_pairs(eng, qkg if isinstance(qkg, list) else [qkg])
                                        eng.copy(k_sb[:, g * 512:(g + 1) * 512],
                                                 pb[b][:]).then_inc(s_s.h, 1)
                                    s_s.n += 1
                                    bank_free[b] = (s_s, s_s.n)
                        qk_ready = (s_v.n, s_s.n)
                        wqk_state["free"][wqk_idx] = (s_mm, s_mm.n)

                        for ch in range(2):
                            jt_tiles = list(range(0, 8)) if ch == 0 else list(range(4, 12))
                            first = (h == 0 and ch == 0)
                            av_bank, sum_bank = 4 + ((h * 2 + ch) % 2), 6
                            e_fill = {}

                            if first:
                                first_banks = [2, 3, 7, 1]
                                vmasks = []
                                for jj in range(4):
                                    b = first_banks[jj]
                                    vmasks.append(emit_scores(jj, jt_tiles[jj], b, ch,
                                                              qk_ready, first))

                                    @emit("V")
                                    def _(eng, jj=jj, b=b, vm=vmasks[-1]):
                                        eng.wait_ge(s_mm.h, vm)
                                        eng.reduce_max(cchain[:, jj:jj + 1], pb[b][:],
                                                       AXX).then_inc(s_v.h, 1)
                                    s_v.n += 1
                                v_rm = s_v.n

                                @emit("V")
                                def _(eng, v_rm=v_rm):
                                    eng.wait_ge(s_v.h, v_rm)
                                    eng.reduce_max(cchain[:, 4:5], cchain[:, 0:4],
                                                   AXX).then_inc(s_v.h, 1)
                                s_v.n += 1
                                v_m8 = s_v.n

                                @emit("T")
                                def _(eng, v_m8=v_m8, bfw=bw(0)):
                                    eng.wait_ge(s_v.h, v_m8)
                                    wait_pairs(eng, [bfw])
                                    eng.transpose(pb[0][0:1, 0:128], cchain[:, 4:5],
                                                  ident_sb[:]).then_inc(s_mm.h, 1)
                                s_mm.n += 1
                                bank_fill[0] = s_mm.n
                                f0 = s_mm.n

                                @emit("V")
                                def _(eng, f0=f0):
                                    eng.wait_ge(s_mm.h, f0)
                                    eng.reduce_max(cmax[:], pb[0][0:1, 0:128],
                                                   AXX).then_inc(s_v.h, 1)
                                s_v.n += 1
                                bank_free[0] = (s_v, s_v.n)
                                v_cm = s_v.n

                                @emit("T")
                                def _(eng, v_cm=v_cm, bfw=bw(0)):
                                    eng.wait_ge(s_v.h, v_cm)
                                    wait_pairs(eng, [bfw])
                                    eng.matmul(pb[0][:, 0:1], ones_row_f[:], cmax[:],
                                               start=True, stop=True,
                                               skip_group_check=True).then_inc(s_mm.h, 1)
                                s_mm.n += 1
                                bank_fill[0] = s_mm.n
                                f0b = s_mm.n

                                @emit("V")
                                def _(eng, f0b=f0b):
                                    eng.wait_ge(s_mm.h, f0b)
                                    eng.tensor_scalar(neg_c[:], pb[0][:, 0:1], -1.0,
                                                      None, ALU.mult).then_inc(s_v.h, 1)
                                s_v.n += 1
                                bank_free[0] = (s_v, s_v.n)
                                negc_done[0] = s_v.n

                                while pending_tail:
                                    pending_tail.pop(0)()
                                for jj in range(4):
                                    emit_exp_avsum(jj, jt_tiles[jj], first_banks[jj],
                                                   hh2, e_fill, av_bank, sum_bank,
                                                   vmasks[jj])
                                for jj in range(4, 8):
                                    b = 2 + (jj % 2)
                                    vm = emit_scores(jj, jt_tiles[jj], b, ch, qk_ready,
                                                     first)
                                    emit_exp_avsum(jj, jt_tiles[jj], b, hh2, e_fill,
                                                   av_bank, sum_bank, vm)
                                negc_ready_first = negc_done[0]
                            else:
                                sc_banks = [2, 3, 7]
                                fills_jj = {}
                                for jj in range(3):
                                    fills_jj[jj] = emit_scores(jj, jt_tiles[jj],
                                                               sc_banks[jj % 3], ch,
                                                               qk_ready, first)
                                while pending_tail:
                                    pending_tail.pop(0)()
                                for jj in range(8):
                                    emit_exp_avsum(jj, jt_tiles[jj], sc_banks[jj % 3],
                                                   hh2, e_fill, av_bank, sum_bank,
                                                   fills_jj[jj])
                                    if jj + 3 < 8:
                                        fills_jj[jj + 3] = emit_scores(
                                            jj + 3, jt_tiles[jj + 3],
                                            sc_banks[(jj + 3) % 3], ch, qk_ready, first)

                            avsum_fill = s_mm.n

                            def emit_tail(h=h, ch=ch, av_bank=av_bank,
                                          fill=avsum_fill, ag=av_guard[0],
                                          rg=rsbc_guard[0]):
                                @emit("V")
                                def _(eng, fill=fill):
                                    eng.wait_ge(s_mm.h, fill)
                                    eng.reciprocal(rs_sb[:],
                                                   pb[6][0:1, :]).then_inc(s_v.h, 1)
                                s_v.n += 1
                                bank_free[6] = (s_v, s_v.n)
                                v_rs = s_v.n

                                @emit("T")
                                def _(eng, v_rs=v_rs, bfw=bw(6)):
                                    eng.wait_ge(s_v.h, v_rs)
                                    wait_pairs(eng, [bfw])
                                    eng.matmul(pb[6][:], ones_row_f[:], rs_sb[:],
                                               start=True, stop=True,
                                               skip_group_check=True).then_inc(s_mm.h, 1)
                                s_mm.n += 1
                                bank_fill[6] = s_mm.n
                                f_bc = s_mm.n

                                @emit("S")
                                def _(eng, f_bc=f_bc, rg=rg):
                                    eng.wait_ge(s_mm.h, f_bc)
                                    if rg is not None:
                                        eng.wait_ge(s_v.h, rg)
                                    eng.copy(rsbc_sb[:], pb[6][:]).then_inc(s_s.h, 1)
                                s_s.n += 1
                                bank_free[6] = (s_s, s_s.n)
                                s_bc = s_s.n

                                @emit("V")
                                def _(eng, h=h, ch=ch, s_bc=s_bc, ag=ag,
                                      av_bank=av_bank):
                                    eng.wait_ge(s_s.h, s_bc)
                                    wait_pairs(eng, ag if isinstance(ag, list) else [ag])
                                    hl = h % 4
                                    eng.tensor_tensor(
                                        av_sb[:, hl * 1024 + ch * 512:hl * 1024 + ch * 512 + 512],
                                        pb[av_bank][:], rsbc_sb[:],
                                        ALU.mult).then_inc(s_v.h, 1)
                                s_v.n += 1
                                bank_free[av_bank] = (s_v, s_v.n)
                                av_writes.append(s_v.n)
                                rsbc_guard[0] = s_v.n

                            pending_tail.append(emit_tail)
                        qk_guard[0] = (s_mm, s_mm.n)
                        if h == 15:
                            att_qk_done[0] = (s_mm, s_mm.n)
                    v4_guard[0] = (s_mm, s_mm.n)

                while pending_tail:
                    pending_tail.pop(0)()
                av_ready = max(av_writes)
                for dg in range(4):
                    wslot, wval, widx = load_wout(qset, dg)
                    for j, tt in enumerate(range(4, 12)):
                        b = 6 + (j % 2)

                        @emit("T")
                        def _(eng, dg=dg, tt=tt, b=b, j=j, wslot=wslot, wval=wval,
                              av_ready=av_ready, bfw=bw(b)):
                            if j == 0:
                                eng.wait_ge(s_wout[wslot].h, wval)
                                eng.wait_ge(s_v.h, av_ready)
                            wait_pairs(eng, [bfw])
                            ch2 = (tt - 4) // 4
                            pos = ((tt - 4) % 4) * 128
                            mm = None
                            for ekt in range(4):
                                mm = eng.matmul(
                                    pb[b][:],
                                    av_sb[:, ekt * 1024 + ch2 * 512 + pos:
                                          ekt * 1024 + ch2 * 512 + pos + 128],
                                    wout_sb[wslot][:, ekt * 512:(ekt + 1) * 512],
                                    start=(ekt == 0), stop=(ekt == 3),
                                    skip_group_check=True)
                            mm.then_inc(s_mm.h, 1)
                        s_mm.n += 1
                        bank_fill[b] = s_mm.n

                        @emit("V")
                        def _(eng, tt=tt, b=b, fill=bank_fill[b], dg=dg):
                            eng.wait_ge(s_mm.h, fill)
                            xs = X[:, (tt - 4) * D + dg * 512:(tt - 4) * D + dg * 512 + 512]
                            eng.tensor_tensor(xs, xs, pb[b][:], ALU.add).then_inc(s_v.h, 1)
                        s_v.n += 1
                        bank_free[b] = (s_v, s_v.n)
                        if qset == 3 and dg == 3:
                            att_tile_evac[tt] = s_v.n
                            if tile_hook is not None:
                                tile_hook(tt, s_v.n)
                    wout_state["free"][widx] = (s_mm, s_mm.n)
                av_guard[0] = (s_mm, s_mm.n)

            g = guard_now()
            if stage == "attn":
                _dbg(emit, X, X_halo, dbg, s_v, s_st)
            return g, att_tile_evac, att_qk_done[0]

    def _dbg(emit_, X_, Xh_, dbg_, sv_, sst_):
        rv = sv_.n

        @emit_("Y")
        def _(eng, rv=rv):
            eng.wait_ge(sv_.h, rv)
            eng.dma_start(out=dbg_["own"].rearrange("(t p) d -> p t d", p=128),
                          in_=X_[:].rearrange("p (t d) -> p t d", t=NT_OWN)
                          ).then_inc(sst_.h, 16)
        sst_.n += 16

        @emit_("G")
        def _(eng, rv=rv):
            eng.wait_ge(sv_.h, rv)
            eng.dma_start(out=dbg_["halo"].rearrange("(t p) d -> p t d", p=128),
                          in_=Xh_[:].rearrange("p (t d) -> p t d", t=4)
                          ).then_inc(sst_.h, 16)
        sst_.n += 16

        @emit_("Y")
        def _(eng, tot=sst_.n):
            eng.wait_ge(sst_.h, tot)

    xr, thr1 = emit_ln(0, list(range(12)), [])
    vr = emit_routing(0, list(range(12)), xr, thr1)
    pre1 = (lambda: _dbg(emit, X, X_halo, dbg, s_v, s_st)) if stage == "ffn1" else None
    ln2_hook, ln2_state = make_ln_hook(1, list(range(12)), [], xr)
    g_ffn1, te1, g_h7 = emit_ffn(0, list(range(12)), [0, 512, 1024], vr, xr,
                                 guard_now(), pre_drain=pre1,
                                 tile_hook=None if stage == "ffn1" else ln2_hook)
    if stage == "ffn1":
        drain()
        es.close()
        return nc

    xr2, thr2 = emit_ln(1, list(range(12)), [], hook_state=ln2_state)
    ln3_hook, ln3_state = make_ln_hook(2, list(range(4, 12)), [], xr2)
    g_att, te2, g_qk15 = emit_attention(xr2, thr2,
                                        g_ffn1 + [(s_v, s_v.n), (s_s, s_s.n)],
                                        g_h7,
                                        tile_hook=None if stage == "attn" else ln3_hook)
    if stage == "attn":
        drain()
        es.close()
        return nc

    xr3, thr3 = emit_ln(2, list(range(4, 12)), [], hook_state=ln3_state)
    vr3 = emit_routing(2, list(range(4, 12)), xr3, thr3)

    def pre_final():
        final_v = s_v.n

        @emit("Y")
        def _(eng, final_v=final_v):
            eng.wait_ge(s_v.h, final_v)
            eng.dma_start(out=dr_y.rearrange("(t p) d -> p t d", p=128),
                          in_=X[:].rearrange("p (t d) -> p t d", t=NT_OWN)
                          ).then_inc(s_st.h, 16)
        s_st.n += 16

        @emit("Y")
        def _(eng, tot=s_st.n):
            eng.wait_ge(s_st.h, tot)

    full_att_guard = g_att + [(s_v, s_v.n), (s_s, s_s.n)]
    emit_ffn(2, list(range(4, 12)), [512, 1024], vr3, xr3,
             {"wi0": [g_qk15], "wi1": full_att_guard, "wo": full_att_guard,
              "inter": full_att_guard}, pre_drain=pre_final)
    drain()
    es.close()
    return nc


def host_inputs(L0_attn_x, L0_attn_w_qkv, L0_attn_w_out, L0_ffn_w_gate,
                L0_ffn_wi_0, L0_ffn_wi_1, L0_ffn_wo):
    x = np.ascontiguousarray(np.asarray(L0_attn_x, dtype=np.float32).reshape(8192, D))
    wqkT = np.ascontiguousarray(np.asarray(L0_attn_w_qkv, dtype=np.float32).T)
    woutT = np.ascontiguousarray(np.asarray(L0_attn_w_out, dtype=np.float32).T)
    wg = np.asarray(L0_ffn_w_gate, dtype=np.float32)
    wi0 = np.asarray(L0_ffn_wi_0, dtype=np.float32)
    wi1 = np.asarray(L0_ffn_wi_1, dtype=np.float32)
    wo = np.asarray(L0_ffn_wo, dtype=np.float32)

    wg_t = np.ascontiguousarray(wg.reshape(16, 128, 8).transpose(1, 0, 2))
    wi0_t = np.ascontiguousarray(wi0.reshape(E, 16, 128, 4, 256).transpose(0, 3, 2, 1, 4))
    wi1_t = np.ascontiguousarray(wi1.reshape(E, 16, 128, 4, 256).transpose(0, 3, 2, 1, 4))
    wo_t = np.ascontiguousarray(wo.reshape(E, 2, 4, 128, 2048).transpose(0, 1, 3, 2, 4))
    wqk_t = np.ascontiguousarray(
        wqkT[:, :4096].reshape(16, 128, 32, 128).transpose(2, 1, 0, 3))
    wv_t = np.ascontiguousarray(
        wqkT[:, 4096:].reshape(16, 128, 8, 256).transpose(2, 1, 0, 3))
    wout_t = np.ascontiguousarray(
        woutT.reshape(4, 4, 128, 4, 512).transpose(0, 3, 2, 1, 4))

    p = np.arange(128)[:, None]
    i = np.arange(512)[None, :]
    band = np.zeros((128, 8, 512), dtype=np.float32)
    for jt in range(8):
        j = 128 * jt + p
        ok = (i >= j - 512) & (i <= j)
        band[:, jt, :] = np.where(ok, 0.0, np.float32(-1e9))
    band = band.astype(BF)
    kill = np.full((128, 4, 512), -1e9, dtype=np.float32).astype(BF)
    ident = np.eye(128, dtype=np.float32)

    shared = dict(wg=wg_t, wi0=wi0_t, wi1=wi1_t, wo=wo_t, wqk=wqk_t, wv=wv_t,
                  wout=wout_t, band=band, ident=ident)
    in_maps = []
    for c in range(8):
        g0, g1, g2 = 2 * c, 2 * c + 1, (2 * c + 2) % 16
        x_halo = x[g0 * CH:(g0 + 1) * CH]
        x_own = np.concatenate([x[g1 * CH:(g1 + 1) * CH], x[g2 * CH:(g2 + 1) * CH]])
        q2_first = (g2 % 4 == 0)
        mpre_c = kill if q2_first else np.ascontiguousarray(band[:, 0:4])
        m = dict(shared)
        m["x_own"] = np.ascontiguousarray(x_own)
        m["x_halo"] = np.ascontiguousarray(x_halo)
        m["mpre"] = np.ascontiguousarray(mpre_c)
        in_maps.append(m)
    return in_maps


def assemble(results):
    y = np.empty((8192, D), dtype=np.float32)
    for c in range(8):
        g1, g2 = 2 * c + 1, (2 * c + 2) % 16
        yc = results[c]["y"]
        y[g1 * CH:(g1 + 1) * CH] = yc[0:CH]
        y[g2 * CH:(g2 + 1) * CH] = yc[CH:2 * CH]
    return y.reshape(4, 2048, D)


def kernel(L0_attn_x, L0_attn_w_qkv, L0_attn_w_out, L0_ffn_w_gate,
           L0_ffn_wi_0, L0_ffn_wi_1, L0_ffn_wo):
    in_maps = host_inputs(L0_attn_x, L0_attn_w_qkv, L0_attn_w_out, L0_ffn_w_gate,
                          L0_ffn_wi_0, L0_ffn_wi_1, L0_ffn_wo)
    nc = build_program("full")
    res = run_bass_kernel_spmd(nc, in_maps, core_ids=list(range(8)))
    return assemble(res.results)


# version 12
# speedup vs baseline: 1.1562x; 1.0026x over previous
import sys
sys.path.insert(0, "/opt/trn_rl_repo")
import contextlib
import numpy as np
import ml_dtypes
import concourse.bass as bass
import concourse.mybir as mybir
from concourse.bass_utils import run_bass_kernel_spmd

F32 = mybir.dt.float32
BF16 = mybir.dt.bfloat16
AF = mybir.ActivationFunctionType
ALU = mybir.AluOpType
AXX = mybir.AxisListType.X
BF = ml_dtypes.bfloat16

D = 2048
E = 8
CH = 512
NT3 = 12
NT_OWN = 8
TOKW = 1536
QSCALE = float(1.0 / np.float64(np.float32(np.sqrt(np.float64(128.0)))))
EPS = 1e-5


class Ctr:
    def __init__(self, h):
        self.h = h
        self.n = 0


def build_program(stage="full"):
    nc = bass.Bass("TRN2", target_bir_lowering=False)

    dr_x_own = nc.dram_tensor("x_own", [1024, D], F32, kind="ExternalInput").ap()
    dr_x_halo = nc.dram_tensor("x_halo", [512, D], F32, kind="ExternalInput").ap()
    dr_wg = nc.dram_tensor("wg", [128, 16, 8], F32, kind="ExternalInput").ap()
    dr_wi0 = nc.dram_tensor("wi0", [E, 4, 128, 16, 256], F32, kind="ExternalInput").ap()
    dr_wi1 = nc.dram_tensor("wi1", [E, 4, 128, 16, 256], F32, kind="ExternalInput").ap()
    dr_wo = nc.dram_tensor("wo", [E, 2, 128, 4, 2048], F32, kind="ExternalInput").ap()
    dr_wqk = nc.dram_tensor("wqk", [32, 128, 16, 128], F32, kind="ExternalInput").ap()
    dr_wv = nc.dram_tensor("wv", [8, 128, 16, 256], F32, kind="ExternalInput").ap()
    dr_wout = nc.dram_tensor("wout", [4, 4, 128, 4, 512], F32, kind="ExternalInput").ap()
    dr_band = nc.dram_tensor("band", [128, 8, 512], BF16, kind="ExternalInput").ap()
    dr_mpre = nc.dram_tensor("mpre", [128, 4, 512], BF16, kind="ExternalInput").ap()
    dr_ident = nc.dram_tensor("ident", [128, 128], F32, kind="ExternalInput").ap()
    dr_y = nc.dram_tensor("y", [1024, D], F32, kind="ExternalOutput").ap()
    dbg = {}
    if stage != "full":
        dbg["own"] = nc.dram_tensor("dbg_own", [1024, D], F32, kind="ExternalOutput").ap()
        dbg["halo"] = nc.dram_tensor("dbg_halo", [512, D], F32, kind="ExternalOutput").ap()

    es = contextlib.ExitStack()
    sbuf = lambda n, s, d: es.enter_context(nc.sbuf_tensor(n, s, d))

    X = sbuf("X", [128, NT_OWN * D], F32)
    XNT = sbuf("XNT", [128, 16 * TOKW], BF16)
    X_halo = sbuf("X_halo", [128, 4 * D], BF16)
    xn_tmp = [sbuf(f"xn_tmp{i}", [128, D], BF16) for i in range(2)]
    wg_sb = sbuf("wg_sb", [128, 16 * 8], BF16)
    what_sb = sbuf("what_sb", [128, NT3 * 8], F32)
    stats = sbuf("stats", [128, 64], F32)
    lg_all = sbuf("lg_all", [128, NT3 * 8], F32)
    top_all = sbuf("top_all", [128, NT3 * 8], F32)
    r_scr = sbuf("r_scr", [128, 3 * 16], F32)
    m_scr = sbuf("m_scr", [128, 16], F32)
    ident_sb = sbuf("ident_sb", [128, 128], F32)
    ident_bf = sbuf("ident_bf", [128, 128], BF16)
    ones_col_bf = sbuf("ones_col_bf", [128, 1], BF16)
    ones_row_f = sbuf("ones_row_f", [1, 128], F32)
    neg_c = sbuf("neg_c", [128, 1], F32)
    cchain = sbuf("cchain", [128, 8], F32)
    cmax = sbuf("cmax", [1, 1], F32)
    rs_sb = sbuf("rs_sb", [1, 512], F32)
    rsbc_sb = sbuf("rsbc_sb", [128, 512], BF16)

    pb = [es.enter_context(nc.psum_tensor(f"pb{i}", [128, 512], F32)) for i in range(8)]

    sem = lambda n: Ctr(es.enter_context(nc.semaphore(n)))
    s_mm = sem("s_mm")
    s_v = sem("s_v")
    s_s = sem("s_s")
    s_g = sem("s_g")
    s_stpY = sem("s_stpY")
    s_stpG = sem("s_stpG")
    s_tp = [sem("s_tpA"), sem("s_tpB")]
    s_wi = [sem("s_wi0"), sem("s_wi1")]
    s_wo = [sem("s_wo0"), sem("s_wo1")]
    s_wqk = [sem("s_wqk0"), sem("s_wqk1")]
    s_wv = sem("s_wv")
    s_wout = [sem("s_wout0"), sem("s_wout1")]
    s_mask = sem("s_mask")
    s_st = sem("s_st")

    ev = {k: [] for k in ("T", "V", "S", "G", "Y")}

    def emit(which):
        def deco(f):
            ev[which].append(f)
        return deco

    def drain():
        with nc.Block() as block:
            @block.tensor
            def _(eng):
                for f in ev["T"]:
                    f(eng)

            @block.vector
            def _(eng):
                for f in ev["V"]:
                    f(eng)

            @block.scalar
            def _(eng):
                for f in ev["S"]:
                    f(eng)

            @block.gpsimd
            def _(eng):
                for f in ev["G"]:
                    f(eng)

            @block.sync
            def _(eng):
                for f in ev["Y"]:
                    f(eng)
        for k in ev:
            ev[k].clear()

    bank_free = [None] * 8
    bank_fill = [0] * 8

    def bw(b):
        return bank_free[b]

    def wait_pairs(eng, pairs):
        for p in pairs:
            if p is not None:
                eng.wait_ge(p[0].h, p[1])

    def guard_now():
        return [(s_mm, s_mm.n), (s_v, s_v.n), (s_s, s_s.n)]

    @emit("Y")
    def _(eng):
        eng.dma_start(out=X[:].rearrange("p (t d) -> p t d", t=NT_OWN),
                      in_=dr_x_own.rearrange("(t p) d -> p t d", p=128)).then_inc(s_stpY.h, 16)
        eng.dma_start(out=ident_sb[:], in_=dr_ident).then_inc(s_stpY.h, 16)
    s_stpY.n += 32

    @emit("G")
    def _(eng):
        eng.memset(ones_col_bf[:], 1.0)
        eng.memset(ones_row_f[:], 1.0).then_inc(s_g.h, 1)
        eng.dma_start(out=X_halo[:].rearrange("p (t d) -> p t d", t=4),
                      in_=dr_x_halo.rearrange("(t p) d -> p t d", p=128)).then_inc(s_stpG.h, 16)
        eng.dma_start(out=wg_sb[:].rearrange("p (k j) -> p k j", k=16),
                      in_=dr_wg).then_inc(s_stpG.h, 16)
        eng.dma_start(out=ident_bf[:], in_=dr_ident).then_inc(s_stpG.h, 16)
    s_g.n += 1
    s_stpG.n += 48
    setupY_val = s_stpY.n
    setupG_val = s_stpG.n

    xnt_uses = [0, 0]
    last_xnt_reader = [0]

    def x_tile(tt):
        if tt < 4:
            return X_halo[:, tt * D:(tt + 1) * D]
        return X[:, (tt - 4) * D:(tt - 3) * D]

    def emit_ln_tile_stats(call, i, tt, dep, td, st):
        @emit("V")
        def _(eng, i=i, tt=tt, dep=list(dep), call=call, td=td):
            if i == 0 and call == 0:
                eng.wait_ge(s_stpY.h, setupY_val)
                eng.wait_ge(s_stpG.h, setupG_val)
            if td is not None:
                eng.wait_ge(s_v.h, td)
            elif i == 0:
                wait_pairs(eng, dep)
            eng.tensor_reduce(stats[:, i:i + 1], x_tile(tt), AXX,
                              ALU.add).then_inc(s_v.h, 1)
        s_v.n += 1
        st["red"][i] = s_v.n

        @emit("S")
        def _(eng, i=i, tt=tt, dep=list(dep), call=call, td=td,
              w=st["sq"].get(i - 2, 0), w_tp=16 * xnt_uses[i % 2]):
            if i == 0:
                if call == 0:
                    eng.wait_ge(s_stpY.h, setupY_val)
                    eng.wait_ge(s_stpG.h, setupG_val)
                wait_pairs(eng, dep)
            if w_tp > 0:
                eng.wait_ge(s_tp[i % 2].h, w_tp)
            if td is not None:
                eng.wait_ge(s_v.h, td)
            if w > 0:
                eng.wait_ge(s_s.h, w)
            eng.activation(xn_tmp[i % 2][:], x_tile(tt), AF.Square,
                           accum_out=stats[:, 12 + i:13 + i]).then_inc(s_s.h, 1)
        s_s.n += 1
        st["sq"][i] = s_s.n

    def emit_ln_batch(call, st, h0, h1):
        vw = max(st["red"][j] for j in range(h0, h1))
        sw = max(st["sq"][j] for j in range(h0, h1))

        @emit("V")
        def _(eng, h0=h0, h1=h1, vw=vw):
            eng.wait_ge(s_v.h, vw)
            eng.tensor_scalar(stats[:, 24 + h0:24 + h1], stats[:, h0:h1], 1.0 / D,
                              None, ALU.mult).then_inc(s_v.h, 1)
        s_v.n += 1
        v_mu = s_v.n

        @emit("V")
        def _(eng, h0=h0, h1=h1, v_mu=v_mu):
            eng.wait_ge(s_v.h, v_mu)
            eng.tensor_tensor(stats[:, 48 + h0:48 + h1], stats[:, 24 + h0:24 + h1],
                              stats[:, 24 + h0:24 + h1], ALU.mult).then_inc(s_v.h, 1)
        s_v.n += 1
        v_musq = s_v.n

        @emit("V")
        def _(eng, h0=h0, h1=h1, v_musq=v_musq, sw=sw):
            eng.wait_ge(s_v.h, v_musq)
            eng.wait_ge(s_s.h, sw)
            eng.scalar_tensor_tensor(stats[:, 48 + h0:48 + h1],
                                     stats[:, 12 + h0:12 + h1], 1.0 / D,
                                     stats[:, 48 + h0:48 + h1], ALU.mult,
                                     ALU.subtract).then_inc(s_v.h, 1)
        s_v.n += 1
        v_var = s_v.n

        @emit("V")
        def _(eng, h0=h0, h1=h1, v_var=v_var):
            eng.wait_ge(s_v.h, v_var)
            eng.tensor_scalar(stats[:, 48 + h0:48 + h1], stats[:, 48 + h0:48 + h1],
                              EPS, None, ALU.add).then_inc(s_v.h, 1)
        s_v.n += 1
        v_vpe = s_v.n

        @emit("S")
        def _(eng, h0=h0, h1=h1, v_vpe=v_vpe):
            eng.wait_ge(s_v.h, v_vpe)
            eng.activation(stats[:, 36 + h0:36 + h1], stats[:, 48 + h0:48 + h1],
                           AF.Sqrt).then_inc(s_s.h, 1)
        s_s.n += 1
        s_sqrt = s_s.n

        @emit("V")
        def _(eng, h0=h0, h1=h1, s_sqrt=s_sqrt):
            eng.wait_ge(s_s.h, s_sqrt)
            eng.reciprocal(stats[:, 36 + h0:36 + h1],
                           stats[:, 36 + h0:36 + h1]).then_inc(s_v.h, 1)
        s_v.n += 1
        rstd_v = s_v.n
        for j in range(h0, h1):
            _emit_ln_norm(call, j, st["tts"][j], st, rstd_v)

    def _emit_ln_norm(call, i, tt, st, rstd_v):
        slot = i % 2
        tp_wait = 16 * xnt_uses[slot]
        xnt_uses[slot] += 1

        @emit("V")
        def _(eng, i=i, tt=tt, slot=slot, tp_wait=tp_wait, rstd_v=rstd_v):
            eng.wait_ge(s_v.h, rstd_v)
            if tp_wait:
                eng.wait_ge(s_tp[slot].h, tp_wait)
            eng.tensor_scalar(xn_tmp[slot][:], x_tile(tt), stats[:, 24 + i:25 + i],
                              stats[:, 36 + i:37 + i], ALU.subtract,
                              ALU.mult).then_inc(s_v.h, 1)
        s_v.n += 1
        v_norm = s_v.n

        @emit("Y" if slot == 0 else "S")
        def _(eng, i=i, tt=tt, slot=slot, v_norm=v_norm, xg=st["xg"]):
            eng.wait_ge(s_v.h, v_norm)
            if i <= 1:
                eng.wait_ge(s_mm.h, xg)
            eng.dma_start(
                out=XNT[:].rearrange("p (k j) -> p k j", k=16)[:, :, tt * 128:(tt + 1) * 128],
                in_=xn_tmp[slot][:].rearrange("p (k j) -> p k j", k=16),
                transpose=True).then_inc(s_tp[slot].h, 16)
        s_tp[slot].n += 16
        st["thr"][tt] = (slot, s_tp[slot].n)

    def _new_ln_state(tts, tp_prev):
        return {"sq": {}, "red": {}, "thr": {}, "done": [],
                "tts": list(tts), "tp_prev": tp_prev, "xg": last_xnt_reader[0]}

    def make_ln_hook(call, tts, dep, tp_prev):
        st = _new_ln_state(tts, tp_prev)
        n = len(tts)

        q = n // 4

        def hook(tt, td):
            if not st["done"]:
                st["xg"] = last_xnt_reader[0]
            i = tts.index(tt)
            emit_ln_tile_stats(call, i, tt, dep, td, st)
            st["done"].append(tt)
            if (i + 1) % q == 0:
                emit_ln_batch(call, st, i + 1 - q, i + 1)
        return hook, st

    def emit_ln(call, tts, dep, dep_tile=None, hook_state=None, tp_prev=None):
        n = len(tts)
        if hook_state is None:
            st = _new_ln_state(tts, tp_prev)
            st["xg"] = last_xnt_reader[0]
            q = n // 4
            for i, tt in enumerate(tts):
                td = dep_tile.get(tt) if dep_tile else None
                emit_ln_tile_stats(call, i, tt, dep, td, st)
                if (i + 1) % q == 0:
                    emit_ln_batch(call, st, i + 1 - q, i + 1)
        else:
            st = hook_state
            assert len(st["done"]) == len(tts), st["done"]
        return (s_tp[0].n, s_tp[1].n), st["thr"]

    def emit_routing(call, tts, xnt_ready, tile_thr):
        n = len(tts)
        for i, tt in enumerate(tts):
            b = 6 + (i % 2)

            @emit("T")
            def _(eng, i=i, tt=tt, b=b, bfw=bw(b), call=call, thr=tile_thr[tt]):
                eng.wait_ge(s_tp[thr[0]].h, thr[1])
                if i == 0 and call == 0:
                    eng.wait_ge(s_stpG.h, setupG_val)
                wait_pairs(eng, [bfw])
                mm = None
                for kt in range(16):
                    mm = eng.matmul(pb[b][:, 0:8],
                                    XNT[:, kt * TOKW + tt * 128:kt * TOKW + tt * 128 + 128],
                                    wg_sb[:, kt * 8:(kt + 1) * 8],
                                    start=(kt == 0), stop=(kt == 15),
                                    skip_group_check=True)
                mm.then_inc(s_mm.h, 1)
            s_mm.n += 1
            bank_fill[b] = s_mm.n

            @emit("S")
            def _(eng, tt=tt, b=b, fill=bank_fill[b]):
                eng.wait_ge(s_mm.h, fill)
                eng.copy(lg_all[:, tt * 8:(tt + 1) * 8], pb[b][:, 0:8]).then_inc(s_s.h, 1)
            s_s.n += 1
            bank_free[b] = (s_s, s_s.n)
            s_lg = s_s.n

            @emit("V")
            def _(eng, tt=tt, s_lg=s_lg):
                eng.wait_ge(s_s.h, s_lg)
                eng.max(top_all[:, tt * 8:(tt + 1) * 8],
                        lg_all[:, tt * 8:(tt + 1) * 8]).then_inc(s_v.h, 1)
            s_v.n += 1
            v_max = s_v.n

            @emit("V")
            def _(eng, i=i, tt=tt, v_max=v_max):
                eng.wait_ge(s_v.h, v_max)
                eng.tensor_tensor(r_scr[:, i:i + 1], top_all[:, tt * 8:tt * 8 + 1],
                                  top_all[:, tt * 8 + 1:tt * 8 + 2],
                                  ALU.subtract).then_inc(s_v.h, 1)
            s_v.n += 1
        v_d12 = s_v.n

        @emit("S")
        def _(eng, v_d12=v_d12, n=n):
            eng.wait_ge(s_v.h, v_d12)
            eng.activation(r_scr[:, 16:16 + n], r_scr[:, 0:n],
                           AF.Sigmoid).then_inc(s_s.h, 1)
        s_s.n += 1
        s_sig = s_s.n

        @emit("V")
        def _(eng, s_sig=s_sig, n=n):
            eng.wait_ge(s_s.h, s_sig)
            eng.tensor_scalar(r_scr[:, 16:16 + n], r_scr[:, 16:16 + n], 0.5, None,
                              ALU.mult).then_inc(s_v.h, 1)
        s_v.n += 1
        v_w1 = s_v.n

        @emit("V")
        def _(eng, v_w1=v_w1, n=n):
            eng.wait_ge(s_v.h, v_w1)
            eng.tensor_scalar(r_scr[:, 32:32 + n], r_scr[:, 16:16 + n], -1.0, 0.5,
                              ALU.mult, ALU.add).then_inc(s_v.h, 1)
        s_v.n += 1
        prev = s_v.n

        for i, tt in enumerate(tts):
            @emit("V")
            def _(eng, i=i, tt=tt, pw=prev):
                eng.wait_ge(s_v.h, pw)
                sl = slice(tt * 8, (tt + 1) * 8)
                eng.tensor_scalar(m_scr[:, 0:8], lg_all[:, sl],
                                  top_all[:, tt * 8:tt * 8 + 1], None, ALU.is_equal)
                eng.tensor_scalar(m_scr[:, 8:16], lg_all[:, sl],
                                  top_all[:, tt * 8 + 1:tt * 8 + 2], None,
                                  ALU.is_equal).then_inc(s_v.h, 1)
            s_v.n += 1
            v_m = s_v.n

            @emit("V")
            def _(eng, i=i, v_m=v_m):
                eng.wait_ge(s_v.h, v_m)
                eng.tensor_scalar(m_scr[:, 8:16], m_scr[:, 8:16],
                                  r_scr[:, 32 + i:33 + i], None,
                                  ALU.mult).then_inc(s_v.h, 1)
            s_v.n += 1
            v_m2 = s_v.n

            @emit("V")
            def _(eng, i=i, tt=tt, v_m2=v_m2):
                eng.wait_ge(s_v.h, v_m2)
                eng.scalar_tensor_tensor(what_sb[:, tt * 8:(tt + 1) * 8],
                                         m_scr[:, 0:8], r_scr[:, 16 + i:17 + i],
                                         m_scr[:, 8:16], ALU.mult,
                                         ALU.add).then_inc(s_v.h, 1)
            s_v.n += 1
            prev = s_v.n
        return s_v.n

    def emit_ffn(call, tts, groups, v_routing, xnt_ready, region_guard, pre_drain=None, tile_hook=None):
        ng = len(groups)
        with contextlib.ExitStack() as fes:
            fsb = lambda nme, s, d: fes.enter_context(
                nc.sbuf_tensor(f"{nme}_c{call}", s, d))
            wi_sb = [fsb(f"wi_sb{i}", [128, 2 * 16 * 256], BF16) for i in range(2)]
            wo_sb = fsb("wo_sb", [128, 4 * 2048], BF16)
            inter = fsb("inter", [128, 4 * TOKW], BF16)

            rg = dict(region_guard) if isinstance(region_guard, dict) else {
                "wi0": region_guard, "wi1": region_guard, "wo": region_guard,
                "inter": region_guard}
            wi_state = {"n": 0, "free": {-2: rg["wi0"], -1: rg["wi1"]}}
            wo_state = {"n": 0, "free": {-1: rg["wo"]}}

            def load_wi(e, q):
                nl = wi_state["n"]
                slot = nl % 2
                gw = wi_state["free"].get(nl - 2)

                @emit("G")
                def _(eng, e=e, q=q, slot=slot, gw=gw):
                    if gw is not None:
                        wait_pairs(eng, gw if isinstance(gw, list) else [gw])
                    eng.dma_start(
                        out=wi_sb[slot][:, 0:4096].rearrange("p (k j) -> p k j", k=16),
                        in_=dr_wi0[e, q]).then_inc(s_wi[slot].h, 16)
                    eng.dma_start(
                        out=wi_sb[slot][:, 4096:8192].rearrange("p (k j) -> p k j", k=16),
                        in_=dr_wi1[e, q]).then_inc(s_wi[slot].h, 16)
                s_wi[slot].n += 32
                wi_state["n"] += 1
                return slot, s_wi[slot].n, nl

            def load_wo(e, ha):
                nl = wo_state["n"]
                gw = wo_state["free"].get(nl - 1)

                @emit("G")
                def _(eng, e=e, ha=ha, gw=gw):
                    if gw is not None:
                        wait_pairs(eng, gw if isinstance(gw, list) else [gw])
                    eng.dma_start(out=wo_sb[:].rearrange("p (k j) -> p k j", k=4),
                                  in_=dr_wo[e, ha]).then_inc(s_wo[0].h, 16)
                s_wo[0].n += 16
                wo_state["n"] += 1
                return s_wo[0].n, nl

            inter_guard = [rg["inter"]]
            wo_bank_ctr = [0]
            tile_evac = {}
            h_done_guard = [None]

            def emit_wo(e, ha, inter_ready):
                wo_val, wo_idx = load_wo(e, ha)
                for j, tt in enumerate(tts):
                    record_evac = (e == E - 1 and ha == 1)
                    bs = (0, 1, 2, 3) if (wo_bank_ctr[0] % 2 == 0) else (4, 5, 6, 7)
                    wo_bank_ctr[0] += 1
                    bfws = [bw(b) for b in bs]

                    @emit("T")
                    def _(eng, tt=tt, bs=bs, j=j, wo_val=wo_val,
                          inter_ready=inter_ready, bfws=bfws):
                        if j == 0:
                            eng.wait_ge(s_wo[0].h, wo_val)
                            eng.wait_ge(s_v.h, inter_ready)
                        wait_pairs(eng, bfws)
                        mm = None
                        for kt in range(4):
                            for dg in range(4):
                                mm = eng.matmul(
                                    pb[bs[dg]][:],
                                    inter[:, kt * TOKW + tt * 128:kt * TOKW + tt * 128 + 128],
                                    wo_sb[:, kt * 2048 + dg * 512:kt * 2048 + dg * 512 + 512],
                                    start=(kt == 0), stop=(kt == 3),
                                    skip_group_check=True)
                        mm.then_inc(s_mm.h, 1)
                    s_mm.n += 1
                    for b in bs:
                        bank_fill[b] = s_mm.n

                    @emit("V")
                    def _(eng, e=e, tt=tt, bs=bs, fill=s_mm.n):
                        eng.wait_ge(s_mm.h, fill)
                        xa = x_tile(tt)
                        last = None
                        for dg in range(4):
                            xs = xa[:, dg * 512:(dg + 1) * 512]
                            last = eng.scalar_tensor_tensor(
                                xs, pb[bs[dg]][:], what_sb[:, tt * 8 + e:tt * 8 + e + 1],
                                xs, ALU.mult, ALU.add)
                        last.then_inc(s_v.h, 1)
                    s_v.n += 1
                    for b in bs:
                        bank_free[b] = (s_v, s_v.n)
                    if record_evac:
                        tile_evac[tt] = s_v.n
                        if tile_hook is not None:
                            tile_hook(tt, s_v.n)
                wo_state["free"][wo_idx] = (s_mm, s_mm.n)
                inter_guard[0] = (s_mm, s_mm.n)

            wi_cur = None
            for e in range(E):
                for mt in range(8):
                    q, idx = mt // 2, mt % 2
                    if idx == 0:
                        wi_cur = load_wi(e, q)
                    wi_slot, wi_val, wi_idx = wi_cur
                    fills = {}
                    for w in (0, 1):
                        banks = (0, 1, 2) if w == 0 else (3, 4, 5)
                        bfws = [bw(banks[gi]) for gi in range(ng)]

                        @emit("T")
                        def _(eng, e=e, mt=mt, w=w, banks=banks, idx=idx,
                              wi_slot=wi_slot, wi_val=wi_val, bfws=bfws,
                              groups=list(groups), xnt_ready=xnt_ready):
                            if idx == 0 and w == 0:
                                eng.wait_ge(s_wi[wi_slot].h, wi_val)
                            if e == 0 and mt == 0 and w == 0:
                                eng.wait_ge(s_tp[0].h, xnt_ready[0])
                                eng.wait_ge(s_tp[1].h, xnt_ready[1])
                            wait_pairs(eng, bfws)
                            woff = w * 4096
                            mm = None
                            for kt in range(16):
                                for gi, g0 in enumerate(groups):
                                    mm = eng.matmul(
                                        pb[banks[gi]][:],
                                        wi_sb[wi_slot][:, woff + kt * 256 + idx * 128:
                                                       woff + kt * 256 + idx * 128 + 128],
                                        XNT[:, kt * TOKW + g0:kt * TOKW + g0 + 512],
                                        start=(kt == 0), stop=(kt == 15),
                                        skip_group_check=True)
                            mm.then_inc(s_mm.h, 1)
                        s_mm.n += 1
                        for gi in range(ng):
                            bank_fill[banks[gi]] = s_mm.n
                        fills[w] = s_mm.n
                    if idx == 1:
                        wi_state["free"][wi_idx] = (s_mm, s_mm.n)

                    ig = inter_guard[0] if mt % 4 == 0 else None

                    @emit("S")
                    def _(eng, mt=mt, h0f=fills[0], ng=ng, groups=list(groups), ig=ig):
                        eng.wait_ge(s_mm.h, h0f)
                        if ig is not None:
                            wait_pairs(eng, ig if isinstance(ig, list) else [ig])
                        last = None
                        for gi, g0 in enumerate(groups):
                            last = eng.activation(
                                inter[:, (mt % 4) * TOKW + g0:(mt % 4) * TOKW + g0 + 512],
                                pb[gi][:], AF.Silu)
                        last.then_inc(s_s.h, 1)
                    s_s.n += 1
                    for b in (0, 1, 2)[:ng]:
                        bank_free[b] = (s_s, s_s.n)
                    s_silu = s_s.n

                    @emit("V")
                    def _(eng, mt=mt, s_silu=s_silu, h1f=fills[1], ng=ng,
                          groups=list(groups)):
                        eng.wait_ge(s_s.h, s_silu)
                        eng.wait_ge(s_mm.h, h1f)
                        last = None
                        for gi, g0 in enumerate(groups):
                            sl = slice((mt % 4) * TOKW + g0, (mt % 4) * TOKW + g0 + 512)
                            last = eng.tensor_tensor(inter[:, sl], inter[:, sl],
                                                     pb[(3, 4, 5)[gi]][:], ALU.mult)
                        last.then_inc(s_v.h, 1)
                    s_v.n += 1
                    for b in (3, 4, 5)[:ng]:
                        bank_free[b] = (s_v, s_v.n)
                    if e == E - 1 and mt == 7:
                        h_done_guard[0] = (s_mm, s_mm.n)
                    if mt % 4 == 3:
                        emit_wo(e, mt // 4, s_v.n)

            g = guard_now()
            if pre_drain is not None:
                pre_drain()
            return g, tile_evac, h_done_guard[0]

    def emit_attention(xnt_ready, tile_thr, region_guard, early_guard,
                       tile_hook=None):
        with contextlib.ExitStack() as aes:
            asb = lambda nme, s, d: aes.enter_context(nc.sbuf_tensor(nme, s, d))
            wv_sb = asb("wv_sb", [128, 16 * 256], BF16)
            wqk_sb = [asb(f"wqk_sb{i}", [128, 2 * 16 * 128], BF16) for i in range(2)]
            wout_sb = [asb(f"wout_sb{i}", [128, 4 * 512], BF16) for i in range(2)]
            av_sb = asb("av_sb", [128, 4 * 1024], BF16)
            E_sb = [asb(f"E_sb{i}", [128, 512], BF16) for i in range(2)]
            v4 = asb("v4", [128, NT3 * 256], BF16)
            q_sb = asb("q_sb", [128, 1024], BF16)
            k_sb = asb("k_sb", [128, TOKW], BF16)
            masks = asb("masks", [128, 8 * 512], BF16)
            mpre = asb("mpre_sb", [128, 4 * 512], BF16)

            masks_state = [False]

            def emit_masks_load():
                @emit("G")
                def _(eng, rg=list(region_guard)):
                    wait_pairs(eng, rg)
                    eng.dma_start(out=masks[:].rearrange("p (j i) -> p j i", j=8),
                                  in_=dr_band).then_inc(s_mask.h, 16)
                    eng.dma_start(out=mpre[:].rearrange("p (j i) -> p j i", j=4),
                                  in_=dr_mpre).then_inc(s_mask.h, 16)
                s_mask.n += 32
                masks_state[0] = True
            mask_val = s_mask.n + 32

            wqk_state = {"n": 0, "free": {-2: [early_guard], -1: [early_guard]}}
            wv_state = {"n": 0, "free": {-1: [early_guard]}}
            wout_state = {"n": 0, "free": {-2: [early_guard], -1: [early_guard]}}

            def load_wqk(h):
                nl = wqk_state["n"]
                slot = nl % 2
                gw = wqk_state["free"].get(nl - 2)

                @emit("G")
                def _(eng, h=h, slot=slot, gw=gw):
                    if gw is not None:
                        wait_pairs(eng, gw if isinstance(gw, list) else [gw])
                    eng.dma_start(
                        out=wqk_sb[slot][:, 0:2048].rearrange("p (k j) -> p k j", k=16),
                        in_=dr_wqk[h]).then_inc(s_wqk[slot].h, 16)
                    eng.dma_start(
                        out=wqk_sb[slot][:, 2048:4096].rearrange("p (k j) -> p k j", k=16),
                        in_=dr_wqk[16 + h]).then_inc(s_wqk[slot].h, 16)
                s_wqk[slot].n += 32
                wqk_state["n"] += 1
                return slot, s_wqk[slot].n, nl

            def load_wv(vg):
                nl = wv_state["n"]
                gw = wv_state["free"].get(nl - 1)

                @emit("G")
                def _(eng, vg=vg, gw=gw):
                    if gw is not None:
                        wait_pairs(eng, gw if isinstance(gw, list) else [gw])
                    eng.dma_start(out=wv_sb[:].rearrange("p (k j) -> p k j", k=16),
                                  in_=dr_wv[vg]).then_inc(s_wv.h, 16)
                s_wv.n += 16
                wv_state["n"] += 1
                return s_wv.n, nl

            def load_wout(qset, dg):
                nl = wout_state["n"]
                slot = nl % 2
                gw = wout_state["free"].get(nl - 2)

                @emit("G")
                def _(eng, qset=qset, dg=dg, slot=slot, gw=gw):
                    if gw is not None:
                        wait_pairs(eng, gw if isinstance(gw, list) else [gw])
                    eng.dma_start(out=wout_sb[slot][:].rearrange("p (k j) -> p k j", k=4),
                                  in_=dr_wout[qset, dg]).then_inc(s_wout[slot].h, 16)
                s_wout[slot].n += 16
                wout_state["n"] += 1
                return slot, s_wout[slot].n, nl

            negc_done = [None]
            att_tile_evac = {}
            pending_tail = []
            att_qk_done = [None]
            v4_guard = [region_guard]
            qk_guard = [region_guard]
            av_guard = [region_guard]
            rsbc_guard = [None]
            qkb_ctr = [0]

            def emit_scores(jj, jtile, b, ch, qk_ready, first):
                bfw_ = bw(b)

                @emit("T")
                def _(eng, jj=jj, jtile=jtile, b=b, ch=ch, bfw_=bfw_,
                      qk_ready=qk_ready, first=first, mv=mask_val):
                    if jj == 0:
                        eng.wait_ge(s_v.h, qk_ready[0])
                        eng.wait_ge(s_s.h, qk_ready[1])
                        if first:
                            eng.wait_ge(s_mask.h, mv)
                            eng.wait_ge(s_stpG.h, setupG_val)
                    wait_pairs(eng, [bfw_])
                    eng.matmul(pb[b][:], k_sb[:, jtile * 128:(jtile + 1) * 128],
                               q_sb[:, ch * 512:(ch + 1) * 512],
                               start=True, stop=False, skip_group_check=True)
                    if ch == 1 and jj < 4:
                        msk = mpre[:, jj * 512:(jj + 1) * 512]
                    else:
                        msk = masks[:, jj * 512:(jj + 1) * 512]
                    eng.matmul(pb[b][:], ident_bf[:], msk, start=False, stop=True,
                               skip_group_check=True).then_inc(s_mm.h, 1)
                s_mm.n += 1
                bank_fill[b] = s_mm.n
                return s_mm.n

            def emit_exp_avsum(jj, jtile, b, hh, e_fill, av_bank, sum_bank, sc_fill):
                eslot = jj % 2
                e_prev = e_fill.get(jj - 2)
                ndv = negc_done[0]

                @emit("S")
                def _(eng, jj=jj, b=b, eslot=eslot, sc_fill=sc_fill, e_prev=e_prev,
                      ndv=ndv):
                    eng.wait_ge(s_mm.h, sc_fill)
                    eng.wait_ge(s_v.h, ndv)
                    if e_prev is not None:
                        eng.wait_ge(s_mm.h, e_prev)
                    eng.activation(E_sb[eslot][:], pb[b][:], AF.Exp,
                                   bias=neg_c[:, 0:1]).then_inc(s_s.h, 1)
                s_s.n += 1
                bank_free[b] = (s_s, s_s.n)
                s_exp = s_s.n

                abw = bw(av_bank) if jj == 0 else None
                sbw = bw(sum_bank) if jj == 0 else None

                @emit("T")
                def _(eng, hh=hh, jj=jj, jtile=jtile, eslot=eslot, s_exp=s_exp,
                      abw=abw, sbw=sbw, av_bank=av_bank, sum_bank=sum_bank):
                    eng.wait_ge(s_s.h, s_exp)
                    if jj == 0:
                        wait_pairs(eng, [abw, sbw])
                    eng.matmul(pb[av_bank][:],
                               v4[:, jtile * 256 + hh * 128:jtile * 256 + hh * 128 + 128],
                               E_sb[eslot][:], start=(jj == 0), stop=(jj == 7),
                               skip_group_check=True)
                    eng.matmul(pb[sum_bank][0:1, :], ones_col_bf[:], E_sb[eslot][:],
                               start=(jj == 0), stop=(jj == 7),
                               skip_group_check=True).then_inc(s_mm.h, 1)
                s_mm.n += 1
                e_fill[jj] = s_mm.n
                if jj == 7:
                    bank_fill[av_bank] = s_mm.n
                    bank_fill[sum_bank] = s_mm.n

            for qset in range(4):
                av_writes = []
                for vgi in range(2):
                    vg = qset * 2 + vgi
                    wv_val, wv_idx = load_wv(vg)
                    v4g = v4_guard[0]
                    for tt in range(NT3):
                        b = tt % 2

                        @emit("T")
                        def _(eng, vg=vg, tt=tt, b=b, wv_val=wv_val, bfw=bw(b),
                              thr=tile_thr[tt]):
                            if tt == 0:
                                eng.wait_ge(s_wv.h, wv_val)
                            if vg == 0:
                                eng.wait_ge(s_tp[thr[0]].h, thr[1])
                            wait_pairs(eng, [bfw])
                            mm = None
                            for kt in range(16):
                                mm = eng.matmul(
                                    pb[b][:, 0:256],
                                    XNT[:, kt * TOKW + tt * 128:kt * TOKW + tt * 128 + 128],
                                    wv_sb[:, kt * 256:(kt + 1) * 256],
                                    start=(kt == 0), stop=(kt == 15),
                                    skip_group_check=True)
                            mm.then_inc(s_mm.h, 1)
                        s_mm.n += 1
                        bank_fill[b] = s_mm.n
                        last_xnt_reader[0] = s_mm.n

                        @emit("V")
                        def _(eng, tt=tt, b=b, fill=bank_fill[b], v4g=v4g):
                            eng.wait_ge(s_mm.h, fill)
                            if tt == 0:
                                wait_pairs(eng, v4g if isinstance(v4g, list) else [v4g])
                            eng.tensor_copy(v4[:, tt * 256:(tt + 1) * 256],
                                            pb[b][:, 0:256]).then_inc(s_v.h, 1)
                        s_v.n += 1
                        bank_free[b] = (s_v, s_v.n)
                    wv_state["free"][wv_idx] = (s_mm, s_mm.n)

                    for hh2 in range(2):
                        h = vg * 2 + hh2
                        wqk_slot, wqk_val, wqk_idx = load_wqk(h)
                        if not masks_state[0]:
                            emit_masks_load()
                        qkg = qk_guard[0]
                        for which, ngr, base in ((0, 2, 512), (1, 3, 0)):
                            for g in range(ngr):
                                b = 2 + (qkb_ctr[0] % 2)
                                qkb_ctr[0] += 1

                                tls = [tile_thr[t] for t in
                                       range((base + g * 512) // 128,
                                             (base + g * 512) // 128 + 4)]
                                gthr = [(sl, max(v for s2, v in tls if s2 == sl))
                                        for sl in {s2 for s2, _ in tls}]

                                @emit("T")
                                def _(eng, which=which, g=g, b=b, base=base,
                                      wqk_slot=wqk_slot, wqk_val=wqk_val, bfw=bw(b),
                                      gthr=gthr, h=h):
                                    if which == 0 and g == 0:
                                        eng.wait_ge(s_wqk[wqk_slot].h, wqk_val)
                                    if h == 0:
                                        for sl, v in gthr:
                                            eng.wait_ge(s_tp[sl].h, v)
                                    wait_pairs(eng, [bfw])
                                    woff = which * 2048
                                    mm = None
                                    for kt in range(16):
                                        mm = eng.matmul(
                                            pb[b][:],
                                            wqk_sb[wqk_slot][:, woff + kt * 128:
                                                             woff + kt * 128 + 128],
                                            XNT[:, kt * TOKW + base + g * 512:
                                                kt * TOKW + base + g * 512 + 512],
                                            start=(kt == 0), stop=(kt == 15),
                                            skip_group_check=True)
                                    mm.then_inc(s_mm.h, 1)
                                s_mm.n += 1
                                bank_fill[b] = s_mm.n
                                last_xnt_reader[0] = s_mm.n

                                if which == 0:
                                    @emit("V")
                                    def _(eng, g=g, b=b, fill=bank_fill[b], qkg=qkg):
                                        eng.wait_ge(s_mm.h, fill)
                                        if g == 0:
                                            wait_pairs(eng, qkg if isinstance(qkg, list) else [qkg])
                                        eng.tensor_scalar(
                                            q_sb[:, g * 512:(g + 1) * 512], pb[b][:],
                                            QSCALE, None, ALU.mult).then_inc(s_v.h, 1)
                                    s_v.n += 1
                                    bank_free[b] = (s_v, s_v.n)
                                else:
                                    @emit("S")
                                    def _(eng, g=g, b=b, fill=bank_fill[b], qkg=qkg):
                                        eng.wait_ge(s_mm.h, fill)
                                        if g == 0:
                                            wait_pairs(eng, qkg if isinstance(qkg, list) else [qkg])
                                        eng.copy(k_sb[:, g * 512:(g + 1) * 512],
                                                 pb[b][:]).then_inc(s_s.h, 1)
                                    s_s.n += 1
                                    bank_free[b] = (s_s, s_s.n)
                        qk_ready = (s_v.n, s_s.n)
                        wqk_state["free"][wqk_idx] = (s_mm, s_mm.n)

                        for ch in range(2):
                            jt_tiles = list(range(0, 8)) if ch == 0 else list(range(4, 12))
                            first = (h == 0 and ch == 0)
                            av_bank, sum_bank = 4 + ((h * 2 + ch) % 2), 6
                            e_fill = {}

                            if first:
                                first_banks = [2, 3, 7, 1]
                                vmasks = []
                                for jj in range(4):
                                    b = first_banks[jj]
                                    vmasks.append(emit_scores(jj, jt_tiles[jj], b, ch,
                                                              qk_ready, first))

                                    @emit("V")
                                    def _(eng, jj=jj, b=b, vm=vmasks[-1]):
                                        eng.wait_ge(s_mm.h, vm)
                                        eng.reduce_max(cchain[:, jj:jj + 1], pb[b][:],
                                                       AXX).then_inc(s_v.h, 1)
                                    s_v.n += 1
                                v_rm = s_v.n

                                @emit("V")
                                def _(eng, v_rm=v_rm):
                                    eng.wait_ge(s_v.h, v_rm)
                                    eng.reduce_max(cchain[:, 4:5], cchain[:, 0:4],
                                                   AXX).then_inc(s_v.h, 1)
                                s_v.n += 1
                                v_m8 = s_v.n

                                @emit("T")
                                def _(eng, v_m8=v_m8, bfw=bw(0)):
                                    eng.wait_ge(s_v.h, v_m8)
                                    wait_pairs(eng, [bfw])
                                    eng.transpose(pb[0][0:1, 0:128], cchain[:, 4:5],
                                                  ident_sb[:]).then_inc(s_mm.h, 1)
                                s_mm.n += 1
                                bank_fill[0] = s_mm.n
                                f0 = s_mm.n

                                @emit("V")
                                def _(eng, f0=f0):
                                    eng.wait_ge(s_mm.h, f0)
                                    eng.reduce_max(cmax[:], pb[0][0:1, 0:128],
                                                   AXX).then_inc(s_v.h, 1)
                                s_v.n += 1
                                bank_free[0] = (s_v, s_v.n)
                                v_cm = s_v.n

                                @emit("T")
                                def _(eng, v_cm=v_cm, bfw=bw(0)):
                                    eng.wait_ge(s_v.h, v_cm)
                                    wait_pairs(eng, [bfw])
                                    eng.matmul(pb[0][:, 0:1], ones_row_f[:], cmax[:],
                                               start=True, stop=True,
                                               skip_group_check=True).then_inc(s_mm.h, 1)
                                s_mm.n += 1
                                bank_fill[0] = s_mm.n
                                f0b = s_mm.n

                                @emit("V")
                                def _(eng, f0b=f0b):
                                    eng.wait_ge(s_mm.h, f0b)
                                    eng.tensor_scalar(neg_c[:], pb[0][:, 0:1], -1.0,
                                                      None, ALU.mult).then_inc(s_v.h, 1)
                                s_v.n += 1
                                bank_free[0] = (s_v, s_v.n)
                                negc_done[0] = s_v.n

                                while pending_tail:
                                    pending_tail.pop(0)()
                                for jj in range(4):
                                    emit_exp_avsum(jj, jt_tiles[jj], first_banks[jj],
                                                   hh2, e_fill, av_bank, sum_bank,
                                                   vmasks[jj])
                                for jj in range(4, 8):
                                    b = 2 + (jj % 2)
                                    vm = emit_scores(jj, jt_tiles[jj], b, ch, qk_ready,
                                                     first)
                                    emit_exp_avsum(jj, jt_tiles[jj], b, hh2, e_fill,
                                                   av_bank, sum_bank, vm)
                                negc_ready_first = negc_done[0]
                            else:
                                sc_banks = [2, 3, 7]
                                fills_jj = {}
                                for jj in range(3):
                                    fills_jj[jj] = emit_scores(jj, jt_tiles[jj],
                                                               sc_banks[jj % 3], ch,
                                                               qk_ready, first)
                                while pending_tail:
                                    pending_tail.pop(0)()
                                for jj in range(8):
                                    emit_exp_avsum(jj, jt_tiles[jj], sc_banks[jj % 3],
                                                   hh2, e_fill, av_bank, sum_bank,
                                                   fills_jj[jj])
                                    if jj + 3 < 8:
                                        fills_jj[jj + 3] = emit_scores(
                                            jj + 3, jt_tiles[jj + 3],
                                            sc_banks[(jj + 3) % 3], ch, qk_ready, first)

                            avsum_fill = s_mm.n

                            def emit_tail(h=h, ch=ch, av_bank=av_bank,
                                          fill=avsum_fill, ag=av_guard[0],
                                          rg=rsbc_guard[0]):
                                @emit("V")
                                def _(eng, fill=fill):
                                    eng.wait_ge(s_mm.h, fill)
                                    eng.reciprocal(rs_sb[:],
                                                   pb[6][0:1, :]).then_inc(s_v.h, 1)
                                s_v.n += 1
                                bank_free[6] = (s_v, s_v.n)
                                v_rs = s_v.n

                                @emit("T")
                                def _(eng, v_rs=v_rs, bfw=bw(6)):
                                    eng.wait_ge(s_v.h, v_rs)
                                    wait_pairs(eng, [bfw])
                                    eng.matmul(pb[6][:], ones_row_f[:], rs_sb[:],
                                               start=True, stop=True,
                                               skip_group_check=True).then_inc(s_mm.h, 1)
                                s_mm.n += 1
                                bank_fill[6] = s_mm.n
                                f_bc = s_mm.n

                                @emit("S")
                                def _(eng, f_bc=f_bc, rg=rg):
                                    eng.wait_ge(s_mm.h, f_bc)
                                    if rg is not None:
                                        eng.wait_ge(s_v.h, rg)
                                    eng.copy(rsbc_sb[:], pb[6][:]).then_inc(s_s.h, 1)
                                s_s.n += 1
                                bank_free[6] = (s_s, s_s.n)
                                s_bc = s_s.n

                                @emit("V")
                                def _(eng, h=h, ch=ch, s_bc=s_bc, ag=ag,
                                      av_bank=av_bank):
                                    eng.wait_ge(s_s.h, s_bc)
                                    wait_pairs(eng, ag if isinstance(ag, list) else [ag])
                                    hl = h % 4
                                    eng.tensor_tensor(
                                        av_sb[:, hl * 1024 + ch * 512:hl * 1024 + ch * 512 + 512],
                                        pb[av_bank][:], rsbc_sb[:],
                                        ALU.mult).then_inc(s_v.h, 1)
                                s_v.n += 1
                                bank_free[av_bank] = (s_v, s_v.n)
                                av_writes.append(s_v.n)
                                rsbc_guard[0] = s_v.n

                            pending_tail.append(emit_tail)
                        qk_guard[0] = (s_mm, s_mm.n)
                        if h == 15:
                            att_qk_done[0] = (s_mm, s_mm.n)
                    v4_guard[0] = (s_mm, s_mm.n)

                while pending_tail:
                    pending_tail.pop(0)()
                av_ready = max(av_writes)
                for dg in range(4):
                    wslot, wval, widx = load_wout(qset, dg)
                    for j, tt in enumerate(range(4, 12)):
                        b = 6 + (j % 2)

                        @emit("T")
                        def _(eng, dg=dg, tt=tt, b=b, j=j, wslot=wslot, wval=wval,
                              av_ready=av_ready, bfw=bw(b)):
                            if j == 0:
                                eng.wait_ge(s_wout[wslot].h, wval)
                                eng.wait_ge(s_v.h, av_ready)
                            wait_pairs(eng, [bfw])
                            ch2 = (tt - 4) // 4
                            pos = ((tt - 4) % 4) * 128
                            mm = None
                            for ekt in range(4):
                                mm = eng.matmul(
                                    pb[b][:],
                                    av_sb[:, ekt * 1024 + ch2 * 512 + pos:
                                          ekt * 1024 + ch2 * 512 + pos + 128],
                                    wout_sb[wslot][:, ekt * 512:(ekt + 1) * 512],
                                    start=(ekt == 0), stop=(ekt == 3),
                                    skip_group_check=True)
                            mm.then_inc(s_mm.h, 1)
                        s_mm.n += 1
                        bank_fill[b] = s_mm.n

                        @emit("V")
                        def _(eng, tt=tt, b=b, fill=bank_fill[b], dg=dg):
                            eng.wait_ge(s_mm.h, fill)
                            xs = X[:, (tt - 4) * D + dg * 512:(tt - 4) * D + dg * 512 + 512]
                            eng.tensor_tensor(xs, xs, pb[b][:], ALU.add).then_inc(s_v.h, 1)
                        s_v.n += 1
                        bank_free[b] = (s_v, s_v.n)
                        if qset == 3 and dg == 3:
                            att_tile_evac[tt] = s_v.n
                            if tile_hook is not None:
                                tile_hook(tt, s_v.n)
                    wout_state["free"][widx] = (s_mm, s_mm.n)
                av_guard[0] = (s_mm, s_mm.n)

            g = guard_now()
            if stage == "attn":
                _dbg(emit, X, X_halo, dbg, s_v, s_st)
            return g, att_tile_evac, att_qk_done[0]

    def _dbg(emit_, X_, Xh_, dbg_, sv_, sst_):
        rv = sv_.n

        @emit_("Y")
        def _(eng, rv=rv):
            eng.wait_ge(sv_.h, rv)
            eng.dma_start(out=dbg_["own"].rearrange("(t p) d -> p t d", p=128),
                          in_=X_[:].rearrange("p (t d) -> p t d", t=NT_OWN)
                          ).then_inc(sst_.h, 16)
        sst_.n += 16

        @emit_("G")
        def _(eng, rv=rv):
            eng.wait_ge(sv_.h, rv)
            eng.dma_start(out=dbg_["halo"].rearrange("(t p) d -> p t d", p=128),
                          in_=Xh_[:].rearrange("p (t d) -> p t d", t=4)
                          ).then_inc(sst_.h, 16)
        sst_.n += 16

        @emit_("Y")
        def _(eng, tot=sst_.n):
            eng.wait_ge(sst_.h, tot)

    xr, thr1 = emit_ln(0, list(range(12)), [])
    vr = emit_routing(0, list(range(12)), xr, thr1)
    pre1 = (lambda: _dbg(emit, X, X_halo, dbg, s_v, s_st)) if stage == "ffn1" else None
    ln2_hook, ln2_state = make_ln_hook(1, list(range(12)), [], xr)
    g_ffn1, te1, g_h7 = emit_ffn(0, list(range(12)), [0, 512, 1024], vr, xr,
                                 guard_now(), pre_drain=pre1,
                                 tile_hook=None if stage == "ffn1" else ln2_hook)
    if stage == "ffn1":
        drain()
        es.close()
        return nc

    xr2, thr2 = emit_ln(1, list(range(12)), [], hook_state=ln2_state)
    ln3_hook, ln3_state = make_ln_hook(2, list(range(4, 12)), [], xr2)
    g_att, te2, g_qk15 = emit_attention(xr2, thr2,
                                        g_ffn1 + [(s_v, s_v.n), (s_s, s_s.n)],
                                        g_h7,
                                        tile_hook=None if stage == "attn" else ln3_hook)
    if stage == "attn":
        drain()
        es.close()
        return nc

    xr3, thr3 = emit_ln(2, list(range(4, 12)), [], hook_state=ln3_state)
    vr3 = emit_routing(2, list(range(4, 12)), xr3, thr3)

    def store_hook(tt, v):
        @emit("Y")
        def _(eng, tt=tt, v=v):
            eng.wait_ge(s_v.h, v)
            eng.dma_start(
                out=dr_y.rearrange("(t p) d -> p t d", p=128)[:, tt - 4:tt - 3, :],
                in_=X[:, (tt - 4) * D:(tt - 3) * D].rearrange("p (t d) -> p t d", t=1)
                ).then_inc(s_st.h, 16)
        s_st.n += 16

    def pre_final():
        @emit("Y")
        def _(eng, tot=s_st.n):
            eng.wait_ge(s_st.h, tot)

    full_att_guard = g_att + [(s_v, s_v.n), (s_s, s_s.n)]
    emit_ffn(2, list(range(4, 12)), [512, 1024], vr3, xr3,
             {"wi0": [g_qk15], "wi1": full_att_guard, "wo": full_att_guard,
              "inter": full_att_guard}, pre_drain=pre_final, tile_hook=store_hook)
    drain()
    es.close()
    return nc


def host_inputs(L0_attn_x, L0_attn_w_qkv, L0_attn_w_out, L0_ffn_w_gate,
                L0_ffn_wi_0, L0_ffn_wi_1, L0_ffn_wo):
    x = np.ascontiguousarray(np.asarray(L0_attn_x, dtype=np.float32).reshape(8192, D))
    wqkT = np.ascontiguousarray(np.asarray(L0_attn_w_qkv, dtype=np.float32).T)
    woutT = np.ascontiguousarray(np.asarray(L0_attn_w_out, dtype=np.float32).T)
    wg = np.asarray(L0_ffn_w_gate, dtype=np.float32)
    wi0 = np.asarray(L0_ffn_wi_0, dtype=np.float32)
    wi1 = np.asarray(L0_ffn_wi_1, dtype=np.float32)
    wo = np.asarray(L0_ffn_wo, dtype=np.float32)

    wg_t = np.ascontiguousarray(wg.reshape(16, 128, 8).transpose(1, 0, 2))
    wi0_t = np.ascontiguousarray(wi0.reshape(E, 16, 128, 4, 256).transpose(0, 3, 2, 1, 4))
    wi1_t = np.ascontiguousarray(wi1.reshape(E, 16, 128, 4, 256).transpose(0, 3, 2, 1, 4))
    wo_t = np.ascontiguousarray(wo.reshape(E, 2, 4, 128, 2048).transpose(0, 1, 3, 2, 4))
    wqk_t = np.ascontiguousarray(
        wqkT[:, :4096].reshape(16, 128, 32, 128).transpose(2, 1, 0, 3))
    wv_t = np.ascontiguousarray(
        wqkT[:, 4096:].reshape(16, 128, 8, 256).transpose(2, 1, 0, 3))
    wout_t = np.ascontiguousarray(
        woutT.reshape(4, 4, 128, 4, 512).transpose(0, 3, 2, 1, 4))

    p = np.arange(128)[:, None]
    i = np.arange(512)[None, :]
    band = np.zeros((128, 8, 512), dtype=np.float32)
    for jt in range(8):
        j = 128 * jt + p
        ok = (i >= j - 512) & (i <= j)
        band[:, jt, :] = np.where(ok, 0.0, np.float32(-1e9))
    band = band.astype(BF)
    kill = np.full((128, 4, 512), -1e9, dtype=np.float32).astype(BF)
    ident = np.eye(128, dtype=np.float32)

    shared = dict(wg=wg_t, wi0=wi0_t, wi1=wi1_t, wo=wo_t, wqk=wqk_t, wv=wv_t,
                  wout=wout_t, band=band, ident=ident)
    in_maps = []
    for c in range(8):
        g0, g1, g2 = 2 * c, 2 * c + 1, (2 * c + 2) % 16
        x_halo = x[g0 * CH:(g0 + 1) * CH]
        x_own = np.concatenate([x[g1 * CH:(g1 + 1) * CH], x[g2 * CH:(g2 + 1) * CH]])
        q2_first = (g2 % 4 == 0)
        mpre_c = kill if q2_first else np.ascontiguousarray(band[:, 0:4])
        m = dict(shared)
        m["x_own"] = np.ascontiguousarray(x_own)
        m["x_halo"] = np.ascontiguousarray(x_halo)
        m["mpre"] = np.ascontiguousarray(mpre_c)
        in_maps.append(m)
    return in_maps


def assemble(results):
    y = np.empty((8192, D), dtype=np.float32)
    for c in range(8):
        g1, g2 = 2 * c + 1, (2 * c + 2) % 16
        yc = results[c]["y"]
        y[g1 * CH:(g1 + 1) * CH] = yc[0:CH]
        y[g2 * CH:(g2 + 1) * CH] = yc[CH:2 * CH]
    return y.reshape(4, 2048, D)


def kernel(L0_attn_x, L0_attn_w_qkv, L0_attn_w_out, L0_ffn_w_gate,
           L0_ffn_wi_0, L0_ffn_wi_1, L0_ffn_wo):
    in_maps = host_inputs(L0_attn_x, L0_attn_w_qkv, L0_attn_w_out, L0_ffn_w_gate,
                          L0_ffn_wi_0, L0_ffn_wi_1, L0_ffn_wo)
    nc = build_program("full")
    res = run_bass_kernel_spmd(nc, in_maps, core_ids=list(range(8)))
    return assemble(res.results)


# version 13
# speedup vs baseline: 1.1601x; 1.0034x over previous
import sys
sys.path.insert(0, "/opt/trn_rl_repo")
import contextlib
import numpy as np
import ml_dtypes
import concourse.bass as bass
import concourse.mybir as mybir
from concourse.bass_utils import run_bass_kernel_spmd

F32 = mybir.dt.float32
BF16 = mybir.dt.bfloat16
AF = mybir.ActivationFunctionType
ALU = mybir.AluOpType
AXX = mybir.AxisListType.X
BF = ml_dtypes.bfloat16

D = 2048
E = 8
CH = 512
NT3 = 12
NT_OWN = 8
TOKW = 1536
QSCALE = float(1.0 / np.float64(np.float32(np.sqrt(np.float64(128.0)))))
EPS = 1e-5


class Ctr:
    def __init__(self, h):
        self.h = h
        self.n = 0


def build_program(stage="full"):
    nc = bass.Bass("TRN2", target_bir_lowering=False)

    dr_x_own = nc.dram_tensor("x_own", [1024, D], F32, kind="ExternalInput").ap()
    dr_x_halo = nc.dram_tensor("x_halo", [512, D], F32, kind="ExternalInput").ap()
    dr_wg = nc.dram_tensor("wg", [128, 16, 8], F32, kind="ExternalInput").ap()
    dr_wi0 = nc.dram_tensor("wi0", [E, 4, 128, 16, 256], F32, kind="ExternalInput").ap()
    dr_wi1 = nc.dram_tensor("wi1", [E, 4, 128, 16, 256], F32, kind="ExternalInput").ap()
    dr_wo = nc.dram_tensor("wo", [E, 2, 128, 4, 2048], F32, kind="ExternalInput").ap()
    dr_wqk = nc.dram_tensor("wqk", [32, 128, 16, 128], F32, kind="ExternalInput").ap()
    dr_wv = nc.dram_tensor("wv", [8, 128, 16, 256], F32, kind="ExternalInput").ap()
    dr_wout = nc.dram_tensor("wout", [4, 4, 128, 4, 512], F32, kind="ExternalInput").ap()
    dr_band = nc.dram_tensor("band", [128, 8, 512], BF16, kind="ExternalInput").ap()
    dr_mpre = nc.dram_tensor("mpre", [128, 4, 512], BF16, kind="ExternalInput").ap()
    dr_ident = nc.dram_tensor("ident", [128, 128], F32, kind="ExternalInput").ap()
    dr_y = nc.dram_tensor("y", [1024, D], F32, kind="ExternalOutput").ap()
    dbg = {}
    if stage != "full":
        dbg["own"] = nc.dram_tensor("dbg_own", [1024, D], F32, kind="ExternalOutput").ap()
        dbg["halo"] = nc.dram_tensor("dbg_halo", [512, D], F32, kind="ExternalOutput").ap()

    es = contextlib.ExitStack()
    sbuf = lambda n, s, d: es.enter_context(nc.sbuf_tensor(n, s, d))

    X = sbuf("X", [128, NT_OWN * D], F32)
    XNT = sbuf("XNT", [128, 16 * TOKW], BF16)
    X_halo = sbuf("X_halo", [128, 4 * D], BF16)
    xn_tmp = [sbuf(f"xn_tmp{i}", [128, D], BF16) for i in range(2)]
    wg_sb = sbuf("wg_sb", [128, 16 * 8], BF16)
    what_sb = sbuf("what_sb", [128, NT3 * 8], F32)
    stats = sbuf("stats", [128, 64], F32)
    lg_all = sbuf("lg_all", [128, NT3 * 8], F32)
    top_all = sbuf("top_all", [128, NT3 * 8], F32)
    r_scr = sbuf("r_scr", [128, 3 * 16], F32)
    m_scr = sbuf("m_scr", [128, 16], F32)
    ident_sb = sbuf("ident_sb", [128, 128], F32)
    ident_bf = sbuf("ident_bf", [128, 128], BF16)
    ones_col_bf = sbuf("ones_col_bf", [128, 1], BF16)
    ones_row_f = sbuf("ones_row_f", [1, 128], F32)
    neg_c = sbuf("neg_c", [128, 1], F32)
    cchain = sbuf("cchain", [128, 8], F32)
    cmax = sbuf("cmax", [1, 1], F32)
    rs_sb = sbuf("rs_sb", [1, 512], F32)
    rsbc_sb = sbuf("rsbc_sb", [128, 512], BF16)

    pb = [es.enter_context(nc.psum_tensor(f"pb{i}", [128, 512], F32)) for i in range(8)]

    sem = lambda n: Ctr(es.enter_context(nc.semaphore(n)))
    s_mm = sem("s_mm")
    s_v = sem("s_v")
    s_s = sem("s_s")
    s_g = sem("s_g")
    s_stpY = sem("s_stpY")
    s_stpG = sem("s_stpG")
    s_tp = [sem("s_tpA"), sem("s_tpB")]
    s_wi = [sem("s_wi0"), sem("s_wi1")]
    s_wo = [sem("s_wo0"), sem("s_wo1")]
    s_wqk = [sem("s_wqk0"), sem("s_wqk1")]
    s_wv = sem("s_wv")
    s_wout = [sem("s_wout0"), sem("s_wout1")]
    s_mask = sem("s_mask")
    s_halo = sem("s_halo")
    s_st = sem("s_st")

    ev = {k: [] for k in ("T", "V", "S", "G", "Y")}

    def emit(which):
        def deco(f):
            ev[which].append(f)
        return deco

    def drain():
        with nc.Block() as block:
            @block.tensor
            def _(eng):
                for f in ev["T"]:
                    f(eng)

            @block.vector
            def _(eng):
                for f in ev["V"]:
                    f(eng)

            @block.scalar
            def _(eng):
                for f in ev["S"]:
                    f(eng)

            @block.gpsimd
            def _(eng):
                for f in ev["G"]:
                    f(eng)

            @block.sync
            def _(eng):
                for f in ev["Y"]:
                    f(eng)
        for k in ev:
            ev[k].clear()

    bank_free = [None] * 8
    bank_fill = [0] * 8

    def bw(b):
        return bank_free[b]

    def wait_pairs(eng, pairs):
        for p in pairs:
            if p is not None:
                eng.wait_ge(p[0].h, p[1])

    def guard_now():
        return [(s_mm, s_mm.n), (s_v, s_v.n), (s_s, s_s.n)]

    @emit("Y")
    def _(eng):
        eng.dma_start(out=X[:].rearrange("p (t d) -> p t d", t=NT_OWN),
                      in_=dr_x_own.rearrange("(t p) d -> p t d", p=128)).then_inc(s_stpY.h, 16)
        eng.dma_start(out=ident_sb[:], in_=dr_ident).then_inc(s_stpY.h, 16)
    s_stpY.n += 32

    @emit("G")
    def _(eng):
        eng.dma_start(out=X_halo[:].rearrange("p (t d) -> p t d", t=4),
                      in_=dr_x_halo.rearrange("(t p) d -> p t d", p=128)).then_inc(s_halo.h, 16)
        eng.memset(ones_col_bf[:], 1.0)
        eng.memset(ones_row_f[:], 1.0).then_inc(s_g.h, 1)
        eng.dma_start(out=wg_sb[:].rearrange("p (k j) -> p k j", k=16),
                      in_=dr_wg).then_inc(s_stpG.h, 16)
        eng.dma_start(out=ident_bf[:], in_=dr_ident).then_inc(s_stpG.h, 16)
    s_g.n += 1
    s_halo.n += 16
    s_stpG.n += 32
    setupY_val = s_stpY.n
    setupG_val = s_stpG.n

    xnt_uses = [0, 0]
    last_xnt_reader = [0]

    def x_tile(tt):
        if tt < 4:
            return X_halo[:, tt * D:(tt + 1) * D]
        return X[:, (tt - 4) * D:(tt - 3) * D]

    def emit_ln_tile_stats(call, i, tt, dep, td, st):
        @emit("V")
        def _(eng, i=i, tt=tt, dep=list(dep), call=call, td=td):
            if call == 0 and tt == 0:
                eng.wait_ge(s_halo.h, 16)
            if call == 0 and tt == 4:
                eng.wait_ge(s_stpY.h, setupY_val)
            if td is not None:
                eng.wait_ge(s_v.h, td)
            elif i == 0:
                wait_pairs(eng, dep)
            eng.tensor_reduce(stats[:, i:i + 1], x_tile(tt), AXX,
                              ALU.add).then_inc(s_v.h, 1)
        s_v.n += 1
        st["red"][i] = s_v.n

        @emit("S")
        def _(eng, i=i, tt=tt, dep=list(dep), call=call, td=td,
              w=st["sq"].get(i - 2, 0), w_tp=16 * xnt_uses[i % 2]):
            if call == 0 and tt == 0:
                eng.wait_ge(s_halo.h, 16)
            if call == 0 and tt == 4:
                eng.wait_ge(s_stpY.h, setupY_val)
            if i == 0:
                wait_pairs(eng, dep)
            if w_tp > 0:
                eng.wait_ge(s_tp[i % 2].h, w_tp)
            if td is not None:
                eng.wait_ge(s_v.h, td)
            if w > 0:
                eng.wait_ge(s_s.h, w)
            eng.activation(xn_tmp[i % 2][:], x_tile(tt), AF.Square,
                           accum_out=stats[:, 12 + i:13 + i]).then_inc(s_s.h, 1)
        s_s.n += 1
        st["sq"][i] = s_s.n

    def emit_ln_batch(call, st, h0, h1):
        vw = max(st["red"][j] for j in range(h0, h1))
        sw = max(st["sq"][j] for j in range(h0, h1))

        @emit("V")
        def _(eng, h0=h0, h1=h1, vw=vw):
            eng.wait_ge(s_v.h, vw)
            eng.tensor_scalar(stats[:, 24 + h0:24 + h1], stats[:, h0:h1], 1.0 / D,
                              None, ALU.mult).then_inc(s_v.h, 1)
        s_v.n += 1
        v_mu = s_v.n

        @emit("V")
        def _(eng, h0=h0, h1=h1, v_mu=v_mu):
            eng.wait_ge(s_v.h, v_mu)
            eng.tensor_tensor(stats[:, 48 + h0:48 + h1], stats[:, 24 + h0:24 + h1],
                              stats[:, 24 + h0:24 + h1], ALU.mult).then_inc(s_v.h, 1)
        s_v.n += 1
        v_musq = s_v.n

        @emit("V")
        def _(eng, h0=h0, h1=h1, v_musq=v_musq, sw=sw):
            eng.wait_ge(s_v.h, v_musq)
            eng.wait_ge(s_s.h, sw)
            eng.scalar_tensor_tensor(stats[:, 48 + h0:48 + h1],
                                     stats[:, 12 + h0:12 + h1], 1.0 / D,
                                     stats[:, 48 + h0:48 + h1], ALU.mult,
                                     ALU.subtract).then_inc(s_v.h, 1)
        s_v.n += 1
        v_var = s_v.n

        @emit("V")
        def _(eng, h0=h0, h1=h1, v_var=v_var):
            eng.wait_ge(s_v.h, v_var)
            eng.tensor_scalar(stats[:, 48 + h0:48 + h1], stats[:, 48 + h0:48 + h1],
                              EPS, None, ALU.add).then_inc(s_v.h, 1)
        s_v.n += 1
        v_vpe = s_v.n

        @emit("S")
        def _(eng, h0=h0, h1=h1, v_vpe=v_vpe):
            eng.wait_ge(s_v.h, v_vpe)
            eng.activation(stats[:, 36 + h0:36 + h1], stats[:, 48 + h0:48 + h1],
                           AF.Sqrt).then_inc(s_s.h, 1)
        s_s.n += 1
        s_sqrt = s_s.n

        @emit("V")
        def _(eng, h0=h0, h1=h1, s_sqrt=s_sqrt):
            eng.wait_ge(s_s.h, s_sqrt)
            eng.reciprocal(stats[:, 36 + h0:36 + h1],
                           stats[:, 36 + h0:36 + h1]).then_inc(s_v.h, 1)
        s_v.n += 1
        rstd_v = s_v.n
        for j in range(h0, h1):
            _emit_ln_norm(call, j, st["tts"][j], st, rstd_v)

    def _emit_ln_norm(call, i, tt, st, rstd_v):
        slot = i % 2
        tp_wait = 16 * xnt_uses[slot]
        xnt_uses[slot] += 1

        @emit("V")
        def _(eng, i=i, tt=tt, slot=slot, tp_wait=tp_wait, rstd_v=rstd_v):
            eng.wait_ge(s_v.h, rstd_v)
            if tp_wait:
                eng.wait_ge(s_tp[slot].h, tp_wait)
            eng.tensor_scalar(xn_tmp[slot][:], x_tile(tt), stats[:, 24 + i:25 + i],
                              stats[:, 36 + i:37 + i], ALU.subtract,
                              ALU.mult).then_inc(s_v.h, 1)
        s_v.n += 1
        v_norm = s_v.n

        @emit("Y" if slot == 0 else "S")
        def _(eng, i=i, tt=tt, slot=slot, v_norm=v_norm, xg=st["xg"]):
            eng.wait_ge(s_v.h, v_norm)
            if i <= 1:
                eng.wait_ge(s_mm.h, xg)
            eng.dma_start(
                out=XNT[:].rearrange("p (k j) -> p k j", k=16)[:, :, tt * 128:(tt + 1) * 128],
                in_=xn_tmp[slot][:].rearrange("p (k j) -> p k j", k=16),
                transpose=True).then_inc(s_tp[slot].h, 16)
        s_tp[slot].n += 16
        st["thr"][tt] = (slot, s_tp[slot].n)

    def _new_ln_state(tts, tp_prev):
        return {"sq": {}, "red": {}, "thr": {}, "done": [],
                "tts": list(tts), "tp_prev": tp_prev, "xg": last_xnt_reader[0]}

    def make_ln_hook(call, tts, dep, tp_prev):
        st = _new_ln_state(tts, tp_prev)
        n = len(tts)

        q = n // 4

        def hook(tt, td):
            if not st["done"]:
                st["xg"] = last_xnt_reader[0]
            i = tts.index(tt)
            emit_ln_tile_stats(call, i, tt, dep, td, st)
            st["done"].append(tt)
            if (i + 1) % q == 0:
                emit_ln_batch(call, st, i + 1 - q, i + 1)
        return hook, st

    def emit_ln(call, tts, dep, dep_tile=None, hook_state=None, tp_prev=None):
        n = len(tts)
        if hook_state is None:
            st = _new_ln_state(tts, tp_prev)
            st["xg"] = last_xnt_reader[0]
            q = n // 4
            for i, tt in enumerate(tts):
                td = dep_tile.get(tt) if dep_tile else None
                emit_ln_tile_stats(call, i, tt, dep, td, st)
                if (i + 1) % q == 0:
                    emit_ln_batch(call, st, i + 1 - q, i + 1)
        else:
            st = hook_state
            assert len(st["done"]) == len(tts), st["done"]
        return (s_tp[0].n, s_tp[1].n), st["thr"]

    def emit_routing(call, tts, xnt_ready, tile_thr):
        n = len(tts)
        for i, tt in enumerate(tts):
            b = 6 + (i % 2)

            @emit("T")
            def _(eng, i=i, tt=tt, b=b, bfw=bw(b), call=call, thr=tile_thr[tt]):
                eng.wait_ge(s_tp[thr[0]].h, thr[1])
                if i == 0 and call == 0:
                    eng.wait_ge(s_stpG.h, setupG_val)
                wait_pairs(eng, [bfw])
                mm = None
                for kt in range(16):
                    mm = eng.matmul(pb[b][:, 0:8],
                                    XNT[:, kt * TOKW + tt * 128:kt * TOKW + tt * 128 + 128],
                                    wg_sb[:, kt * 8:(kt + 1) * 8],
                                    start=(kt == 0), stop=(kt == 15),
                                    skip_group_check=True)
                mm.then_inc(s_mm.h, 1)
            s_mm.n += 1
            bank_fill[b] = s_mm.n

            @emit("S")
            def _(eng, tt=tt, b=b, fill=bank_fill[b]):
                eng.wait_ge(s_mm.h, fill)
                eng.copy(lg_all[:, tt * 8:(tt + 1) * 8], pb[b][:, 0:8]).then_inc(s_s.h, 1)
            s_s.n += 1
            bank_free[b] = (s_s, s_s.n)
            s_lg = s_s.n

            @emit("V")
            def _(eng, tt=tt, s_lg=s_lg):
                eng.wait_ge(s_s.h, s_lg)
                eng.max(top_all[:, tt * 8:(tt + 1) * 8],
                        lg_all[:, tt * 8:(tt + 1) * 8]).then_inc(s_v.h, 1)
            s_v.n += 1
            v_max = s_v.n

            @emit("V")
            def _(eng, i=i, tt=tt, v_max=v_max):
                eng.wait_ge(s_v.h, v_max)
                eng.tensor_tensor(r_scr[:, i:i + 1], top_all[:, tt * 8:tt * 8 + 1],
                                  top_all[:, tt * 8 + 1:tt * 8 + 2],
                                  ALU.subtract).then_inc(s_v.h, 1)
            s_v.n += 1
        v_d12 = s_v.n

        @emit("S")
        def _(eng, v_d12=v_d12, n=n):
            eng.wait_ge(s_v.h, v_d12)
            eng.activation(r_scr[:, 16:16 + n], r_scr[:, 0:n],
                           AF.Sigmoid).then_inc(s_s.h, 1)
        s_s.n += 1
        s_sig = s_s.n

        @emit("V")
        def _(eng, s_sig=s_sig, n=n):
            eng.wait_ge(s_s.h, s_sig)
            eng.tensor_scalar(r_scr[:, 16:16 + n], r_scr[:, 16:16 + n], 0.5, None,
                              ALU.mult).then_inc(s_v.h, 1)
        s_v.n += 1
        v_w1 = s_v.n

        @emit("V")
        def _(eng, v_w1=v_w1, n=n):
            eng.wait_ge(s_v.h, v_w1)
            eng.tensor_scalar(r_scr[:, 32:32 + n], r_scr[:, 16:16 + n], -1.0, 0.5,
                              ALU.mult, ALU.add).then_inc(s_v.h, 1)
        s_v.n += 1
        prev = s_v.n

        for i, tt in enumerate(tts):
            @emit("V")
            def _(eng, i=i, tt=tt, pw=prev):
                eng.wait_ge(s_v.h, pw)
                sl = slice(tt * 8, (tt + 1) * 8)
                eng.tensor_scalar(m_scr[:, 0:8], lg_all[:, sl],
                                  top_all[:, tt * 8:tt * 8 + 1], None, ALU.is_equal)
                eng.tensor_scalar(m_scr[:, 8:16], lg_all[:, sl],
                                  top_all[:, tt * 8 + 1:tt * 8 + 2], None,
                                  ALU.is_equal).then_inc(s_v.h, 1)
            s_v.n += 1
            v_m = s_v.n

            @emit("V")
            def _(eng, i=i, v_m=v_m):
                eng.wait_ge(s_v.h, v_m)
                eng.tensor_scalar(m_scr[:, 8:16], m_scr[:, 8:16],
                                  r_scr[:, 32 + i:33 + i], None,
                                  ALU.mult).then_inc(s_v.h, 1)
            s_v.n += 1
            v_m2 = s_v.n

            @emit("V")
            def _(eng, i=i, tt=tt, v_m2=v_m2):
                eng.wait_ge(s_v.h, v_m2)
                eng.scalar_tensor_tensor(what_sb[:, tt * 8:(tt + 1) * 8],
                                         m_scr[:, 0:8], r_scr[:, 16 + i:17 + i],
                                         m_scr[:, 8:16], ALU.mult,
                                         ALU.add).then_inc(s_v.h, 1)
            s_v.n += 1
            prev = s_v.n
        return s_v.n

    def emit_ffn(call, tts, groups, v_routing, xnt_ready, region_guard, pre_drain=None, tile_hook=None):
        ng = len(groups)
        with contextlib.ExitStack() as fes:
            fsb = lambda nme, s, d: fes.enter_context(
                nc.sbuf_tensor(f"{nme}_c{call}", s, d))
            wi_sb = [fsb(f"wi_sb{i}", [128, 2 * 16 * 256], BF16) for i in range(2)]
            wo_sb = fsb("wo_sb", [128, 4 * 2048], BF16)
            inter = fsb("inter", [128, 4 * TOKW], BF16)

            rg = dict(region_guard) if isinstance(region_guard, dict) else {
                "wi0": region_guard, "wi1": region_guard, "wo": region_guard,
                "inter": region_guard}
            wi_state = {"n": 0, "free": {-2: rg["wi0"], -1: rg["wi1"]}}
            wo_state = {"n": 0, "free": {-1: rg["wo"]}}

            def load_wi(e, q):
                nl = wi_state["n"]
                slot = nl % 2
                gw = wi_state["free"].get(nl - 2)

                @emit("G")
                def _(eng, e=e, q=q, slot=slot, gw=gw):
                    if gw is not None:
                        wait_pairs(eng, gw if isinstance(gw, list) else [gw])
                    eng.dma_start(
                        out=wi_sb[slot][:, 0:4096].rearrange("p (k j) -> p k j", k=16),
                        in_=dr_wi0[e, q]).then_inc(s_wi[slot].h, 16)
                    eng.dma_start(
                        out=wi_sb[slot][:, 4096:8192].rearrange("p (k j) -> p k j", k=16),
                        in_=dr_wi1[e, q]).then_inc(s_wi[slot].h, 16)
                s_wi[slot].n += 32
                wi_state["n"] += 1
                return slot, s_wi[slot].n, nl

            def load_wo(e, ha):
                nl = wo_state["n"]
                gw = wo_state["free"].get(nl - 1)

                @emit("G")
                def _(eng, e=e, ha=ha, gw=gw):
                    if gw is not None:
                        wait_pairs(eng, gw if isinstance(gw, list) else [gw])
                    eng.dma_start(out=wo_sb[:].rearrange("p (k j) -> p k j", k=4),
                                  in_=dr_wo[e, ha]).then_inc(s_wo[0].h, 16)
                s_wo[0].n += 16
                wo_state["n"] += 1
                return s_wo[0].n, nl

            inter_guard = [rg["inter"]]
            wo_bank_ctr = [0]
            tile_evac = {}
            h_done_guard = [None]

            def emit_wo(e, ha, inter_ready):
                wo_val, wo_idx = load_wo(e, ha)
                for j, tt in enumerate(tts):
                    record_evac = (e == E - 1 and ha == 1)
                    bs = (0, 1, 2, 3) if (wo_bank_ctr[0] % 2 == 0) else (4, 5, 6, 7)
                    wo_bank_ctr[0] += 1
                    bfws = [bw(b) for b in bs]

                    @emit("T")
                    def _(eng, tt=tt, bs=bs, j=j, wo_val=wo_val,
                          inter_ready=inter_ready, bfws=bfws):
                        if j == 0:
                            eng.wait_ge(s_wo[0].h, wo_val)
                            eng.wait_ge(s_v.h, inter_ready)
                        wait_pairs(eng, bfws)
                        mm = None
                        for kt in range(4):
                            for dg in range(4):
                                mm = eng.matmul(
                                    pb[bs[dg]][:],
                                    inter[:, kt * TOKW + tt * 128:kt * TOKW + tt * 128 + 128],
                                    wo_sb[:, kt * 2048 + dg * 512:kt * 2048 + dg * 512 + 512],
                                    start=(kt == 0), stop=(kt == 3),
                                    skip_group_check=True)
                        mm.then_inc(s_mm.h, 1)
                    s_mm.n += 1
                    for b in bs:
                        bank_fill[b] = s_mm.n

                    @emit("V")
                    def _(eng, e=e, tt=tt, bs=bs, fill=s_mm.n):
                        eng.wait_ge(s_mm.h, fill)
                        xa = x_tile(tt)
                        last = None
                        for dg in range(4):
                            xs = xa[:, dg * 512:(dg + 1) * 512]
                            last = eng.scalar_tensor_tensor(
                                xs, pb[bs[dg]][:], what_sb[:, tt * 8 + e:tt * 8 + e + 1],
                                xs, ALU.mult, ALU.add)
                        last.then_inc(s_v.h, 1)
                    s_v.n += 1
                    for b in bs:
                        bank_free[b] = (s_v, s_v.n)
                    if record_evac:
                        tile_evac[tt] = s_v.n
                        if tile_hook is not None:
                            tile_hook(tt, s_v.n)
                wo_state["free"][wo_idx] = (s_mm, s_mm.n)
                inter_guard[0] = (s_mm, s_mm.n)

            wi_cur = None
            for e in range(E):
                for mt in range(8):
                    q, idx = mt // 2, mt % 2
                    if idx == 0:
                        wi_cur = load_wi(e, q)
                    wi_slot, wi_val, wi_idx = wi_cur
                    fills = {}
                    for w in (0, 1):
                        banks = (0, 1, 2) if w == 0 else (3, 4, 5)
                        bfws = [bw(banks[gi]) for gi in range(ng)]

                        @emit("T")
                        def _(eng, e=e, mt=mt, w=w, banks=banks, idx=idx,
                              wi_slot=wi_slot, wi_val=wi_val, bfws=bfws,
                              groups=list(groups), xnt_ready=xnt_ready):
                            if idx == 0 and w == 0:
                                eng.wait_ge(s_wi[wi_slot].h, wi_val)
                            if e == 0 and mt == 0 and w == 0:
                                eng.wait_ge(s_tp[0].h, xnt_ready[0])
                                eng.wait_ge(s_tp[1].h, xnt_ready[1])
                            wait_pairs(eng, bfws)
                            woff = w * 4096
                            mm = None
                            for kt in range(16):
                                for gi, g0 in enumerate(groups):
                                    mm = eng.matmul(
                                        pb[banks[gi]][:],
                                        wi_sb[wi_slot][:, woff + kt * 256 + idx * 128:
                                                       woff + kt * 256 + idx * 128 + 128],
                                        XNT[:, kt * TOKW + g0:kt * TOKW + g0 + 512],
                                        start=(kt == 0), stop=(kt == 15),
                                        skip_group_check=True)
                            mm.then_inc(s_mm.h, 1)
                        s_mm.n += 1
                        for gi in range(ng):
                            bank_fill[banks[gi]] = s_mm.n
                        fills[w] = s_mm.n
                    if idx == 1:
                        wi_state["free"][wi_idx] = (s_mm, s_mm.n)

                    ig = inter_guard[0] if mt % 4 == 0 else None

                    @emit("S")
                    def _(eng, mt=mt, h0f=fills[0], ng=ng, groups=list(groups), ig=ig):
                        eng.wait_ge(s_mm.h, h0f)
                        if ig is not None:
                            wait_pairs(eng, ig if isinstance(ig, list) else [ig])
                        last = None
                        for gi, g0 in enumerate(groups):
                            last = eng.activation(
                                inter[:, (mt % 4) * TOKW + g0:(mt % 4) * TOKW + g0 + 512],
                                pb[gi][:], AF.Silu)
                        last.then_inc(s_s.h, 1)
                    s_s.n += 1
                    for b in (0, 1, 2)[:ng]:
                        bank_free[b] = (s_s, s_s.n)
                    s_silu = s_s.n

                    @emit("V")
                    def _(eng, mt=mt, s_silu=s_silu, h1f=fills[1], ng=ng,
                          groups=list(groups)):
                        eng.wait_ge(s_s.h, s_silu)
                        eng.wait_ge(s_mm.h, h1f)
                        last = None
                        for gi, g0 in enumerate(groups):
                            sl = slice((mt % 4) * TOKW + g0, (mt % 4) * TOKW + g0 + 512)
                            last = eng.tensor_tensor(inter[:, sl], inter[:, sl],
                                                     pb[(3, 4, 5)[gi]][:], ALU.mult)
                        last.then_inc(s_v.h, 1)
                    s_v.n += 1
                    for b in (3, 4, 5)[:ng]:
                        bank_free[b] = (s_v, s_v.n)
                    if e == E - 1 and mt == 7:
                        h_done_guard[0] = (s_mm, s_mm.n)
                    if mt % 4 == 3:
                        emit_wo(e, mt // 4, s_v.n)

            g = guard_now()
            if pre_drain is not None:
                pre_drain()
            return g, tile_evac, h_done_guard[0]

    def emit_attention(xnt_ready, tile_thr, region_guard, early_guard,
                       tile_hook=None):
        with contextlib.ExitStack() as aes:
            asb = lambda nme, s, d: aes.enter_context(nc.sbuf_tensor(nme, s, d))
            wv_sb = asb("wv_sb", [128, 16 * 256], BF16)
            wqk_sb = [asb(f"wqk_sb{i}", [128, 2 * 16 * 128], BF16) for i in range(2)]
            wout_sb = [asb(f"wout_sb{i}", [128, 4 * 512], BF16) for i in range(2)]
            av_sb = asb("av_sb", [128, 4 * 1024], BF16)
            E_sb = [asb(f"E_sb{i}", [128, 512], BF16) for i in range(2)]
            v4 = asb("v4", [128, NT3 * 256], BF16)
            q_sb = asb("q_sb", [128, 1024], BF16)
            k_sb = asb("k_sb", [128, TOKW], BF16)
            masks = asb("masks", [128, 8 * 512], BF16)
            mpre = asb("mpre_sb", [128, 4 * 512], BF16)

            masks_state = [False]

            def emit_masks_load():
                @emit("G")
                def _(eng, rg=list(region_guard)):
                    wait_pairs(eng, rg)
                    eng.dma_start(out=masks[:].rearrange("p (j i) -> p j i", j=8),
                                  in_=dr_band).then_inc(s_mask.h, 16)
                    eng.dma_start(out=mpre[:].rearrange("p (j i) -> p j i", j=4),
                                  in_=dr_mpre).then_inc(s_mask.h, 16)
                s_mask.n += 32
                masks_state[0] = True
            mask_val = s_mask.n + 32

            wqk_state = {"n": 0, "free": {-2: [early_guard], -1: [early_guard]}}
            wv_state = {"n": 0, "free": {-1: [early_guard]}}
            wout_state = {"n": 0, "free": {-2: [early_guard], -1: [early_guard]}}

            def load_wqk(h):
                nl = wqk_state["n"]
                slot = nl % 2
                gw = wqk_state["free"].get(nl - 2)

                @emit("G")
                def _(eng, h=h, slot=slot, gw=gw):
                    if gw is not None:
                        wait_pairs(eng, gw if isinstance(gw, list) else [gw])
                    eng.dma_start(
                        out=wqk_sb[slot][:, 0:2048].rearrange("p (k j) -> p k j", k=16),
                        in_=dr_wqk[h]).then_inc(s_wqk[slot].h, 16)
                    eng.dma_start(
                        out=wqk_sb[slot][:, 2048:4096].rearrange("p (k j) -> p k j", k=16),
                        in_=dr_wqk[16 + h]).then_inc(s_wqk[slot].h, 16)
                s_wqk[slot].n += 32
                wqk_state["n"] += 1
                return slot, s_wqk[slot].n, nl

            def load_wv(vg):
                nl = wv_state["n"]
                gw = wv_state["free"].get(nl - 1)

                @emit("G")
                def _(eng, vg=vg, gw=gw):
                    if gw is not None:
                        wait_pairs(eng, gw if isinstance(gw, list) else [gw])
                    eng.dma_start(out=wv_sb[:].rearrange("p (k j) -> p k j", k=16),
                                  in_=dr_wv[vg]).then_inc(s_wv.h, 16)
                s_wv.n += 16
                wv_state["n"] += 1
                return s_wv.n, nl

            def load_wout(qset, dg):
                nl = wout_state["n"]
                slot = nl % 2
                gw = wout_state["free"].get(nl - 2)

                @emit("G")
                def _(eng, qset=qset, dg=dg, slot=slot, gw=gw):
                    if gw is not None:
                        wait_pairs(eng, gw if isinstance(gw, list) else [gw])
                    eng.dma_start(out=wout_sb[slot][:].rearrange("p (k j) -> p k j", k=4),
                                  in_=dr_wout[qset, dg]).then_inc(s_wout[slot].h, 16)
                s_wout[slot].n += 16
                wout_state["n"] += 1
                return slot, s_wout[slot].n, nl

            negc_done = [None]
            att_tile_evac = {}
            pending_tail = []
            att_qk_done = [None]
            v4_guard = [region_guard]
            qk_guard = [region_guard]
            av_guard = [region_guard]
            rsbc_guard = [None]
            qkb_ctr = [0]

            def emit_scores(jj, jtile, b, ch, qk_ready, first):
                bfw_ = bw(b)

                @emit("T")
                def _(eng, jj=jj, jtile=jtile, b=b, ch=ch, bfw_=bfw_,
                      qk_ready=qk_ready, first=first, mv=mask_val):
                    if jj == 0:
                        eng.wait_ge(s_v.h, qk_ready[0])
                        eng.wait_ge(s_s.h, qk_ready[1])
                        if first:
                            eng.wait_ge(s_mask.h, mv)
                            eng.wait_ge(s_stpG.h, setupG_val)
                    wait_pairs(eng, [bfw_])
                    eng.matmul(pb[b][:], k_sb[:, jtile * 128:(jtile + 1) * 128],
                               q_sb[:, ch * 512:(ch + 1) * 512],
                               start=True, stop=False, skip_group_check=True)
                    if ch == 1 and jj < 4:
                        msk = mpre[:, jj * 512:(jj + 1) * 512]
                    else:
                        msk = masks[:, jj * 512:(jj + 1) * 512]
                    eng.matmul(pb[b][:], ident_bf[:], msk, start=False, stop=True,
                               skip_group_check=True).then_inc(s_mm.h, 1)
                s_mm.n += 1
                bank_fill[b] = s_mm.n
                return s_mm.n

            def emit_exp_avsum(jj, jtile, b, hh, e_fill, av_bank, sum_bank, sc_fill):
                eslot = jj % 2
                e_prev = e_fill.get(jj - 2)
                ndv = negc_done[0]

                @emit("S")
                def _(eng, jj=jj, b=b, eslot=eslot, sc_fill=sc_fill, e_prev=e_prev,
                      ndv=ndv):
                    eng.wait_ge(s_mm.h, sc_fill)
                    eng.wait_ge(s_v.h, ndv)
                    if e_prev is not None:
                        eng.wait_ge(s_mm.h, e_prev)
                    eng.activation(E_sb[eslot][:], pb[b][:], AF.Exp,
                                   bias=neg_c[:, 0:1]).then_inc(s_s.h, 1)
                s_s.n += 1
                bank_free[b] = (s_s, s_s.n)
                s_exp = s_s.n

                abw = bw(av_bank) if jj == 0 else None
                sbw = bw(sum_bank) if jj == 0 else None

                @emit("T")
                def _(eng, hh=hh, jj=jj, jtile=jtile, eslot=eslot, s_exp=s_exp,
                      abw=abw, sbw=sbw, av_bank=av_bank, sum_bank=sum_bank):
                    eng.wait_ge(s_s.h, s_exp)
                    if jj == 0:
                        wait_pairs(eng, [abw, sbw])
                    eng.matmul(pb[av_bank][:],
                               v4[:, jtile * 256 + hh * 128:jtile * 256 + hh * 128 + 128],
                               E_sb[eslot][:], start=(jj == 0), stop=(jj == 7),
                               skip_group_check=True)
                    eng.matmul(pb[sum_bank][0:1, :], ones_col_bf[:], E_sb[eslot][:],
                               start=(jj == 0), stop=(jj == 7),
                               skip_group_check=True).then_inc(s_mm.h, 1)
                s_mm.n += 1
                e_fill[jj] = s_mm.n
                if jj == 7:
                    bank_fill[av_bank] = s_mm.n
                    bank_fill[sum_bank] = s_mm.n

            for qset in range(4):
                av_writes = []
                for vgi in range(2):
                    vg = qset * 2 + vgi
                    wv_val, wv_idx = load_wv(vg)
                    v4g = v4_guard[0]
                    for tt in range(NT3):
                        b = tt % 2

                        @emit("T")
                        def _(eng, vg=vg, tt=tt, b=b, wv_val=wv_val, bfw=bw(b),
                              thr=tile_thr[tt]):
                            if tt == 0:
                                eng.wait_ge(s_wv.h, wv_val)
                            if vg == 0:
                                eng.wait_ge(s_tp[thr[0]].h, thr[1])
                            wait_pairs(eng, [bfw])
                            mm = None
                            for kt in range(16):
                                mm = eng.matmul(
                                    pb[b][:, 0:256],
                                    XNT[:, kt * TOKW + tt * 128:kt * TOKW + tt * 128 + 128],
                                    wv_sb[:, kt * 256:(kt + 1) * 256],
                                    start=(kt == 0), stop=(kt == 15),
                                    skip_group_check=True)
                            mm.then_inc(s_mm.h, 1)
                        s_mm.n += 1
                        bank_fill[b] = s_mm.n
                        last_xnt_reader[0] = s_mm.n

                        @emit("V")
                        def _(eng, tt=tt, b=b, fill=bank_fill[b], v4g=v4g):
                            eng.wait_ge(s_mm.h, fill)
                            if tt == 0:
                                wait_pairs(eng, v4g if isinstance(v4g, list) else [v4g])
                            eng.tensor_copy(v4[:, tt * 256:(tt + 1) * 256],
                                            pb[b][:, 0:256]).then_inc(s_v.h, 1)
                        s_v.n += 1
                        bank_free[b] = (s_v, s_v.n)
                    wv_state["free"][wv_idx] = (s_mm, s_mm.n)

                    for hh2 in range(2):
                        h = vg * 2 + hh2
                        wqk_slot, wqk_val, wqk_idx = load_wqk(h)
                        if not masks_state[0]:
                            emit_masks_load()
                        qkg = qk_guard[0]
                        for which, ngr, base in ((0, 2, 512), (1, 3, 0)):
                            for g in range(ngr):
                                b = 2 + (qkb_ctr[0] % 2)
                                qkb_ctr[0] += 1

                                tls = [tile_thr[t] for t in
                                       range((base + g * 512) // 128,
                                             (base + g * 512) // 128 + 4)]
                                gthr = [(sl, max(v for s2, v in tls if s2 == sl))
                                        for sl in {s2 for s2, _ in tls}]

                                @emit("T")
                                def _(eng, which=which, g=g, b=b, base=base,
                                      wqk_slot=wqk_slot, wqk_val=wqk_val, bfw=bw(b),
                                      gthr=gthr, h=h):
                                    if which == 0 and g == 0:
                                        eng.wait_ge(s_wqk[wqk_slot].h, wqk_val)
                                    if h == 0:
                                        for sl, v in gthr:
                                            eng.wait_ge(s_tp[sl].h, v)
                                    wait_pairs(eng, [bfw])
                                    woff = which * 2048
                                    mm = None
                                    for kt in range(16):
                                        mm = eng.matmul(
                                            pb[b][:],
                                            wqk_sb[wqk_slot][:, woff + kt * 128:
                                                             woff + kt * 128 + 128],
                                            XNT[:, kt * TOKW + base + g * 512:
                                                kt * TOKW + base + g * 512 + 512],
                                            start=(kt == 0), stop=(kt == 15),
                                            skip_group_check=True)
                                    mm.then_inc(s_mm.h, 1)
                                s_mm.n += 1
                                bank_fill[b] = s_mm.n
                                last_xnt_reader[0] = s_mm.n

                                if which == 0:
                                    @emit("V")
                                    def _(eng, g=g, b=b, fill=bank_fill[b], qkg=qkg):
                                        eng.wait_ge(s_mm.h, fill)
                                        if g == 0:
                                            wait_pairs(eng, qkg if isinstance(qkg, list) else [qkg])
                                        eng.tensor_scalar(
                                            q_sb[:, g * 512:(g + 1) * 512], pb[b][:],
                                            QSCALE, None, ALU.mult).then_inc(s_v.h, 1)
                                    s_v.n += 1
                                    bank_free[b] = (s_v, s_v.n)
                                else:
                                    @emit("S")
                                    def _(eng, g=g, b=b, fill=bank_fill[b], qkg=qkg):
                                        eng.wait_ge(s_mm.h, fill)
                                        if g == 0:
                                            wait_pairs(eng, qkg if isinstance(qkg, list) else [qkg])
                                        eng.copy(k_sb[:, g * 512:(g + 1) * 512],
                                                 pb[b][:]).then_inc(s_s.h, 1)
                                    s_s.n += 1
                                    bank_free[b] = (s_s, s_s.n)
                        qk_ready = (s_v.n, s_s.n)
                        wqk_state["free"][wqk_idx] = (s_mm, s_mm.n)

                        for ch in range(2):
                            jt_tiles = list(range(0, 8)) if ch == 0 else list(range(4, 12))
                            first = (h == 0 and ch == 0)
                            av_bank, sum_bank = 4 + ((h * 2 + ch) % 2), 6
                            e_fill = {}

                            if first:
                                first_banks = [2, 3, 7, 1]
                                vmasks = []
                                for jj in range(4):
                                    b = first_banks[jj]
                                    vmasks.append(emit_scores(jj, jt_tiles[jj], b, ch,
                                                              qk_ready, first))

                                    @emit("V")
                                    def _(eng, jj=jj, b=b, vm=vmasks[-1]):
                                        eng.wait_ge(s_mm.h, vm)
                                        eng.reduce_max(cchain[:, jj:jj + 1], pb[b][:],
                                                       AXX).then_inc(s_v.h, 1)
                                    s_v.n += 1
                                v_rm = s_v.n

                                @emit("V")
                                def _(eng, v_rm=v_rm):
                                    eng.wait_ge(s_v.h, v_rm)
                                    eng.reduce_max(cchain[:, 4:5], cchain[:, 0:4],
                                                   AXX).then_inc(s_v.h, 1)
                                s_v.n += 1
                                v_m8 = s_v.n

                                @emit("T")
                                def _(eng, v_m8=v_m8, bfw=bw(0)):
                                    eng.wait_ge(s_v.h, v_m8)
                                    wait_pairs(eng, [bfw])
                                    eng.transpose(pb[0][0:1, 0:128], cchain[:, 4:5],
                                                  ident_sb[:]).then_inc(s_mm.h, 1)
                                s_mm.n += 1
                                bank_fill[0] = s_mm.n
                                f0 = s_mm.n

                                @emit("V")
                                def _(eng, f0=f0):
                                    eng.wait_ge(s_mm.h, f0)
                                    eng.reduce_max(cmax[:], pb[0][0:1, 0:128],
                                                   AXX).then_inc(s_v.h, 1)
                                s_v.n += 1
                                bank_free[0] = (s_v, s_v.n)
                                v_cm = s_v.n

                                @emit("T")
                                def _(eng, v_cm=v_cm, bfw=bw(0)):
                                    eng.wait_ge(s_v.h, v_cm)
                                    wait_pairs(eng, [bfw])
                                    eng.matmul(pb[0][:, 0:1], ones_row_f[:], cmax[:],
                                               start=True, stop=True,
                                               skip_group_check=True).then_inc(s_mm.h, 1)
                                s_mm.n += 1
                                bank_fill[0] = s_mm.n
                                f0b = s_mm.n

                                @emit("V")
                                def _(eng, f0b=f0b):
                                    eng.wait_ge(s_mm.h, f0b)
                                    eng.tensor_scalar(neg_c[:], pb[0][:, 0:1], -1.0,
                                                      None, ALU.mult).then_inc(s_v.h, 1)
                                s_v.n += 1
                                bank_free[0] = (s_v, s_v.n)
                                negc_done[0] = s_v.n

                                while pending_tail:
                                    pending_tail.pop(0)()
                                for jj in range(4):
                                    emit_exp_avsum(jj, jt_tiles[jj], first_banks[jj],
                                                   hh2, e_fill, av_bank, sum_bank,
                                                   vmasks[jj])
                                for jj in range(4, 8):
                                    b = 2 + (jj % 2)
                                    vm = emit_scores(jj, jt_tiles[jj], b, ch, qk_ready,
                                                     first)
                                    emit_exp_avsum(jj, jt_tiles[jj], b, hh2, e_fill,
                                                   av_bank, sum_bank, vm)
                                negc_ready_first = negc_done[0]
                            else:
                                sc_banks = [2, 3, 7]
                                fills_jj = {}
                                for jj in range(3):
                                    fills_jj[jj] = emit_scores(jj, jt_tiles[jj],
                                                               sc_banks[jj % 3], ch,
                                                               qk_ready, first)
                                while pending_tail:
                                    pending_tail.pop(0)()
                                for jj in range(8):
                                    emit_exp_avsum(jj, jt_tiles[jj], sc_banks[jj % 3],
                                                   hh2, e_fill, av_bank, sum_bank,
                                                   fills_jj[jj])
                                    if jj + 3 < 8:
                                        fills_jj[jj + 3] = emit_scores(
                                            jj + 3, jt_tiles[jj + 3],
                                            sc_banks[(jj + 3) % 3], ch, qk_ready, first)

                            avsum_fill = s_mm.n

                            def emit_tail(h=h, ch=ch, av_bank=av_bank,
                                          fill=avsum_fill, ag=av_guard[0],
                                          rg=rsbc_guard[0]):
                                @emit("V")
                                def _(eng, fill=fill):
                                    eng.wait_ge(s_mm.h, fill)
                                    eng.reciprocal(rs_sb[:],
                                                   pb[6][0:1, :]).then_inc(s_v.h, 1)
                                s_v.n += 1
                                bank_free[6] = (s_v, s_v.n)
                                v_rs = s_v.n

                                @emit("T")
                                def _(eng, v_rs=v_rs, bfw=bw(6)):
                                    eng.wait_ge(s_v.h, v_rs)
                                    wait_pairs(eng, [bfw])
                                    eng.matmul(pb[6][:], ones_row_f[:], rs_sb[:],
                                               start=True, stop=True,
                                               skip_group_check=True).then_inc(s_mm.h, 1)
                                s_mm.n += 1
                                bank_fill[6] = s_mm.n
                                f_bc = s_mm.n

                                @emit("S")
                                def _(eng, f_bc=f_bc, rg=rg):
                                    eng.wait_ge(s_mm.h, f_bc)
                                    if rg is not None:
                                        eng.wait_ge(s_v.h, rg)
                                    eng.copy(rsbc_sb[:], pb[6][:]).then_inc(s_s.h, 1)
                                s_s.n += 1
                                bank_free[6] = (s_s, s_s.n)
                                s_bc = s_s.n

                                @emit("V")
                                def _(eng, h=h, ch=ch, s_bc=s_bc, ag=ag,
                                      av_bank=av_bank):
                                    eng.wait_ge(s_s.h, s_bc)
                                    wait_pairs(eng, ag if isinstance(ag, list) else [ag])
                                    hl = h % 4
                                    eng.tensor_tensor(
                                        av_sb[:, hl * 1024 + ch * 512:hl * 1024 + ch * 512 + 512],
                                        pb[av_bank][:], rsbc_sb[:],
                                        ALU.mult).then_inc(s_v.h, 1)
                                s_v.n += 1
                                bank_free[av_bank] = (s_v, s_v.n)
                                av_writes.append(s_v.n)
                                rsbc_guard[0] = s_v.n

                            pending_tail.append(emit_tail)
                        qk_guard[0] = (s_mm, s_mm.n)
                        if h == 15:
                            att_qk_done[0] = (s_mm, s_mm.n)
                    v4_guard[0] = (s_mm, s_mm.n)

                while pending_tail:
                    pending_tail.pop(0)()
                av_ready = max(av_writes)
                for dg in range(4):
                    wslot, wval, widx = load_wout(qset, dg)
                    for j, tt in enumerate(range(4, 12)):
                        b = 6 + (j % 2)

                        @emit("T")
                        def _(eng, dg=dg, tt=tt, b=b, j=j, wslot=wslot, wval=wval,
                              av_ready=av_ready, bfw=bw(b)):
                            if j == 0:
                                eng.wait_ge(s_wout[wslot].h, wval)
                                eng.wait_ge(s_v.h, av_ready)
                            wait_pairs(eng, [bfw])
                            ch2 = (tt - 4) // 4
                            pos = ((tt - 4) % 4) * 128
                            mm = None
                            for ekt in range(4):
                                mm = eng.matmul(
                                    pb[b][:],
                                    av_sb[:, ekt * 1024 + ch2 * 512 + pos:
                                          ekt * 1024 + ch2 * 512 + pos + 128],
                                    wout_sb[wslot][:, ekt * 512:(ekt + 1) * 512],
                                    start=(ekt == 0), stop=(ekt == 3),
                                    skip_group_check=True)
                            mm.then_inc(s_mm.h, 1)
                        s_mm.n += 1
                        bank_fill[b] = s_mm.n

                        @emit("V")
                        def _(eng, tt=tt, b=b, fill=bank_fill[b], dg=dg):
                            eng.wait_ge(s_mm.h, fill)
                            xs = X[:, (tt - 4) * D + dg * 512:(tt - 4) * D + dg * 512 + 512]
                            eng.tensor_tensor(xs, xs, pb[b][:], ALU.add).then_inc(s_v.h, 1)
                        s_v.n += 1
                        bank_free[b] = (s_v, s_v.n)
                        if qset == 3 and dg == 3:
                            att_tile_evac[tt] = s_v.n
                            if tile_hook is not None:
                                tile_hook(tt, s_v.n)
                    wout_state["free"][widx] = (s_mm, s_mm.n)
                av_guard[0] = (s_mm, s_mm.n)

            g = guard_now()
            if stage == "attn":
                _dbg(emit, X, X_halo, dbg, s_v, s_st)
            return g, att_tile_evac, att_qk_done[0]

    def _dbg(emit_, X_, Xh_, dbg_, sv_, sst_):
        rv = sv_.n

        @emit_("Y")
        def _(eng, rv=rv):
            eng.wait_ge(sv_.h, rv)
            eng.dma_start(out=dbg_["own"].rearrange("(t p) d -> p t d", p=128),
                          in_=X_[:].rearrange("p (t d) -> p t d", t=NT_OWN)
                          ).then_inc(sst_.h, 16)
        sst_.n += 16

        @emit_("G")
        def _(eng, rv=rv):
            eng.wait_ge(sv_.h, rv)
            eng.dma_start(out=dbg_["halo"].rearrange("(t p) d -> p t d", p=128),
                          in_=Xh_[:].rearrange("p (t d) -> p t d", t=4)
                          ).then_inc(sst_.h, 16)
        sst_.n += 16

        @emit_("Y")
        def _(eng, tot=sst_.n):
            eng.wait_ge(sst_.h, tot)

    xr, thr1 = emit_ln(0, list(range(12)), [])
    vr = emit_routing(0, list(range(12)), xr, thr1)
    pre1 = (lambda: _dbg(emit, X, X_halo, dbg, s_v, s_st)) if stage == "ffn1" else None
    ln2_hook, ln2_state = make_ln_hook(1, list(range(12)), [], xr)
    g_ffn1, te1, g_h7 = emit_ffn(0, list(range(12)), [0, 512, 1024], vr, xr,
                                 guard_now(), pre_drain=pre1,
                                 tile_hook=None if stage == "ffn1" else ln2_hook)
    if stage == "ffn1":
        drain()
        es.close()
        return nc

    xr2, thr2 = emit_ln(1, list(range(12)), [], hook_state=ln2_state)
    ln3_hook, ln3_state = make_ln_hook(2, list(range(4, 12)), [], xr2)
    g_att, te2, g_qk15 = emit_attention(xr2, thr2,
                                        g_ffn1 + [(s_v, s_v.n), (s_s, s_s.n)],
                                        g_h7,
                                        tile_hook=None if stage == "attn" else ln3_hook)
    if stage == "attn":
        drain()
        es.close()
        return nc

    xr3, thr3 = emit_ln(2, list(range(4, 12)), [], hook_state=ln3_state)
    vr3 = emit_routing(2, list(range(4, 12)), xr3, thr3)

    def store_hook(tt, v):
        @emit("Y")
        def _(eng, tt=tt, v=v):
            eng.wait_ge(s_v.h, v)
            eng.dma_start(
                out=dr_y.rearrange("(t p) d -> p t d", p=128)[:, tt - 4:tt - 3, :],
                in_=X[:, (tt - 4) * D:(tt - 3) * D].rearrange("p (t d) -> p t d", t=1)
                ).then_inc(s_st.h, 16)
        s_st.n += 16

    def pre_final():
        @emit("Y")
        def _(eng, tot=s_st.n):
            eng.wait_ge(s_st.h, tot)

    full_att_guard = g_att + [(s_v, s_v.n), (s_s, s_s.n)]
    emit_ffn(2, list(range(4, 12)), [512, 1024], vr3, xr3,
             {"wi0": [g_qk15], "wi1": full_att_guard, "wo": full_att_guard,
              "inter": full_att_guard}, pre_drain=pre_final, tile_hook=store_hook)
    drain()
    es.close()
    return nc


def host_inputs(L0_attn_x, L0_attn_w_qkv, L0_attn_w_out, L0_ffn_w_gate,
                L0_ffn_wi_0, L0_ffn_wi_1, L0_ffn_wo):
    x = np.ascontiguousarray(np.asarray(L0_attn_x, dtype=np.float32).reshape(8192, D))
    wqkT = np.ascontiguousarray(np.asarray(L0_attn_w_qkv, dtype=np.float32).T)
    woutT = np.ascontiguousarray(np.asarray(L0_attn_w_out, dtype=np.float32).T)
    wg = np.asarray(L0_ffn_w_gate, dtype=np.float32)
    wi0 = np.asarray(L0_ffn_wi_0, dtype=np.float32)
    wi1 = np.asarray(L0_ffn_wi_1, dtype=np.float32)
    wo = np.asarray(L0_ffn_wo, dtype=np.float32)

    wg_t = np.ascontiguousarray(wg.reshape(16, 128, 8).transpose(1, 0, 2))
    wi0_t = np.ascontiguousarray(wi0.reshape(E, 16, 128, 4, 256).transpose(0, 3, 2, 1, 4))
    wi1_t = np.ascontiguousarray(wi1.reshape(E, 16, 128, 4, 256).transpose(0, 3, 2, 1, 4))
    wo_t = np.ascontiguousarray(wo.reshape(E, 2, 4, 128, 2048).transpose(0, 1, 3, 2, 4))
    wqk_t = np.ascontiguousarray(
        wqkT[:, :4096].reshape(16, 128, 32, 128).transpose(2, 1, 0, 3))
    wv_t = np.ascontiguousarray(
        wqkT[:, 4096:].reshape(16, 128, 8, 256).transpose(2, 1, 0, 3))
    wout_t = np.ascontiguousarray(
        woutT.reshape(4, 4, 128, 4, 512).transpose(0, 3, 2, 1, 4))

    p = np.arange(128)[:, None]
    i = np.arange(512)[None, :]
    band = np.zeros((128, 8, 512), dtype=np.float32)
    for jt in range(8):
        j = 128 * jt + p
        ok = (i >= j - 512) & (i <= j)
        band[:, jt, :] = np.where(ok, 0.0, np.float32(-1e9))
    band = band.astype(BF)
    kill = np.full((128, 4, 512), -1e9, dtype=np.float32).astype(BF)
    ident = np.eye(128, dtype=np.float32)

    shared = dict(wg=wg_t, wi0=wi0_t, wi1=wi1_t, wo=wo_t, wqk=wqk_t, wv=wv_t,
                  wout=wout_t, band=band, ident=ident)
    in_maps = []
    for c in range(8):
        g0, g1, g2 = 2 * c, 2 * c + 1, (2 * c + 2) % 16
        x_halo = x[g0 * CH:(g0 + 1) * CH]
        x_own = np.concatenate([x[g1 * CH:(g1 + 1) * CH], x[g2 * CH:(g2 + 1) * CH]])
        q2_first = (g2 % 4 == 0)
        mpre_c = kill if q2_first else np.ascontiguousarray(band[:, 0:4])
        m = dict(shared)
        m["x_own"] = np.ascontiguousarray(x_own)
        m["x_halo"] = np.ascontiguousarray(x_halo)
        m["mpre"] = np.ascontiguousarray(mpre_c)
        in_maps.append(m)
    return in_maps


def assemble(results):
    y = np.empty((8192, D), dtype=np.float32)
    for c in range(8):
        g1, g2 = 2 * c + 1, (2 * c + 2) % 16
        yc = results[c]["y"]
        y[g1 * CH:(g1 + 1) * CH] = yc[0:CH]
        y[g2 * CH:(g2 + 1) * CH] = yc[CH:2 * CH]
    return y.reshape(4, 2048, D)


def kernel(L0_attn_x, L0_attn_w_qkv, L0_attn_w_out, L0_ffn_w_gate,
           L0_ffn_wi_0, L0_ffn_wi_1, L0_ffn_wo):
    in_maps = host_inputs(L0_attn_x, L0_attn_w_qkv, L0_attn_w_out, L0_ffn_w_gate,
                          L0_ffn_wi_0, L0_ffn_wi_1, L0_ffn_wo)
    nc = build_program("full")
    res = run_bass_kernel_spmd(nc, in_maps, core_ids=list(range(8)))
    return assemble(res.results)


# version 14
# speedup vs baseline: 1.1607x; 1.0005x over previous
import sys
sys.path.insert(0, "/opt/trn_rl_repo")
import contextlib
import numpy as np
import ml_dtypes
import concourse.bass as bass
import concourse.mybir as mybir
from concourse.bass_utils import run_bass_kernel_spmd

F32 = mybir.dt.float32
BF16 = mybir.dt.bfloat16
AF = mybir.ActivationFunctionType
ALU = mybir.AluOpType
AXX = mybir.AxisListType.X
BF = ml_dtypes.bfloat16

D = 2048
E = 8
CH = 512
NT3 = 12
NT_OWN = 8
TOKW = 1536
QSCALE = float(1.0 / np.float64(np.float32(np.sqrt(np.float64(128.0)))))
EPS = 1e-5


class Ctr:
    def __init__(self, h):
        self.h = h
        self.n = 0


def build_program(stage="full"):
    nc = bass.Bass("TRN2", target_bir_lowering=False)

    dr_x_own = nc.dram_tensor("x_own", [1024, D], F32, kind="ExternalInput").ap()
    dr_x_halo = nc.dram_tensor("x_halo", [512, D], F32, kind="ExternalInput").ap()
    dr_wg = nc.dram_tensor("wg", [128, 16, 8], F32, kind="ExternalInput").ap()
    dr_wi0 = nc.dram_tensor("wi0", [E, 4, 128, 16, 256], F32, kind="ExternalInput").ap()
    dr_wi1 = nc.dram_tensor("wi1", [E, 4, 128, 16, 256], F32, kind="ExternalInput").ap()
    dr_wo = nc.dram_tensor("wo", [E, 2, 128, 4, 2048], F32, kind="ExternalInput").ap()
    dr_wqk = nc.dram_tensor("wqk", [32, 128, 16, 128], F32, kind="ExternalInput").ap()
    dr_wv = nc.dram_tensor("wv", [8, 128, 16, 256], F32, kind="ExternalInput").ap()
    dr_wout = nc.dram_tensor("wout", [4, 4, 128, 4, 512], F32, kind="ExternalInput").ap()
    dr_band = nc.dram_tensor("band", [128, 8, 512], BF16, kind="ExternalInput").ap()
    dr_mpre = nc.dram_tensor("mpre", [128, 4, 512], BF16, kind="ExternalInput").ap()
    dr_ident = nc.dram_tensor("ident", [128, 128], F32, kind="ExternalInput").ap()
    dr_y = nc.dram_tensor("y", [1024, D], F32, kind="ExternalOutput").ap()
    dbg = {}
    if stage != "full":
        dbg["own"] = nc.dram_tensor("dbg_own", [1024, D], F32, kind="ExternalOutput").ap()
        dbg["halo"] = nc.dram_tensor("dbg_halo", [512, D], F32, kind="ExternalOutput").ap()

    es = contextlib.ExitStack()
    sbuf = lambda n, s, d: es.enter_context(nc.sbuf_tensor(n, s, d))

    X = sbuf("X", [128, NT_OWN * D], F32)
    XNT = sbuf("XNT", [128, 16 * TOKW], BF16)
    X_halo = sbuf("X_halo", [128, 4 * D], BF16)
    xn_tmp = [sbuf(f"xn_tmp{i}", [128, D], BF16) for i in range(2)]
    wg_sb = sbuf("wg_sb", [128, 16 * 8], BF16)
    what_sb = sbuf("what_sb", [128, NT3 * 8], F32)
    stats = sbuf("stats", [128, 64], F32)
    lg_all = sbuf("lg_all", [128, NT3 * 8], F32)
    top_all = sbuf("top_all", [128, NT3 * 8], F32)
    r_scr = sbuf("r_scr", [128, 3 * 16], F32)
    m_scr = sbuf("m_scr", [128, 16], F32)
    ident_bf = sbuf("ident_bf", [128, 128], BF16)
    cmax_col_bf = sbuf("cmax_col_bf", [128, 1], BF16)
    ones_col_bf = sbuf("ones_col_bf", [128, 1], BF16)
    ones_row_f = sbuf("ones_row_f", [1, 128], F32)
    neg_c = sbuf("neg_c", [128, 1], F32)
    cchain = sbuf("cchain", [128, 8], F32)
    cmax = sbuf("cmax", [1, 1], F32)
    rs_sb = sbuf("rs_sb", [1, 512], F32)
    rsbc_sb = sbuf("rsbc_sb", [128, 512], BF16)

    pb = [es.enter_context(nc.psum_tensor(f"pb{i}", [128, 512], F32)) for i in range(8)]

    sem = lambda n: Ctr(es.enter_context(nc.semaphore(n)))
    s_mm = sem("s_mm")
    s_v = sem("s_v")
    s_s = sem("s_s")
    s_g = sem("s_g")
    s_stpY = sem("s_stpY")
    s_stpG = sem("s_stpG")
    s_tp = [sem("s_tpA"), sem("s_tpB")]
    s_wi = [sem("s_wi0"), sem("s_wi1")]
    s_wo = [sem("s_wo0"), sem("s_wo1")]
    s_wqk = [sem("s_wqk0"), sem("s_wqk1")]
    s_wv = sem("s_wv")
    s_wout = [sem("s_wout0"), sem("s_wout1")]
    s_mask = sem("s_mask")
    s_halo = sem("s_halo")
    s_x2 = sem("s_x2")
    s_st = sem("s_st")

    ev = {k: [] for k in ("T", "V", "S", "G", "Y")}

    def emit(which):
        def deco(f):
            ev[which].append(f)
        return deco

    def drain():
        with nc.Block() as block:
            @block.tensor
            def _(eng):
                for f in ev["T"]:
                    f(eng)

            @block.vector
            def _(eng):
                for f in ev["V"]:
                    f(eng)

            @block.scalar
            def _(eng):
                for f in ev["S"]:
                    f(eng)

            @block.gpsimd
            def _(eng):
                for f in ev["G"]:
                    f(eng)

            @block.sync
            def _(eng):
                for f in ev["Y"]:
                    f(eng)
        for k in ev:
            ev[k].clear()

    bank_free = [None] * 8
    bank_fill = [0] * 8

    def bw(b):
        return bank_free[b]

    def wait_pairs(eng, pairs):
        for p in pairs:
            if p is not None:
                eng.wait_ge(p[0].h, p[1])

    def guard_now():
        return [(s_mm, s_mm.n), (s_v, s_v.n), (s_s, s_s.n)]

    @emit("Y")
    def _(eng):
        xo = dr_x_own.rearrange("(t p) d -> p t d", p=128)
        eng.dma_start(out=X[:, 0:4 * D].rearrange("p (t d) -> p t d", t=4),
                      in_=xo[:, 0:4, :]).then_inc(s_stpY.h, 16)
        eng.dma_start(out=X[:, 4 * D:8 * D].rearrange("p (t d) -> p t d", t=4),
                      in_=xo[:, 4:8, :]).then_inc(s_x2.h, 16)
    s_stpY.n += 16
    s_x2.n += 16

    @emit("G")
    def _(eng):
        eng.dma_start(out=X_halo[:].rearrange("p (t d) -> p t d", t=4),
                      in_=dr_x_halo.rearrange("(t p) d -> p t d", p=128)).then_inc(s_halo.h, 16)
        eng.memset(ones_col_bf[:], 1.0)
        eng.memset(ones_row_f[:], 1.0).then_inc(s_g.h, 1)
        eng.dma_start(out=wg_sb[:].rearrange("p (k j) -> p k j", k=16),
                      in_=dr_wg).then_inc(s_stpG.h, 16)
        eng.dma_start(out=ident_bf[:], in_=dr_ident).then_inc(s_stpG.h, 16)
    s_g.n += 1
    s_halo.n += 16
    s_stpG.n += 32
    setupY_val = s_stpY.n
    setupG_val = s_stpG.n

    xnt_uses = [0, 0]
    last_xnt_reader = [0]

    def x_tile(tt):
        if tt < 4:
            return X_halo[:, tt * D:(tt + 1) * D]
        return X[:, (tt - 4) * D:(tt - 3) * D]

    def emit_ln_tile_stats(call, i, tt, dep, td, st):
        @emit("V")
        def _(eng, i=i, tt=tt, dep=list(dep), call=call, td=td):
            if call == 0 and tt == 0:
                eng.wait_ge(s_halo.h, 16)
            if call == 0 and tt == 4:
                eng.wait_ge(s_stpY.h, setupY_val)
            if call == 0 and tt == 8:
                eng.wait_ge(s_x2.h, 16)
            if call == 0 and tt == 8:
                eng.wait_ge(s_x2.h, 16)
            if td is not None:
                eng.wait_ge(s_v.h, td)
            elif i == 0:
                wait_pairs(eng, dep)
            eng.tensor_reduce(stats[:, i:i + 1], x_tile(tt), AXX,
                              ALU.add).then_inc(s_v.h, 1)
        s_v.n += 1
        st["red"][i] = s_v.n

        @emit("S")
        def _(eng, i=i, tt=tt, dep=list(dep), call=call, td=td,
              w=st["sq"].get(i - 2, 0), w_tp=16 * xnt_uses[i % 2]):
            if call == 0 and tt == 0:
                eng.wait_ge(s_halo.h, 16)
            if call == 0 and tt == 4:
                eng.wait_ge(s_stpY.h, setupY_val)
            if i == 0:
                wait_pairs(eng, dep)
            if w_tp > 0:
                eng.wait_ge(s_tp[i % 2].h, w_tp)
            if td is not None:
                eng.wait_ge(s_v.h, td)
            if w > 0:
                eng.wait_ge(s_s.h, w)
            eng.activation(xn_tmp[i % 2][:], x_tile(tt), AF.Square,
                           accum_out=stats[:, 12 + i:13 + i]).then_inc(s_s.h, 1)
        s_s.n += 1
        st["sq"][i] = s_s.n

    def emit_ln_batch(call, st, h0, h1):
        vw = max(st["red"][j] for j in range(h0, h1))
        sw = max(st["sq"][j] for j in range(h0, h1))

        @emit("V")
        def _(eng, h0=h0, h1=h1, vw=vw):
            eng.wait_ge(s_v.h, vw)
            eng.tensor_scalar(stats[:, 24 + h0:24 + h1], stats[:, h0:h1], 1.0 / D,
                              None, ALU.mult).then_inc(s_v.h, 1)
        s_v.n += 1
        v_mu = s_v.n

        @emit("V")
        def _(eng, h0=h0, h1=h1, v_mu=v_mu):
            eng.wait_ge(s_v.h, v_mu)
            eng.tensor_tensor(stats[:, 48 + h0:48 + h1], stats[:, 24 + h0:24 + h1],
                              stats[:, 24 + h0:24 + h1], ALU.mult).then_inc(s_v.h, 1)
        s_v.n += 1
        v_musq = s_v.n

        @emit("V")
        def _(eng, h0=h0, h1=h1, v_musq=v_musq, sw=sw):
            eng.wait_ge(s_v.h, v_musq)
            eng.wait_ge(s_s.h, sw)
            eng.scalar_tensor_tensor(stats[:, 48 + h0:48 + h1],
                                     stats[:, 12 + h0:12 + h1], 1.0 / D,
                                     stats[:, 48 + h0:48 + h1], ALU.mult,
                                     ALU.subtract).then_inc(s_v.h, 1)
        s_v.n += 1
        v_var = s_v.n

        @emit("V")
        def _(eng, h0=h0, h1=h1, v_var=v_var):
            eng.wait_ge(s_v.h, v_var)
            eng.tensor_scalar(stats[:, 48 + h0:48 + h1], stats[:, 48 + h0:48 + h1],
                              EPS, None, ALU.add).then_inc(s_v.h, 1)
        s_v.n += 1
        v_vpe = s_v.n

        @emit("S")
        def _(eng, h0=h0, h1=h1, v_vpe=v_vpe):
            eng.wait_ge(s_v.h, v_vpe)
            eng.activation(stats[:, 36 + h0:36 + h1], stats[:, 48 + h0:48 + h1],
                           AF.Sqrt).then_inc(s_s.h, 1)
        s_s.n += 1
        s_sqrt = s_s.n

        @emit("V")
        def _(eng, h0=h0, h1=h1, s_sqrt=s_sqrt):
            eng.wait_ge(s_s.h, s_sqrt)
            eng.reciprocal(stats[:, 36 + h0:36 + h1],
                           stats[:, 36 + h0:36 + h1]).then_inc(s_v.h, 1)
        s_v.n += 1
        rstd_v = s_v.n
        for j in range(h0, h1):
            _emit_ln_norm(call, j, st["tts"][j], st, rstd_v)

    def _emit_ln_norm(call, i, tt, st, rstd_v):
        slot = i % 2
        tp_wait = 16 * xnt_uses[slot]
        xnt_uses[slot] += 1

        @emit("V")
        def _(eng, i=i, tt=tt, slot=slot, tp_wait=tp_wait, rstd_v=rstd_v):
            eng.wait_ge(s_v.h, rstd_v)
            if tp_wait:
                eng.wait_ge(s_tp[slot].h, tp_wait)
            eng.tensor_scalar(xn_tmp[slot][:], x_tile(tt), stats[:, 24 + i:25 + i],
                              stats[:, 36 + i:37 + i], ALU.subtract,
                              ALU.mult).then_inc(s_v.h, 1)
        s_v.n += 1
        v_norm = s_v.n

        @emit("Y" if slot == 0 else "S")
        def _(eng, i=i, tt=tt, slot=slot, v_norm=v_norm, xg=st["xg"]):
            eng.wait_ge(s_v.h, v_norm)
            if i <= 1:
                eng.wait_ge(s_mm.h, xg)
            eng.dma_start(
                out=XNT[:].rearrange("p (k j) -> p k j", k=16)[:, :, tt * 128:(tt + 1) * 128],
                in_=xn_tmp[slot][:].rearrange("p (k j) -> p k j", k=16),
                transpose=True).then_inc(s_tp[slot].h, 16)
        s_tp[slot].n += 16
        st["thr"][tt] = (slot, s_tp[slot].n)

    def _new_ln_state(tts, tp_prev):
        return {"sq": {}, "red": {}, "thr": {}, "done": [],
                "tts": list(tts), "tp_prev": tp_prev, "xg": last_xnt_reader[0]}

    def make_ln_hook(call, tts, dep, tp_prev):
        st = _new_ln_state(tts, tp_prev)
        n = len(tts)

        q = n // 4

        def hook(tt, td):
            if not st["done"]:
                st["xg"] = last_xnt_reader[0]
            i = tts.index(tt)
            emit_ln_tile_stats(call, i, tt, dep, td, st)
            st["done"].append(tt)
            if (i + 1) % q == 0:
                emit_ln_batch(call, st, i + 1 - q, i + 1)
        return hook, st

    def emit_ln(call, tts, dep, dep_tile=None, hook_state=None, tp_prev=None):
        n = len(tts)
        if hook_state is None:
            st = _new_ln_state(tts, tp_prev)
            st["xg"] = last_xnt_reader[0]
            q = n // 4
            for i, tt in enumerate(tts):
                td = dep_tile.get(tt) if dep_tile else None
                emit_ln_tile_stats(call, i, tt, dep, td, st)
                if (i + 1) % q == 0:
                    emit_ln_batch(call, st, i + 1 - q, i + 1)
        else:
            st = hook_state
            assert len(st["done"]) == len(tts), st["done"]
        return (s_tp[0].n, s_tp[1].n), st["thr"]

    def emit_routing(call, tts, xnt_ready, tile_thr):
        n = len(tts)
        for i, tt in enumerate(tts):
            b = 6 + (i % 2)

            @emit("T")
            def _(eng, i=i, tt=tt, b=b, bfw=bw(b), call=call, thr=tile_thr[tt]):
                eng.wait_ge(s_tp[thr[0]].h, thr[1])
                if i == 0 and call == 0:
                    eng.wait_ge(s_stpG.h, setupG_val)
                wait_pairs(eng, [bfw])
                mm = None
                for kt in range(16):
                    mm = eng.matmul(pb[b][:, 0:8],
                                    XNT[:, kt * TOKW + tt * 128:kt * TOKW + tt * 128 + 128],
                                    wg_sb[:, kt * 8:(kt + 1) * 8],
                                    start=(kt == 0), stop=(kt == 15),
                                    skip_group_check=True)
                mm.then_inc(s_mm.h, 1)
            s_mm.n += 1
            bank_fill[b] = s_mm.n

            @emit("S")
            def _(eng, tt=tt, b=b, fill=bank_fill[b]):
                eng.wait_ge(s_mm.h, fill)
                eng.copy(lg_all[:, tt * 8:(tt + 1) * 8], pb[b][:, 0:8]).then_inc(s_s.h, 1)
            s_s.n += 1
            bank_free[b] = (s_s, s_s.n)
            s_lg = s_s.n

            @emit("V")
            def _(eng, tt=tt, s_lg=s_lg):
                eng.wait_ge(s_s.h, s_lg)
                eng.max(top_all[:, tt * 8:(tt + 1) * 8],
                        lg_all[:, tt * 8:(tt + 1) * 8]).then_inc(s_v.h, 1)
            s_v.n += 1
            v_max = s_v.n

            @emit("V")
            def _(eng, i=i, tt=tt, v_max=v_max):
                eng.wait_ge(s_v.h, v_max)
                eng.tensor_tensor(r_scr[:, i:i + 1], top_all[:, tt * 8:tt * 8 + 1],
                                  top_all[:, tt * 8 + 1:tt * 8 + 2],
                                  ALU.subtract).then_inc(s_v.h, 1)
            s_v.n += 1
        v_d12 = s_v.n

        @emit("S")
        def _(eng, v_d12=v_d12, n=n):
            eng.wait_ge(s_v.h, v_d12)
            eng.activation(r_scr[:, 16:16 + n], r_scr[:, 0:n],
                           AF.Sigmoid).then_inc(s_s.h, 1)
        s_s.n += 1
        s_sig = s_s.n

        @emit("V")
        def _(eng, s_sig=s_sig, n=n):
            eng.wait_ge(s_s.h, s_sig)
            eng.tensor_scalar(r_scr[:, 16:16 + n], r_scr[:, 16:16 + n], 0.5, None,
                              ALU.mult).then_inc(s_v.h, 1)
        s_v.n += 1
        v_w1 = s_v.n

        @emit("V")
        def _(eng, v_w1=v_w1, n=n):
            eng.wait_ge(s_v.h, v_w1)
            eng.tensor_scalar(r_scr[:, 32:32 + n], r_scr[:, 16:16 + n], -1.0, 0.5,
                              ALU.mult, ALU.add).then_inc(s_v.h, 1)
        s_v.n += 1
        prev = s_v.n

        for i, tt in enumerate(tts):
            @emit("V")
            def _(eng, i=i, tt=tt, pw=prev):
                eng.wait_ge(s_v.h, pw)
                sl = slice(tt * 8, (tt + 1) * 8)
                eng.tensor_scalar(m_scr[:, 0:8], lg_all[:, sl],
                                  top_all[:, tt * 8:tt * 8 + 1], None, ALU.is_equal)
                eng.tensor_scalar(m_scr[:, 8:16], lg_all[:, sl],
                                  top_all[:, tt * 8 + 1:tt * 8 + 2], None,
                                  ALU.is_equal).then_inc(s_v.h, 1)
            s_v.n += 1
            v_m = s_v.n

            @emit("V")
            def _(eng, i=i, v_m=v_m):
                eng.wait_ge(s_v.h, v_m)
                eng.tensor_scalar(m_scr[:, 8:16], m_scr[:, 8:16],
                                  r_scr[:, 32 + i:33 + i], None,
                                  ALU.mult).then_inc(s_v.h, 1)
            s_v.n += 1
            v_m2 = s_v.n

            @emit("V")
            def _(eng, i=i, tt=tt, v_m2=v_m2):
                eng.wait_ge(s_v.h, v_m2)
                eng.scalar_tensor_tensor(what_sb[:, tt * 8:(tt + 1) * 8],
                                         m_scr[:, 0:8], r_scr[:, 16 + i:17 + i],
                                         m_scr[:, 8:16], ALU.mult,
                                         ALU.add).then_inc(s_v.h, 1)
            s_v.n += 1
            prev = s_v.n
        return s_v.n

    def emit_ffn(call, tts, groups, v_routing, xnt_ready, region_guard, pre_drain=None, tile_hook=None):
        ng = len(groups)
        with contextlib.ExitStack() as fes:
            fsb = lambda nme, s, d: fes.enter_context(
                nc.sbuf_tensor(f"{nme}_c{call}", s, d))
            wi_sb = [fsb(f"wi_sb{i}", [128, 2 * 16 * 256], BF16) for i in range(2)]
            wo_sb = fsb("wo_sb", [128, 4 * 2048], BF16)
            inter = fsb("inter", [128, 4 * TOKW], BF16)

            rg = dict(region_guard) if isinstance(region_guard, dict) else {
                "wi0": region_guard, "wi1": region_guard, "wo": region_guard,
                "inter": region_guard}
            wi_state = {"n": 0, "free": {-2: rg["wi0"], -1: rg["wi1"]}}
            wo_state = {"n": 0, "free": {-1: rg["wo"]}}

            def load_wi(e, q):
                nl = wi_state["n"]
                slot = nl % 2
                gw = wi_state["free"].get(nl - 2)

                @emit("G")
                def _(eng, e=e, q=q, slot=slot, gw=gw):
                    if gw is not None:
                        wait_pairs(eng, gw if isinstance(gw, list) else [gw])
                    eng.dma_start(
                        out=wi_sb[slot][:, 0:4096].rearrange("p (k j) -> p k j", k=16),
                        in_=dr_wi0[e, q]).then_inc(s_wi[slot].h, 16)
                    eng.dma_start(
                        out=wi_sb[slot][:, 4096:8192].rearrange("p (k j) -> p k j", k=16),
                        in_=dr_wi1[e, q]).then_inc(s_wi[slot].h, 16)
                s_wi[slot].n += 32
                wi_state["n"] += 1
                return slot, s_wi[slot].n, nl

            def load_wo(e, ha):
                nl = wo_state["n"]
                gw = wo_state["free"].get(nl - 1)

                @emit("G")
                def _(eng, e=e, ha=ha, gw=gw):
                    if gw is not None:
                        wait_pairs(eng, gw if isinstance(gw, list) else [gw])
                    eng.dma_start(out=wo_sb[:].rearrange("p (k j) -> p k j", k=4),
                                  in_=dr_wo[e, ha]).then_inc(s_wo[0].h, 16)
                s_wo[0].n += 16
                wo_state["n"] += 1
                return s_wo[0].n, nl

            inter_guard = [rg["inter"]]
            wo_bank_ctr = [0]
            tile_evac = {}
            h_done_guard = [None]

            def emit_wo(e, ha, inter_ready):
                wo_val, wo_idx = load_wo(e, ha)
                for j, tt in enumerate(tts):
                    record_evac = (e == E - 1 and ha == 1)
                    bs = (0, 1, 2, 3) if (wo_bank_ctr[0] % 2 == 0) else (4, 5, 6, 7)
                    wo_bank_ctr[0] += 1
                    bfws = [bw(b) for b in bs]

                    @emit("T")
                    def _(eng, tt=tt, bs=bs, j=j, wo_val=wo_val,
                          inter_ready=inter_ready, bfws=bfws):
                        if j == 0:
                            eng.wait_ge(s_wo[0].h, wo_val)
                            eng.wait_ge(s_v.h, inter_ready)
                        wait_pairs(eng, bfws)
                        mm = None
                        for kt in range(4):
                            for dg in range(4):
                                mm = eng.matmul(
                                    pb[bs[dg]][:],
                                    inter[:, kt * TOKW + tt * 128:kt * TOKW + tt * 128 + 128],
                                    wo_sb[:, kt * 2048 + dg * 512:kt * 2048 + dg * 512 + 512],
                                    start=(kt == 0), stop=(kt == 3),
                                    skip_group_check=True)
                        mm.then_inc(s_mm.h, 1)
                    s_mm.n += 1
                    for b in bs:
                        bank_fill[b] = s_mm.n

                    @emit("V")
                    def _(eng, e=e, tt=tt, bs=bs, fill=s_mm.n):
                        eng.wait_ge(s_mm.h, fill)
                        xa = x_tile(tt)
                        last = None
                        for dg in range(4):
                            xs = xa[:, dg * 512:(dg + 1) * 512]
                            last = eng.scalar_tensor_tensor(
                                xs, pb[bs[dg]][:], what_sb[:, tt * 8 + e:tt * 8 + e + 1],
                                xs, ALU.mult, ALU.add)
                        last.then_inc(s_v.h, 1)
                    s_v.n += 1
                    for b in bs:
                        bank_free[b] = (s_v, s_v.n)
                    if record_evac:
                        tile_evac[tt] = s_v.n
                        if tile_hook is not None:
                            tile_hook(tt, s_v.n)
                wo_state["free"][wo_idx] = (s_mm, s_mm.n)
                inter_guard[0] = (s_mm, s_mm.n)

            wi_cur = None
            for e in range(E):
                for mt in range(8):
                    q, idx = mt // 2, mt % 2
                    if idx == 0:
                        wi_cur = load_wi(e, q)
                    wi_slot, wi_val, wi_idx = wi_cur
                    fills = {}
                    for w in (0, 1):
                        banks = (0, 1, 2) if w == 0 else (3, 4, 5)
                        bfws = [bw(banks[gi]) for gi in range(ng)]

                        @emit("T")
                        def _(eng, e=e, mt=mt, w=w, banks=banks, idx=idx,
                              wi_slot=wi_slot, wi_val=wi_val, bfws=bfws,
                              groups=list(groups), xnt_ready=xnt_ready):
                            if idx == 0 and w == 0:
                                eng.wait_ge(s_wi[wi_slot].h, wi_val)
                            if e == 0 and mt == 0 and w == 0:
                                eng.wait_ge(s_tp[0].h, xnt_ready[0])
                                eng.wait_ge(s_tp[1].h, xnt_ready[1])
                            wait_pairs(eng, bfws)
                            woff = w * 4096
                            mm = None
                            for kt in range(16):
                                for gi, g0 in enumerate(groups):
                                    mm = eng.matmul(
                                        pb[banks[gi]][:],
                                        wi_sb[wi_slot][:, woff + kt * 256 + idx * 128:
                                                       woff + kt * 256 + idx * 128 + 128],
                                        XNT[:, kt * TOKW + g0:kt * TOKW + g0 + 512],
                                        start=(kt == 0), stop=(kt == 15),
                                        skip_group_check=True)
                            mm.then_inc(s_mm.h, 1)
                        s_mm.n += 1
                        for gi in range(ng):
                            bank_fill[banks[gi]] = s_mm.n
                        fills[w] = s_mm.n
                    if idx == 1:
                        wi_state["free"][wi_idx] = (s_mm, s_mm.n)

                    ig = inter_guard[0] if mt % 4 == 0 else None

                    @emit("S")
                    def _(eng, mt=mt, h0f=fills[0], ng=ng, groups=list(groups), ig=ig):
                        eng.wait_ge(s_mm.h, h0f)
                        if ig is not None:
                            wait_pairs(eng, ig if isinstance(ig, list) else [ig])
                        last = None
                        for gi, g0 in enumerate(groups):
                            last = eng.activation(
                                inter[:, (mt % 4) * TOKW + g0:(mt % 4) * TOKW + g0 + 512],
                                pb[gi][:], AF.Silu)
                        last.then_inc(s_s.h, 1)
                    s_s.n += 1
                    for b in (0, 1, 2)[:ng]:
                        bank_free[b] = (s_s, s_s.n)
                    s_silu = s_s.n

                    @emit("V")
                    def _(eng, mt=mt, s_silu=s_silu, h1f=fills[1], ng=ng,
                          groups=list(groups)):
                        eng.wait_ge(s_s.h, s_silu)
                        eng.wait_ge(s_mm.h, h1f)
                        last = None
                        for gi, g0 in enumerate(groups):
                            sl = slice((mt % 4) * TOKW + g0, (mt % 4) * TOKW + g0 + 512)
                            last = eng.tensor_tensor(inter[:, sl], inter[:, sl],
                                                     pb[(3, 4, 5)[gi]][:], ALU.mult)
                        last.then_inc(s_v.h, 1)
                    s_v.n += 1
                    for b in (3, 4, 5)[:ng]:
                        bank_free[b] = (s_v, s_v.n)
                    if e == E - 1 and mt == 7:
                        h_done_guard[0] = (s_mm, s_mm.n)
                    if mt % 4 == 3:
                        emit_wo(e, mt // 4, s_v.n)

            g = guard_now()
            if pre_drain is not None:
                pre_drain()
            return g, tile_evac, h_done_guard[0]

    def emit_attention(xnt_ready, tile_thr, region_guard, early_guard,
                       tile_hook=None):
        with contextlib.ExitStack() as aes:
            asb = lambda nme, s, d: aes.enter_context(nc.sbuf_tensor(nme, s, d))
            wv_sb = asb("wv_sb", [128, 16 * 256], BF16)
            wqk_sb = [asb(f"wqk_sb{i}", [128, 2 * 16 * 128], BF16) for i in range(2)]
            wout_sb = [asb(f"wout_sb{i}", [128, 4 * 512], BF16) for i in range(2)]
            av_sb = asb("av_sb", [128, 4 * 1024], BF16)
            E_sb = [asb(f"E_sb{i}", [128, 512], BF16) for i in range(3)]
            v4 = asb("v4", [128, NT3 * 256], BF16)
            q_sb = asb("q_sb", [128, 1024], BF16)
            k_sb = asb("k_sb", [128, TOKW], BF16)
            masks = asb("masks", [128, 8 * 512], BF16)
            mpre = asb("mpre_sb", [128, 4 * 512], BF16)

            masks_state = [False]

            def emit_masks_load():
                @emit("G")
                def _(eng, rg=list(region_guard)):
                    wait_pairs(eng, rg)
                    eng.dma_start(out=masks[:].rearrange("p (j i) -> p j i", j=8),
                                  in_=dr_band).then_inc(s_mask.h, 16)
                    eng.dma_start(out=mpre[:].rearrange("p (j i) -> p j i", j=4),
                                  in_=dr_mpre).then_inc(s_mask.h, 16)
                s_mask.n += 32
                masks_state[0] = True
            mask_val = s_mask.n + 32

            wqk_state = {"n": 0, "free": {-2: [early_guard], -1: [early_guard]}}
            wv_state = {"n": 0, "free": {-1: [early_guard]}}
            wout_state = {"n": 0, "free": {-2: [early_guard], -1: [early_guard]}}

            def load_wqk(h):
                nl = wqk_state["n"]
                slot = nl % 2
                gw = wqk_state["free"].get(nl - 2)

                @emit("G")
                def _(eng, h=h, slot=slot, gw=gw):
                    if gw is not None:
                        wait_pairs(eng, gw if isinstance(gw, list) else [gw])
                    eng.dma_start(
                        out=wqk_sb[slot][:, 0:2048].rearrange("p (k j) -> p k j", k=16),
                        in_=dr_wqk[h]).then_inc(s_wqk[slot].h, 16)
                    eng.dma_start(
                        out=wqk_sb[slot][:, 2048:4096].rearrange("p (k j) -> p k j", k=16),
                        in_=dr_wqk[16 + h]).then_inc(s_wqk[slot].h, 16)
                s_wqk[slot].n += 32
                wqk_state["n"] += 1
                return slot, s_wqk[slot].n, nl

            def load_wv(vg):
                nl = wv_state["n"]
                gw = wv_state["free"].get(nl - 1)

                @emit("G")
                def _(eng, vg=vg, gw=gw):
                    if gw is not None:
                        wait_pairs(eng, gw if isinstance(gw, list) else [gw])
                    eng.dma_start(out=wv_sb[:].rearrange("p (k j) -> p k j", k=16),
                                  in_=dr_wv[vg]).then_inc(s_wv.h, 16)
                s_wv.n += 16
                wv_state["n"] += 1
                return s_wv.n, nl

            def load_wout(qset, dg):
                nl = wout_state["n"]
                slot = nl % 2
                gw = wout_state["free"].get(nl - 2)

                @emit("G")
                def _(eng, qset=qset, dg=dg, slot=slot, gw=gw):
                    if gw is not None:
                        wait_pairs(eng, gw if isinstance(gw, list) else [gw])
                    eng.dma_start(out=wout_sb[slot][:].rearrange("p (k j) -> p k j", k=4),
                                  in_=dr_wout[qset, dg]).then_inc(s_wout[slot].h, 16)
                s_wout[slot].n += 16
                wout_state["n"] += 1
                return slot, s_wout[slot].n, nl

            negc_done = [None]
            att_tile_evac = {}
            pending_tail = []
            att_qk_done = [None]
            v4_guard = [region_guard]
            qk_guard = [region_guard]
            av_guard = [region_guard]
            rsbc_guard = [None]
            qkb_ctr = [0]

            def emit_scores(jj, jtile, b, ch, qk_ready, first):
                bfw_ = bw(b)

                @emit("T")
                def _(eng, jj=jj, jtile=jtile, b=b, ch=ch, bfw_=bfw_,
                      qk_ready=qk_ready, first=first, mv=mask_val):
                    if jj == 0:
                        eng.wait_ge(s_v.h, qk_ready[0])
                        eng.wait_ge(s_s.h, qk_ready[1])
                        if first:
                            eng.wait_ge(s_mask.h, mv)
                            eng.wait_ge(s_stpG.h, setupG_val)
                    wait_pairs(eng, [bfw_])
                    eng.matmul(pb[b][:], k_sb[:, jtile * 128:(jtile + 1) * 128],
                               q_sb[:, ch * 512:(ch + 1) * 512],
                               start=True, stop=False, skip_group_check=True)
                    if ch == 1 and jj < 4:
                        msk = mpre[:, jj * 512:(jj + 1) * 512]
                    else:
                        msk = masks[:, jj * 512:(jj + 1) * 512]
                    eng.matmul(pb[b][:], ident_bf[:], msk, start=False, stop=True,
                               skip_group_check=True).then_inc(s_mm.h, 1)
                s_mm.n += 1
                bank_fill[b] = s_mm.n
                return s_mm.n

            def emit_exp_avsum(jj, jtile, b, hh, e_fill, av_bank, sum_bank, sc_fill):
                eslot = jj % 3
                e_prev = e_fill.get(jj - 3)
                ndv = negc_done[0]

                @emit("S")
                def _(eng, jj=jj, b=b, eslot=eslot, sc_fill=sc_fill, e_prev=e_prev,
                      ndv=ndv):
                    eng.wait_ge(s_mm.h, sc_fill)
                    eng.wait_ge(s_v.h, ndv)
                    if e_prev is not None:
                        eng.wait_ge(s_mm.h, e_prev)
                    eng.activation(E_sb[eslot][:], pb[b][:], AF.Exp,
                                   bias=neg_c[:, 0:1]).then_inc(s_s.h, 1)
                s_s.n += 1
                bank_free[b] = (s_s, s_s.n)
                s_exp = s_s.n

                abw = bw(av_bank) if jj == 0 else None
                sbw = bw(sum_bank) if jj == 0 else None

                @emit("T")
                def _(eng, hh=hh, jj=jj, jtile=jtile, eslot=eslot, s_exp=s_exp,
                      abw=abw, sbw=sbw, av_bank=av_bank, sum_bank=sum_bank):
                    eng.wait_ge(s_s.h, s_exp)
                    if jj == 0:
                        wait_pairs(eng, [abw, sbw])
                    eng.matmul(pb[av_bank][:],
                               v4[:, jtile * 256 + hh * 128:jtile * 256 + hh * 128 + 128],
                               E_sb[eslot][:], start=(jj == 0), stop=(jj == 7),
                               skip_group_check=True)
                    eng.matmul(pb[sum_bank][0:1, :], ones_col_bf[:], E_sb[eslot][:],
                               start=(jj == 0), stop=(jj == 7),
                               skip_group_check=True).then_inc(s_mm.h, 1)
                s_mm.n += 1
                e_fill[jj] = s_mm.n
                if jj == 7:
                    bank_fill[av_bank] = s_mm.n
                    bank_fill[sum_bank] = s_mm.n

            for qset in range(4):
                av_writes = []
                for vgi in range(2):
                    vg = qset * 2 + vgi
                    wv_val, wv_idx = load_wv(vg)
                    v4g = v4_guard[0]
                    for tt in range(NT3):
                        b = tt % 2

                        @emit("T")
                        def _(eng, vg=vg, tt=tt, b=b, wv_val=wv_val, bfw=bw(b),
                              thr=tile_thr[tt]):
                            if tt == 0:
                                eng.wait_ge(s_wv.h, wv_val)
                            if vg == 0:
                                eng.wait_ge(s_tp[thr[0]].h, thr[1])
                            wait_pairs(eng, [bfw])
                            mm = None
                            for kt in range(16):
                                mm = eng.matmul(
                                    pb[b][:, 0:256],
                                    XNT[:, kt * TOKW + tt * 128:kt * TOKW + tt * 128 + 128],
                                    wv_sb[:, kt * 256:(kt + 1) * 256],
                                    start=(kt == 0), stop=(kt == 15),
                                    skip_group_check=True)
                            mm.then_inc(s_mm.h, 1)
                        s_mm.n += 1
                        bank_fill[b] = s_mm.n
                        last_xnt_reader[0] = s_mm.n

                        @emit("V")
                        def _(eng, tt=tt, b=b, fill=bank_fill[b], v4g=v4g):
                            eng.wait_ge(s_mm.h, fill)
                            if tt == 0:
                                wait_pairs(eng, v4g if isinstance(v4g, list) else [v4g])
                            eng.tensor_copy(v4[:, tt * 256:(tt + 1) * 256],
                                            pb[b][:, 0:256]).then_inc(s_v.h, 1)
                        s_v.n += 1
                        bank_free[b] = (s_v, s_v.n)
                    wv_state["free"][wv_idx] = (s_mm, s_mm.n)

                    for hh2 in range(2):
                        h = vg * 2 + hh2
                        wqk_slot, wqk_val, wqk_idx = load_wqk(h)
                        if not masks_state[0]:
                            emit_masks_load()
                        qkg = qk_guard[0]
                        for which, ngr, base in ((0, 2, 512), (1, 3, 0)):
                            for g in range(ngr):
                                b = 2 + (qkb_ctr[0] % 2)
                                qkb_ctr[0] += 1

                                tls = [tile_thr[t] for t in
                                       range((base + g * 512) // 128,
                                             (base + g * 512) // 128 + 4)]
                                gthr = [(sl, max(v for s2, v in tls if s2 == sl))
                                        for sl in {s2 for s2, _ in tls}]

                                @emit("T")
                                def _(eng, which=which, g=g, b=b, base=base,
                                      wqk_slot=wqk_slot, wqk_val=wqk_val, bfw=bw(b),
                                      gthr=gthr, h=h):
                                    if which == 0 and g == 0:
                                        eng.wait_ge(s_wqk[wqk_slot].h, wqk_val)
                                    if h == 0:
                                        for sl, v in gthr:
                                            eng.wait_ge(s_tp[sl].h, v)
                                    wait_pairs(eng, [bfw])
                                    woff = which * 2048
                                    mm = None
                                    for kt in range(16):
                                        mm = eng.matmul(
                                            pb[b][:],
                                            wqk_sb[wqk_slot][:, woff + kt * 128:
                                                             woff + kt * 128 + 128],
                                            XNT[:, kt * TOKW + base + g * 512:
                                                kt * TOKW + base + g * 512 + 512],
                                            start=(kt == 0), stop=(kt == 15),
                                            skip_group_check=True)
                                    mm.then_inc(s_mm.h, 1)
                                s_mm.n += 1
                                bank_fill[b] = s_mm.n
                                last_xnt_reader[0] = s_mm.n

                                if which == 0:
                                    @emit("V")
                                    def _(eng, g=g, b=b, fill=bank_fill[b], qkg=qkg):
                                        eng.wait_ge(s_mm.h, fill)
                                        if g == 0:
                                            wait_pairs(eng, qkg if isinstance(qkg, list) else [qkg])
                                        eng.tensor_scalar(
                                            q_sb[:, g * 512:(g + 1) * 512], pb[b][:],
                                            QSCALE, None, ALU.mult).then_inc(s_v.h, 1)
                                    s_v.n += 1
                                    bank_free[b] = (s_v, s_v.n)
                                else:
                                    @emit("S")
                                    def _(eng, g=g, b=b, fill=bank_fill[b], qkg=qkg):
                                        eng.wait_ge(s_mm.h, fill)
                                        if g == 0:
                                            wait_pairs(eng, qkg if isinstance(qkg, list) else [qkg])
                                        eng.copy(k_sb[:, g * 512:(g + 1) * 512],
                                                 pb[b][:]).then_inc(s_s.h, 1)
                                    s_s.n += 1
                                    bank_free[b] = (s_s, s_s.n)
                        qk_ready = (s_v.n, s_s.n)
                        wqk_state["free"][wqk_idx] = (s_mm, s_mm.n)

                        for ch in range(2):
                            jt_tiles = list(range(0, 8)) if ch == 0 else list(range(4, 12))
                            first = (h == 0 and ch == 0)
                            av_bank, sum_bank = 4 + ((h * 2 + ch) % 2), 6
                            e_fill = {}

                            if first:
                                first_banks = [2, 3, 7, 1]
                                vmasks = []
                                for jj in range(4):
                                    b = first_banks[jj]
                                    vmasks.append(emit_scores(jj, jt_tiles[jj], b, ch,
                                                              qk_ready, first))

                                    @emit("V")
                                    def _(eng, jj=jj, b=b, vm=vmasks[-1]):
                                        eng.wait_ge(s_mm.h, vm)
                                        eng.reduce_max(cchain[:, jj:jj + 1], pb[b][:],
                                                       AXX).then_inc(s_v.h, 1)
                                    s_v.n += 1
                                v_rm = s_v.n

                                @emit("V")
                                def _(eng, v_rm=v_rm):
                                    eng.wait_ge(s_v.h, v_rm)
                                    eng.reduce_max(cmax_col_bf[:], cchain[:, 0:4],
                                                   AXX).then_inc(s_v.h, 1)
                                s_v.n += 1
                                v_m8 = s_v.n

                                @emit("T")
                                def _(eng, v_m8=v_m8, bfw=bw(0)):
                                    eng.wait_ge(s_v.h, v_m8)
                                    wait_pairs(eng, [bfw])
                                    eng.matmul(pb[0][0:1, 0:128], cmax_col_bf[:],
                                               ident_bf[:], start=True, stop=True,
                                               skip_group_check=True).then_inc(s_mm.h, 1)
                                s_mm.n += 1
                                bank_fill[0] = s_mm.n
                                f0 = s_mm.n

                                @emit("V")
                                def _(eng, f0=f0):
                                    eng.wait_ge(s_mm.h, f0)
                                    eng.reduce_max(cmax[:], pb[0][0:1, 0:128],
                                                   AXX).then_inc(s_v.h, 1)
                                s_v.n += 1
                                bank_free[0] = (s_v, s_v.n)
                                v_cm = s_v.n

                                @emit("T")
                                def _(eng, v_cm=v_cm, bfw=bw(0)):
                                    eng.wait_ge(s_v.h, v_cm)
                                    wait_pairs(eng, [bfw])
                                    eng.matmul(pb[0][:, 0:1], ones_row_f[:], cmax[:],
                                               start=True, stop=True,
                                               skip_group_check=True).then_inc(s_mm.h, 1)
                                s_mm.n += 1
                                bank_fill[0] = s_mm.n
                                f0b = s_mm.n

                                @emit("V")
                                def _(eng, f0b=f0b):
                                    eng.wait_ge(s_mm.h, f0b)
                                    eng.tensor_scalar(neg_c[:], pb[0][:, 0:1], -1.0,
                                                      None, ALU.mult).then_inc(s_v.h, 1)
                                s_v.n += 1
                                bank_free[0] = (s_v, s_v.n)
                                negc_done[0] = s_v.n

                                while pending_tail:
                                    pending_tail.pop(0)()
                                for jj in range(4):
                                    emit_exp_avsum(jj, jt_tiles[jj], first_banks[jj],
                                                   hh2, e_fill, av_bank, sum_bank,
                                                   vmasks[jj])
                                for jj in range(4, 8):
                                    b = 2 + (jj % 2)
                                    vm = emit_scores(jj, jt_tiles[jj], b, ch, qk_ready,
                                                     first)
                                    emit_exp_avsum(jj, jt_tiles[jj], b, hh2, e_fill,
                                                   av_bank, sum_bank, vm)
                                negc_ready_first = negc_done[0]
                            else:
                                sc_banks = [2, 3, 7]
                                fills_jj = {}
                                for jj in range(3):
                                    fills_jj[jj] = emit_scores(jj, jt_tiles[jj],
                                                               sc_banks[jj % 3], ch,
                                                               qk_ready, first)
                                while pending_tail:
                                    pending_tail.pop(0)()
                                for jj in range(8):
                                    emit_exp_avsum(jj, jt_tiles[jj], sc_banks[jj % 3],
                                                   hh2, e_fill, av_bank, sum_bank,
                                                   fills_jj[jj])
                                    if jj + 3 < 8:
                                        fills_jj[jj + 3] = emit_scores(
                                            jj + 3, jt_tiles[jj + 3],
                                            sc_banks[(jj + 3) % 3], ch, qk_ready, first)

                            avsum_fill = s_mm.n

                            def emit_tail(h=h, ch=ch, av_bank=av_bank,
                                          fill=avsum_fill, ag=av_guard[0],
                                          rg=rsbc_guard[0]):
                                @emit("V")
                                def _(eng, fill=fill):
                                    eng.wait_ge(s_mm.h, fill)
                                    eng.reciprocal(rs_sb[:],
                                                   pb[6][0:1, :]).then_inc(s_v.h, 1)
                                s_v.n += 1
                                bank_free[6] = (s_v, s_v.n)
                                v_rs = s_v.n

                                @emit("T")
                                def _(eng, v_rs=v_rs, bfw=bw(6)):
                                    eng.wait_ge(s_v.h, v_rs)
                                    wait_pairs(eng, [bfw])
                                    eng.matmul(pb[6][:], ones_row_f[:], rs_sb[:],
                                               start=True, stop=True,
                                               skip_group_check=True).then_inc(s_mm.h, 1)
                                s_mm.n += 1
                                bank_fill[6] = s_mm.n
                                f_bc = s_mm.n

                                @emit("S")
                                def _(eng, f_bc=f_bc, rg=rg):
                                    eng.wait_ge(s_mm.h, f_bc)
                                    if rg is not None:
                                        eng.wait_ge(s_v.h, rg)
                                    eng.copy(rsbc_sb[:], pb[6][:]).then_inc(s_s.h, 1)
                                s_s.n += 1
                                bank_free[6] = (s_s, s_s.n)
                                s_bc = s_s.n

                                @emit("V")
                                def _(eng, h=h, ch=ch, s_bc=s_bc, ag=ag,
                                      av_bank=av_bank):
                                    eng.wait_ge(s_s.h, s_bc)
                                    wait_pairs(eng, ag if isinstance(ag, list) else [ag])
                                    hl = h % 4
                                    eng.tensor_tensor(
                                        av_sb[:, hl * 1024 + ch * 512:hl * 1024 + ch * 512 + 512],
                                        pb[av_bank][:], rsbc_sb[:],
                                        ALU.mult).then_inc(s_v.h, 1)
                                s_v.n += 1
                                bank_free[av_bank] = (s_v, s_v.n)
                                av_writes.append(s_v.n)
                                rsbc_guard[0] = s_v.n

                            pending_tail.append(emit_tail)
                        qk_guard[0] = (s_mm, s_mm.n)
                        if h == 15:
                            att_qk_done[0] = (s_mm, s_mm.n)
                    v4_guard[0] = (s_mm, s_mm.n)

                while pending_tail:
                    pending_tail.pop(0)()
                av_ready = max(av_writes)
                for dg in range(4):
                    wslot, wval, widx = load_wout(qset, dg)
                    for j, tt in enumerate(range(4, 12)):
                        b = 6 + (j % 2)

                        @emit("T")
                        def _(eng, dg=dg, tt=tt, b=b, j=j, wslot=wslot, wval=wval,
                              av_ready=av_ready, bfw=bw(b)):
                            if j == 0:
                                eng.wait_ge(s_wout[wslot].h, wval)
                                eng.wait_ge(s_v.h, av_ready)
                            wait_pairs(eng, [bfw])
                            ch2 = (tt - 4) // 4
                            pos = ((tt - 4) % 4) * 128
                            mm = None
                            for ekt in range(4):
                                mm = eng.matmul(
                                    pb[b][:],
                                    av_sb[:, ekt * 1024 + ch2 * 512 + pos:
                                          ekt * 1024 + ch2 * 512 + pos + 128],
                                    wout_sb[wslot][:, ekt * 512:(ekt + 1) * 512],
                                    start=(ekt == 0), stop=(ekt == 3),
                                    skip_group_check=True)
                            mm.then_inc(s_mm.h, 1)
                        s_mm.n += 1
                        bank_fill[b] = s_mm.n

                        @emit("V")
                        def _(eng, tt=tt, b=b, fill=bank_fill[b], dg=dg):
                            eng.wait_ge(s_mm.h, fill)
                            xs = X[:, (tt - 4) * D + dg * 512:(tt - 4) * D + dg * 512 + 512]
                            eng.tensor_tensor(xs, xs, pb[b][:], ALU.add).then_inc(s_v.h, 1)
                        s_v.n += 1
                        bank_free[b] = (s_v, s_v.n)
                        if qset == 3 and dg == 3:
                            att_tile_evac[tt] = s_v.n
                            if tile_hook is not None:
                                tile_hook(tt, s_v.n)
                    wout_state["free"][widx] = (s_mm, s_mm.n)
                av_guard[0] = (s_mm, s_mm.n)

            g = guard_now()
            if stage == "attn":
                _dbg(emit, X, X_halo, dbg, s_v, s_st)
            return g, att_tile_evac, att_qk_done[0]

    def _dbg(emit_, X_, Xh_, dbg_, sv_, sst_):
        rv = sv_.n

        @emit_("Y")
        def _(eng, rv=rv):
            eng.wait_ge(sv_.h, rv)
            eng.dma_start(out=dbg_["own"].rearrange("(t p) d -> p t d", p=128),
                          in_=X_[:].rearrange("p (t d) -> p t d", t=NT_OWN)
                          ).then_inc(sst_.h, 16)
        sst_.n += 16

        @emit_("G")
        def _(eng, rv=rv):
            eng.wait_ge(sv_.h, rv)
            eng.dma_start(out=dbg_["halo"].rearrange("(t p) d -> p t d", p=128),
                          in_=Xh_[:].rearrange("p (t d) -> p t d", t=4)
                          ).then_inc(sst_.h, 16)
        sst_.n += 16

        @emit_("Y")
        def _(eng, tot=sst_.n):
            eng.wait_ge(sst_.h, tot)

    xr, thr1 = emit_ln(0, list(range(12)), [])
    vr = emit_routing(0, list(range(12)), xr, thr1)
    pre1 = (lambda: _dbg(emit, X, X_halo, dbg, s_v, s_st)) if stage == "ffn1" else None
    ln2_hook, ln2_state = make_ln_hook(1, list(range(12)), [], xr)
    g_ffn1, te1, g_h7 = emit_ffn(0, list(range(12)), [0, 512, 1024], vr, xr,
                                 guard_now(), pre_drain=pre1,
                                 tile_hook=None if stage == "ffn1" else ln2_hook)
    if stage == "ffn1":
        drain()
        es.close()
        return nc

    xr2, thr2 = emit_ln(1, list(range(12)), [], hook_state=ln2_state)
    ln3_hook, ln3_state = make_ln_hook(2, list(range(4, 12)), [], xr2)
    g_att, te2, g_qk15 = emit_attention(xr2, thr2,
                                        g_ffn1 + [(s_v, s_v.n), (s_s, s_s.n)],
                                        g_h7,
                                        tile_hook=None if stage == "attn" else ln3_hook)
    if stage == "attn":
        drain()
        es.close()
        return nc

    xr3, thr3 = emit_ln(2, list(range(4, 12)), [], hook_state=ln3_state)
    vr3 = emit_routing(2, list(range(4, 12)), xr3, thr3)

    def store_hook(tt, v):
        @emit("Y")
        def _(eng, tt=tt, v=v):
            eng.wait_ge(s_v.h, v)
            eng.dma_start(
                out=dr_y.rearrange("(t p) d -> p t d", p=128)[:, tt - 4:tt - 3, :],
                in_=X[:, (tt - 4) * D:(tt - 3) * D].rearrange("p (t d) -> p t d", t=1)
                ).then_inc(s_st.h, 16)
        s_st.n += 16

    def pre_final():
        @emit("Y")
        def _(eng, tot=s_st.n):
            eng.wait_ge(s_st.h, tot)

    full_att_guard = g_att + [(s_v, s_v.n), (s_s, s_s.n)]
    emit_ffn(2, list(range(4, 12)), [512, 1024], vr3, xr3,
             {"wi0": [g_qk15], "wi1": full_att_guard, "wo": full_att_guard,
              "inter": full_att_guard}, pre_drain=pre_final, tile_hook=store_hook)
    drain()
    es.close()
    return nc


def host_inputs(L0_attn_x, L0_attn_w_qkv, L0_attn_w_out, L0_ffn_w_gate,
                L0_ffn_wi_0, L0_ffn_wi_1, L0_ffn_wo):
    x = np.ascontiguousarray(np.asarray(L0_attn_x, dtype=np.float32).reshape(8192, D))
    wqkT = np.ascontiguousarray(np.asarray(L0_attn_w_qkv, dtype=np.float32).T)
    woutT = np.ascontiguousarray(np.asarray(L0_attn_w_out, dtype=np.float32).T)
    wg = np.asarray(L0_ffn_w_gate, dtype=np.float32)
    wi0 = np.asarray(L0_ffn_wi_0, dtype=np.float32)
    wi1 = np.asarray(L0_ffn_wi_1, dtype=np.float32)
    wo = np.asarray(L0_ffn_wo, dtype=np.float32)

    wg_t = np.ascontiguousarray(wg.reshape(16, 128, 8).transpose(1, 0, 2))
    wi0_t = np.ascontiguousarray(wi0.reshape(E, 16, 128, 4, 256).transpose(0, 3, 2, 1, 4))
    wi1_t = np.ascontiguousarray(wi1.reshape(E, 16, 128, 4, 256).transpose(0, 3, 2, 1, 4))
    wo_t = np.ascontiguousarray(wo.reshape(E, 2, 4, 128, 2048).transpose(0, 1, 3, 2, 4))
    wqk_t = np.ascontiguousarray(
        wqkT[:, :4096].reshape(16, 128, 32, 128).transpose(2, 1, 0, 3))
    wv_t = np.ascontiguousarray(
        wqkT[:, 4096:].reshape(16, 128, 8, 256).transpose(2, 1, 0, 3))
    wout_t = np.ascontiguousarray(
        woutT.reshape(4, 4, 128, 4, 512).transpose(0, 3, 2, 1, 4))

    p = np.arange(128)[:, None]
    i = np.arange(512)[None, :]
    band = np.zeros((128, 8, 512), dtype=np.float32)
    for jt in range(8):
        j = 128 * jt + p
        ok = (i >= j - 512) & (i <= j)
        band[:, jt, :] = np.where(ok, 0.0, np.float32(-1e9))
    band = band.astype(BF)
    kill = np.full((128, 4, 512), -1e9, dtype=np.float32).astype(BF)
    ident = np.eye(128, dtype=np.float32)

    shared = dict(wg=wg_t, wi0=wi0_t, wi1=wi1_t, wo=wo_t, wqk=wqk_t, wv=wv_t,
                  wout=wout_t, band=band, ident=ident)
    in_maps = []
    for c in range(8):
        g0, g1, g2 = 2 * c, 2 * c + 1, (2 * c + 2) % 16
        x_halo = x[g0 * CH:(g0 + 1) * CH]
        x_own = np.concatenate([x[g1 * CH:(g1 + 1) * CH], x[g2 * CH:(g2 + 1) * CH]])
        q2_first = (g2 % 4 == 0)
        mpre_c = kill if q2_first else np.ascontiguousarray(band[:, 0:4])
        m = dict(shared)
        m["x_own"] = np.ascontiguousarray(x_own)
        m["x_halo"] = np.ascontiguousarray(x_halo)
        m["mpre"] = np.ascontiguousarray(mpre_c)
        in_maps.append(m)
    return in_maps


def assemble(results):
    y = np.empty((8192, D), dtype=np.float32)
    for c in range(8):
        g1, g2 = 2 * c + 1, (2 * c + 2) % 16
        yc = results[c]["y"]
        y[g1 * CH:(g1 + 1) * CH] = yc[0:CH]
        y[g2 * CH:(g2 + 1) * CH] = yc[CH:2 * CH]
    return y.reshape(4, 2048, D)


def kernel(L0_attn_x, L0_attn_w_qkv, L0_attn_w_out, L0_ffn_w_gate,
           L0_ffn_wi_0, L0_ffn_wi_1, L0_ffn_wo):
    in_maps = host_inputs(L0_attn_x, L0_attn_w_qkv, L0_attn_w_out, L0_ffn_w_gate,
                          L0_ffn_wi_0, L0_ffn_wi_1, L0_ffn_wo)
    nc = build_program("full")
    res = run_bass_kernel_spmd(nc, in_maps, core_ids=list(range(8)))
    return assemble(res.results)
